# version 10
# baseline (speedup 1.0000x reference)
import contextlib
import math

import numpy as np
import ml_dtypes

import concourse.bass as bass
import concourse.mybir as mybir
from concourse.bass_utils import run_bass_kernel_spmd

F32 = mybir.dt.float32
BF16 = mybir.dt.bfloat16
AF = mybir.ActivationFunctionType
ALU = mybir.AluOpType
AX = mybir.AxisListType
BF16NP = ml_dtypes.bfloat16

D = 1024
ND = 8
NEG = -1.0e30
DBG_LN_SEQ = False

ENGS = ("tensor", "vector", "scalar", "gpsimd", "sync")


def _copy(e, d, s):
    return e.copy(d, s) if hasattr(e, "copy") and not hasattr(e, "tensor_copy") else e.tensor_copy(d, s)


class Sem:
    __slots__ = ("h", "n")

    def __init__(self, h):
        self.h = h
        self.n = 0


class KB:
    def __init__(self, nc, es):
        self.nc = nc
        self.es = es
        self.sems = {}
        for e in ENGS:
            self.sems[e] = Sem(es.enter_context(nc.semaphore("sem_" + e)))
        for q in ("dsync", "dgpsimd", "dscalar"):
            self.sems[q] = Sem(es.enter_context(nc.semaphore(q)))


class Phase:

    def __init__(self, kb, name=""):
        self.kb = kb
        self.name = name
        self.ops = {e: [] for e in ENGS}
        self.waited = {e: {} for e in ENGS}

    def op(self, eng, build, deps=(), inc=False, inc_sem=None, inc_n=1):
        best = {}
        for ms in deps:
            if ms is None:
                continue
            sem, val = ms
            if best.get(id(sem), (None, -1))[1] < val:
                best[id(sem)] = (sem, val)
        todo = []
        w = self.waited[eng]
        for sem, val in best.values():
            if w.get(id(sem), -1) < val:
                w[id(sem)] = val
                todo.append((sem, val))
        out_ms = None
        if inc:
            sem = inc_sem if inc_sem is not None else self.kb.sems[eng]
            sem.n += inc_n
            out_ms = (sem, sem.n)

        def closure(e, todo=todo, build=build, out_ms=out_ms, inc_n=inc_n):
            for (s, v) in todo:
                e.wait_ge(s.h, v)
            r = build(e)
            if out_ms is not None:
                r.then_inc(out_ms[0].h, inc_n)
            return r

        self.ops[eng].append(closure)
        return out_ms

    def dma(self, queue, dst, src, deps=()):
        sem = self.kb.sems["d" + queue]
        return self.op(queue, lambda e, d=dst, s=src: e.dma_start(d, s),
                       deps=deps, inc=True, inc_sem=sem, inc_n=16)

    def qtotal(self, queue):
        sem = self.kb.sems["d" + queue]
        return (sem, sem.n)

    def wait(self, eng, deps):
        self.op(eng, lambda e: None, deps=deps)

    def realize(self):
        nc = self.kb.nc
        with contextlib.ExitStack() as _ns:
            if self.name:
                _ns.enter_context(nc.named_scope(self.name))
            with nc.Block(self.name or None) as blk:
                for e in ENGS:
                    if self.ops[e]:
                        def emit(obj, ops=self.ops[e]):
                            for f in ops:
                                f(obj)
                        getattr(blk, e)(emit)


def host_consts(T):
    c = {}
    c["identb"] = np.eye(128, dtype=np.float32)
    i = np.arange(128)[:, None]
    j = np.arange(128)[None, :]
    c["maskup"] = np.where(j > i, np.float32(NEG), np.float32(0.0)).astype(np.float32)
    c["masklo"] = np.where(j < i, np.float32(NEG), np.float32(0.0)).astype(np.float32)
    inv = (1.0 / 10000.0 ** (np.arange(0, 32, 2, dtype=np.float32) / np.float32(32.0))).astype(np.float32)
    t = np.arange(T, dtype=np.float32)
    fr = np.outer(t, inv).astype(np.float32)
    c["l0cos"] = np.cos(fr).astype(np.float32)
    c["l0sin"] = np.sin(fr).astype(np.float32)
    ids = np.arange(512, dtype=np.float32)
    theta = np.power(np.float32(10000.0), np.float32(-2.0) * ids) / np.float32(512.0)
    r = (t[:, None] * theta[None, :]).astype(np.float32)
    c["l1cos"] = np.cos(r).astype(np.float32)
    c["l1sin"] = np.sin(r).astype(np.float32)
    DH, WIN = 64, 512
    half_idx = np.arange(0, DH, 2, dtype=np.float32)
    inv_freq = (1.0 / 10000.0 ** (half_idx / np.float32(DH))).astype(np.float32)
    tt = np.arange(T, dtype=np.float32)
    fr2 = (tt[:, None] * inv_freq[None, :]).astype(np.float32)
    fr2 = np.concatenate([fr2, fr2], axis=-1)
    base = ((half_idx + 0.4 * DH) / (1.4 * DH)).astype(np.float32)
    power = ((tt - T // 2) / np.float32(WIN // 2)).astype(np.float32)
    sc = (base[None, :] ** power[:, None]).astype(np.float32)
    sc = np.concatenate([sc, sc], axis=-1)
    cosf, sinf = np.cos(fr2).astype(np.float32), np.sin(fr2).astype(np.float32)
    sgn = np.concatenate([-np.ones(32, np.float32), np.ones(32, np.float32)])
    c["xq1"] = (cosf * sc).astype(np.float32)
    c["xq2"] = (sinf * sc * sgn[None, :]).astype(np.float32)
    c["xk1"] = (cosf / sc).astype(np.float32)
    c["xk2"] = (sinf / sc * sgn[None, :]).astype(np.float32)
    return c


def host_weights(inp):
    w = {}
    ct = lambda a: np.ascontiguousarray(np.asarray(a, dtype=np.float32))
    w["wq_t"] = ct(np.asarray(inp["L0_attn_Wq"]).T)
    w["wkv_t"] = ct(np.asarray(inp["L0_attn_Wkv"]).T)
    _wo3 = np.asarray(inp["L0_attn_Wout"]).T
    w["wout3"] = ct(np.ascontiguousarray(_wo3.reshape(16, 64, 1024).transpose(1, 0, 2).reshape(64, 16384)))
    w["lam"] = ct(np.asarray(inp["L0_attn_lambda_init"]).reshape(1, 16))
    w["rmsw"] = ct(np.asarray(inp["L0_attn_rms_weight"]).reshape(64, 1))
    if "L0_ffn_expert_gate" not in inp:
        return w
    eg = np.asarray(inp["L0_ffn_expert_gate"], dtype=np.float32)
    eu = np.asarray(inp["L0_ffn_expert_up"], dtype=np.float32)
    ed = np.asarray(inp["L0_ffn_expert_down"], dtype=np.float32)
    sg = np.asarray(inp["L0_ffn_shared_gate"], dtype=np.float32).T
    su = np.asarray(inp["L0_ffn_shared_up"], dtype=np.float32).T
    sd = np.asarray(inp["L0_ffn_shared_down"], dtype=np.float32).T
    rows = []
    for e in range(32):
        rows.append(np.concatenate([eg[e].T.ravel(), eu[e].T.ravel(), ed[e].T.ravel()]))
    for h in range(2):
        rows.append(np.concatenate([sg[:, h * 512:(h + 1) * 512].ravel(),
                                    su[:, h * 512:(h + 1) * 512].ravel(),
                                    sd[h * 512:(h + 1) * 512, :].ravel()]))
    w["moe_w"] = ct(np.stack(rows))
    w["gw_t"] = ct(np.asarray(inp["L0_ffn_gate_w"], dtype=np.float32).T)
    w["gbias"] = ct(np.asarray(inp["L0_ffn_gate_bias"], dtype=np.float32).reshape(1, 32))
    if "L1_attn_Wd_q_w" not in inp:
        return w
    for k_src, k_dst in (("Wd_q_w", "l1_wdq"), ("Wd_kv_w", "l1_wdkv"), ("Wu_q_w", "l1_wuq"),
                         ("Wq_r_w", "l1_wqr"), ("Wk_r_w", "l1_wkr"), ("Wu_k_w", "l1_wuk"),
                         ("Wu_v_w", "l1_wuv"), ("Wo_w", "l1_wo")):
        w[k_dst] = ct(np.asarray(inp["L1_attn_" + k_src], dtype=np.float32))
    bb = np.zeros((8, 1024), dtype=np.float32)
    bb[0, 0:512] = np.asarray(inp["L1_attn_Wd_q_b"], dtype=np.float32)
    bb[1, 0:512] = np.asarray(inp["L1_attn_Wd_kv_b"], dtype=np.float32)
    bb[2] = np.asarray(inp["L1_attn_Wu_q_b"], dtype=np.float32)
    bb[3] = np.asarray(inp["L1_attn_Wq_r_b"], dtype=np.float32)
    bb[4] = np.asarray(inp["L1_attn_Wk_r_b"], dtype=np.float32)
    bb[5] = np.asarray(inp["L1_attn_Wu_k_b"], dtype=np.float32)
    bb[6] = np.asarray(inp["L1_attn_Wu_v_b"], dtype=np.float32)
    bb[7] = np.asarray(inp["L1_attn_Wo_b"], dtype=np.float32)
    w["l1ab"] = ct(bb)
    if "L1_ffn_wqkv" not in inp:
        return w
    HID = 2730
    HP = 2816
    w1 = np.asarray(inp["L1_ffn_w1"], dtype=np.float32)
    w1p = np.zeros((2, 1024, 2 * HP), dtype=np.float32)
    w1p[:, :, 0:HID] = w1[:, :, 0:HID]
    w1p[:, :, HP:HP + HID] = w1[:, :, HID:2 * HID]
    b1 = np.asarray(inp["L1_ffn_b1"], dtype=np.float32)
    b1p = np.zeros((2, 2 * HP), dtype=np.float32)
    b1p[:, 0:HID] = b1[:, 0:HID]
    b1p[:, HP:HP + HID] = b1[:, HID:2 * HID]
    w2 = np.asarray(inp["L1_ffn_w2"], dtype=np.float32)
    w2p = np.zeros((2, HP, 1024), dtype=np.float32)
    w2p[:, 0:HID, :] = w2
    w["f_w1"] = ct(w1p)
    w["f_b1"] = ct(b1p)
    w["f_w2"] = ct(w2p)
    w["f_wqkv"] = ct(np.asarray(inp["L1_ffn_wqkv"], dtype=np.float32))
    w["f_wo"] = ct(np.asarray(inp["L1_ffn_wo"], dtype=np.float32))
    w["f_b2"] = ct(np.asarray(inp["L1_ffn_b2"], dtype=np.float32))
    w["f_lng"] = ct(np.asarray(inp["L1_ffn_ln_g"], dtype=np.float32))
    w["f_lnb"] = ct(np.asarray(inp["L1_ffn_ln_b"], dtype=np.float32))
    w["f_qs"] = ct(np.asarray(inp["L1_ffn_q_scale"], dtype=np.float32))
    w["f_ks"] = ct(np.asarray(inp["L1_ffn_k_scale"], dtype=np.float32))
    return w


class AV:

    __slots__ = ("t", "pitch", "base", "w")

    def __init__(self, t, pitch, base, w):
        self.t, self.pitch, self.base, self.w = t, pitch, base, w

    def __getitem__(self, key):
        ps, cs = key
        p0 = ps.start or 0
        p1 = 128 if ps.stop is None else ps.stop
        c0 = cs.start or 0
        c1 = self.w if cs.stop is None else cs.stop
        return bass.AP(self.t, p0 * self.pitch + self.base + c0,
                       [[self.pitch, p1 - p0], [1, c1 - c0]])

    def ap(self, off, dims, pbase=0, pcount=128):
        return bass.AP(self.t, pbase * self.pitch + self.base + off,
                       [[self.pitch, pcount]] + [list(d) for d in dims])

    def sub(self, off, w):
        return AV(self.t, self.pitch, self.base + off, w)


class Builder:
    def __init__(self, T=2048, upto="all"):
        assert T % 512 == 0
        self.T = T
        self.NT = T // 128
        self.NQG = self.NT // 4
        self.upto = upto
        self.nc = bass.Bass("TRN2", target_bir_lowering=False)
        self.es = contextlib.ExitStack()
        self.kb = KB(self.nc, self.es)
        self.phases = []

    def dram_in(self, name, shape, dt=F32):
        return self.nc.dram_tensor(name, list(shape), dt, kind="ExternalInput").ap()

    def dram_out(self, name, shape, dt=F32):
        return self.nc.dram_tensor(name, list(shape), dt, kind="ExternalOutput").ap()

    def sb(self, name, shape, dt):
        return self.es.enter_context(self.nc.sbuf_tensor(name, list(shape), dt))

    def phase(self, name=""):
        p = Phase(self.kb, name)
        self.phases.append(p)
        return p

    def tmark(self):
        return (self.kb.sems["tensor"], self.kb.sems["tensor"].n)

    def finish(self):
        for p in self.phases:
            p.realize()
        self.es.close()
        return self.nc

    FCVALS = (1e-6, 1e-5, 0.1767766952966369, 0.015625, 1.0, 0.031008683647302115, 8.0)

    def fc(self, v, p=128):
        k = self._fcmap[float(v)]
        return self.fct[0:p, k:k + 1]

    ARENA_B = 66480
    ARENA_F = 2560

    def arena_reset(self, keep_xfmb=True):
        self._ab = self._xfmb_resv if keep_xfmb else 0
        self._af = 0

    def aB(self, w):
        w = (w + 15) & ~15
        v = AV(self.arena_b, self.ARENA_B, self._ab, w)
        self._ab += w
        assert self._ab <= self.ARENA_B, f"bf16 arena overflow: {self._ab}"
        return v

    def aF(self, w):
        w = (w + 7) & ~7
        v = AV(self.arena_f, self.ARENA_F, self._af, w)
        self._af += w
        assert self._af <= self.ARENA_F, f"f32 arena overflow: {self._af}"
        return v

    def alloc_stream(self):
        self.fct = self.sb("fct_sb", [128, len(self.FCVALS)], F32)
        self._fcmap = {float(v): i for i, v in enumerate(self.FCVALS)}
        self.arena_b = self.sb("arena_b_sb", [128, self.ARENA_B], BF16)
        self.arena_f = self.sb("arena_f_sb", [128, self.ARENA_F], F32)
        self._xfmb_w = ND * (self.T + 1)
        self._xfmb_resv = (self._xfmb_w + 15) & ~15
        self.Xfmb = AV(self.arena_b, self.ARENA_B, 0, self._xfmb_w)
        self.arena_reset()
        T, NT = self.T, self.NT
        self.Xtm = self.sb("Xtm_sb", [128, NT * 1024], F32)
        self.identb = self.sb("identb_sb", [128, 128], BF16)
        self.identf = self.sb("identf_sb", [128, 128], F32)
        self.maskup = self.sb("maskup_sb", [128, 128], BF16)
        self.masklo = self.sb("masklo_sb", [128, 128], BF16)
        self.onesb = self.sb("onesb_sb", [128, 1], BF16)
        self.onesrow = self.sb("onesrow_sb", [1, 128], BF16)
        self.stat_m = self.sb("stat_m_sb", [128, NT], F32)
        self.stat_s = self.sb("stat_s_sb", [128, NT], F32)
        self.stat_r = self.sb("stat_r_sb", [128, NT], F32)
        self.xtmb_scr = self.sb("xtmb_scr_sb", [128, 1024], BF16)

    def xfmb_chunk(self, c, shifted=False):
        T = self.T
        off = c * (T + 1) + (0 if shifted else 1)
        return self.Xfmb.ap(off, [[1, T]])

    def xtm_tile(self, i):
        return self.Xtm[:, i * 1024:(i + 1) * 1024]

    def rebuild_xfmb(self, ph, x_deps, trb):
        NT, T = self.NT, self.T
        xb = self.xtmb_scr
        for c in range(ND):
            ph.op("vector", lambda e, c=c: e.memset(
                self.Xfmb.ap(c * (T + 1), [[1, 1]]), 0.0))
        guard = [None, None]
        prev_t = getattr(self, "_scr_guard", None)
        last_v = None
        last_s = None
        for i in range(NT):
            dep = [x_deps[i] if isinstance(x_deps, list) else x_deps, prev_t]
            cms = ph.op("scalar", lambda e, i=i: e.copy(xb[:], self.xtm_tile(i)), deps=dep, inc=True)
            for c in range(ND):
                bank = trb[c % 2]
                tms = ph.op("tensor",
                            lambda e, b=bank, c=c: e.transpose(b[:], xb[:, c * 128:(c + 1) * 128], self.identb[:]),
                            deps=[cms, self.ms_identb, guard[c % 2]], inc=True)
                prev_t = tms
                eng = "vector" if c % 2 == 0 else "scalar"
                dst = self.Xfmb.ap(c * (T + 1) + 1 + i * 128, [[1, 128]])
                ems = ph.op(eng, lambda e, b=bank, d=dst: _copy(e, d, b[:]), deps=[tms], inc=True)
                guard[c % 2] = ems
                if eng == "vector":
                    last_v = ems
                else:
                    last_s = ems
        self.ms_xfmb = [last_v, last_s]

    def p_load(self):
        T, NT = self.T, self.NT
        x = self.dram_in("x", [T, 1024])
        identb_d = self.dram_in("identb", [128, 128])
        maskup_d = self.dram_in("maskup", [128, 128])
        masklo_d = self.dram_in("masklo", [128, 128])

        ph = self.phase("load")
        for i in range(NT):
            ph.dma("sync", self.xtm_tile(i), x[i * 128:(i + 1) * 128, :])
        self.ms_identb = ph.dma("gpsimd", self.identb[:], identb_d)
        self.ms_identf = ph.dma("sync", self.identf[:], identb_d)
        ph.dma("gpsimd", self.maskup[:], maskup_d)
        ph.dma("gpsimd", self.masklo[:], masklo_d)
        ms_gp_all = ph.qtotal("gpsimd")
        ms_sync_all = ph.qtotal("sync")
        lds = [ms_sync_all] * NT
        self.ms_maskup = ms_gp_all
        self.ms_masklo = ms_gp_all
        self.ms_identb = ms_gp_all
        self.ms_identf = ms_sync_all
        for v, k in self._fcmap.items():
            ph.op("vector", lambda e, v=v, k=k: e.memset(self.fct[:, k:k + 1], v))
        ph.op("vector", lambda e: e.memset(self.onesb[:], 1.0), inc=True)
        ph.op("vector", lambda e: e.memset(self.onesrow[:], 1.0), inc=True)
        self.ms_ones = (self.kb.sems["vector"], self.kb.sems["vector"].n)
        self.ms_xtm = lds
        with contextlib.ExitStack() as pes:
            trb = [pes.enter_context(self.nc.psum_tensor("trl0", [128, 128], BF16)),
                   pes.enter_context(self.nc.psum_tensor("trl1", [128, 128], BF16))]
            self.rebuild_xfmb(ph, lds, trb)

    def emit_rope_l0(self, ph, src, dst, col0, ngroups, ttile, cos_s, sin_s, rt1, rt2, deps):
        ev = lambda t, o: t.ap(col0 + o, [[32, ngroups], [2, 16]])
        cosap = cos_s.ap(ttile * 16, [[0, ngroups], [1, 16]])
        sinap = sin_s.ap(ttile * 16, [[0, ngroups], [1, 16]])
        r1 = rt1.ap(0, [[16, ngroups], [1, 16]])
        r2 = rt2.ap(0, [[16, ngroups], [1, 16]])
        g = self._rope_guard
        m1 = ph.op("vector", lambda e: e.tensor_tensor(r1, ev(src, 0), cosap, ALU.mult), deps=list(deps) + [g], inc=True)
        m2 = ph.op("gpsimd", lambda e: e.tensor_tensor(r2, ev(src, 1), sinap, ALU.mult), deps=list(deps) + [g], inc=True)
        m3 = ph.op("vector", lambda e: e.tensor_tensor(ev(dst, 0), r1, r2, ALU.subtract), deps=[m1, m2], inc=True)
        m4 = ph.op("vector", lambda e: e.tensor_tensor(r1, ev(src, 0), sinap, ALU.mult), deps=[m3], inc=True)
        m5 = ph.op("gpsimd", lambda e: e.tensor_tensor(r2, ev(src, 1), cosap, ALU.mult), deps=[m3], inc=True)
        m6 = ph.op("vector", lambda e: e.tensor_tensor(ev(dst, 1), r1, r2, ALU.add), deps=[m4, m5], inc=True)
        self._rope_guard = m6
        return m6

    def p_l0_attn(self):
        T, NT, NQG = self.T, self.NT, self.NQG
        nc, kb = self.nc, self.kb
        scale = 1.0 / math.sqrt(32.0)

        wq_d = self.dram_in("wq_t", [1024, 1024])
        wkv_d = self.dram_in("wkv_t", [1024, 512])
        wout_d = self.dram_in("wout3", [64, 16384])
        lam_d = self.dram_in("lam", [1, 16])
        rmsw_d = self.dram_in("rmsw", [64, 1])
        cos_d = self.dram_in("l0cos", [T, 16])
        sin_d = self.dram_in("l0sin", [T, 16])

        self.arena_reset()
        wqh = self.aB(4096)
        wx = self.aB(8192)
        qkv_tm = self.aB(1024)
        qkv2 = self.aB(1024)
        rt1 = self.aB(512)
        rt2 = self.aB(512)
        Krep = self.aB(4 * T)
        Vtm = self.aB(NT * 260)
        Qfm = self.aB(4096)
        PT = self.aB(2048)
        maskcat = self.aB(512)
        qsqb = self.aB(1024)
        ybuf = self.aB(512)
        ysub = self.aB(512)
        ones2 = self.aB(64)
        yfm = self.aB(8192)
        ysq = self.aB(512)
        lam_sb = self.aB(16)
        orb = self.aB(1024)
        lam_s = self.aF(16)
        lamb = self.aF(16)
        rmsw_s = self.aF(8)
        cos_s = self.aF(NT * 16)
        sin_s = self.aF(NT * 16)
        o_row = self.aF(512)
        of1 = self.aF(512)
        of2 = self.aF(512)
        qn2 = self.aF(32)
        qn2m = self.aF(32)
        kn2a = self.aF(8)
        nmb = self.aF(32)
        nrow = self.aF(40)

        with contextlib.ExitStack() as pes:
            E = pes.enter_context
            pst = lambda n, s, d=F32: E(nc.psum_tensor(n, list(s), d))
            ps_gen = pst("ps_gen", [128, 512])
            ps_r = [pst(f"ps_r{j}", [128, 512]) for j in range(4)]
            ps_o1 = pst("ps_o1", [128, 512])
            ps_o2 = pst("ps_o2", [128, 512])
            ps_rr = pst("ps_rr", [128, 512])
            gen_b = ps_gen[:].bitcast(BF16)

            ph = self.phase("l0a_load")
            ph.dma("gpsimd", wx[:, 0:4096].rearrange("p (c f) -> p c f", c=8),
                   wkv_d.rearrange("(c p) f -> p c f", p=128))
            ms_wkv = ph.qtotal("gpsimd")
            ph.dma("sync", lam_s[0:1, 0:16], lam_d)
            ph.dma("sync", rmsw_s[0:64, 0:1], rmsw_d)
            ph.dma("sync", rmsw_s[64:128, 0:1], rmsw_d)
            ph.dma("sync", cos_s[:, 0:NT * 16].rearrange("p (a j) -> p a j", j=16),
                   cos_d.rearrange("(a p) j -> p a j", p=128))
            ph.dma("sync", sin_s[:, 0:NT * 16].rearrange("p (a j) -> p a j", j=16),
                   sin_d.rearrange("(a p) j -> p a j", p=128))
            ms_sl = ph.qtotal("sync")
            ms_lam = ms_rw = ms_cos = ms_sin = ms_sl
            ms_sig = ph.op("scalar", lambda e: e.activation(lam_sb[0:1, 0:16], lam_s[0:1, 0:16], AF.Sigmoid),
                           deps=[ms_lam], inc=True)
            ms_bcl = ph.op("tensor", lambda e: e.matmul(ps_gen[:, 0:16], self.onesrow[0:1, :],
                                                        lam_sb[0:1, 0:16], start=True, stop=True),
                           deps=[ms_sig, self.ms_ones], inc=True)
            ms_lamb = ph.op("vector", lambda e: e.tensor_copy(lamb[:, 0:16], ps_gen[:, 0:16]),
                            deps=[ms_bcl], inc=True)
            ms_rope_tab = [ms_cos, ms_sin]
            self._rope_guard = None
            mc1 = ph.op("vector", lambda e: e.memset(maskcat[:, 0:512], NEG), inc=True)
            ms_mc = ph.op("vector", lambda e: e.tensor_copy(maskcat[:, 384:512], self.masklo[:]),
                          deps=[mc1, self.ms_masklo], inc=True)
            ms_o2 = ph.op("vector", lambda e: e.memset(ones2[:, 0:64], 1.0), inc=True)
            ms_vones = ph.op("gpsimd", lambda e: e.memset(Vtm.ap(64, [[65, 4 * NT], [1, 1]]), 1.0),
                             inc=True)

            ph = self.phase("l0a_kv")
            self._ksq_g = None
            ms_ktile = [None] * NT
            ms_vtile = [None] * NT
            guard_gen = [ms_lamb]
            nkv = NT
            if self.upto.startswith("l0a_proj") and self.upto != "l0a_proj":
                nkv = int(self.upto[len("l0a_proj"):])
            for i in range(nkv):
                deps0 = self.ms_xfmb + [ms_wkv] + list(guard_gen)
                for c in range(ND):
                    lhs = self.xfmb_chunk(c)[:, i * 128:(i + 1) * 128]
                    rhs = wx[:, c * 512:(c + 1) * 512]
                    ph.op("tensor", lambda e, l=lhs, r=rhs, s=(c == 0), t=(c == ND - 1):
                          e.matmul(ps_gen[:], l, r, start=s, stop=t),
                          deps=deps0 if c == 0 else (), inc=(c == ND - 1))
                ms_mm = self.tmark()
                ms_ek = ph.op("vector", lambda e: e.tensor_copy(qkv_tm[:, 0:256], ps_gen[:, 0:256]),
                              deps=[ms_mm], inc=True)
                ms_ev = ph.op("vector", lambda e, i=i: e.tensor_copy(
                    Vtm.ap(i * 260, [[65, 4], [1, 64]]),
                    bass.AP(ps_gen, 256, [[512, 128], [64, 4], [1, 64]])),
                    deps=[ms_mm] + ([ms_vones] if i == 0 else []), inc=True)
                ms_vtile[i] = ms_ev
                msq_k = ph.op("scalar", lambda e: e.activation(qsqb[:, 0:256], qkv_tm[:, 0:256], AF.Square),
                              deps=[ms_ek, self._ksq_g], inc=True)
                mkr = ph.op("vector", lambda e: e.reduce_sum(
                    qn2[:, 0:8], qsqb.ap(0, [[64, 4], [32, 2], [1, 32]]), axis=AX.X),
                    deps=[msq_k], inc=True)
                if i == 0:
                    mkr = ph.op("vector", lambda e: e.tensor_copy(kn2a[:, 0:8], qn2[:, 0:8]),
                                deps=[mkr], inc=True)
                else:
                    mkr = ph.op("vector", lambda e: e.tensor_tensor(
                        kn2a[:, 0:8], kn2a[:, 0:8], qn2[:, 0:8], ALU.max), deps=[mkr], inc=True)
                self._ksq_g = mkr
                if self.upto.startswith("l0a_proj"):
                    guard_gen = [ms_ek, ms_ev]
                    continue
                ms_rope = self.emit_rope_l0(ph, qkv_tm, qkv2, 0, 8, i, cos_s, sin_s, rt1, rt2,
                                            deps=[ms_ek] + ms_rope_tab)
                guard_tr = [ms_ek, ms_ev]
                tms = None
                for g in range(4):
                    d0 = gen_b[:, g * 128:(g + 1) * 128]
                    ph.op("tensor", lambda e, d=d0, g=g: e.transpose(
                        d[0:64, :], qkv2[:, g * 64:(g + 1) * 64], self.identb[:]),
                        deps=([ms_rope, self.ms_identb] + guard_tr) if g == 0 else ())
                    tms = ph.op("tensor", lambda e, d=d0, g=g: e.transpose(
                        d[64:128, :], qkv2[:, g * 64:(g + 1) * 64], self.identb[:]),
                        inc=(g == 3))
                last_e = None
                for g in range(4):
                    d0 = gen_b[:, g * 128:(g + 1) * 128]
                    ph.op("vector", lambda e, d=d0, g=g, i=i: e.tensor_copy(
                        Krep[0:64, g * T + i * 128: g * T + i * 128 + 128], d[0:64, :]),
                        deps=[tms], inc=True)
                    last_e = ph.op("vector", lambda e, d=d0, g=g, i=i: e.tensor_copy(
                        Krep[64:128, g * T + i * 128: g * T + i * 128 + 128], d[64:128, :]),
                        deps=[tms], inc=True)
                guard_gen = [last_e]
                ms_ktile[i] = last_e
            tkq = ph.op("tensor", lambda e: e.transpose(ps_rr[0:8, 0:128], kn2a[:, 0:8], self.identf[:]),
                        deps=[self._ksq_g, self.ms_identf], inc=True)
            rkq = ph.op("vector", lambda e: e.reduce_max(
                qn2m[0:8, 0:1], bass.AP(ps_rr, 0, [[512, 8], [1, 128]]), axis=AX.X),
                deps=[tkq], inc=True)
            tk2 = ph.op("tensor", lambda e: e.transpose(ps_rr[0:1, 0:8], qn2m[0:8, 0:1],
                                                        self.identf[0:8, 0:8]), deps=[rkq], inc=True)
            ms_krow = ph.op("vector", lambda e: e.tensor_copy(nrow[0:1, 32:40], ps_rr[0:1, 0:8]),
                            deps=[tk2], inc=True)
            if self.upto.startswith("l0a_proj") or self.upto in ("l0a_rope", "l0a_kv"):
                return

            self._ring_g = [None] * 4
            self._pt_g = [None] * 4
            self._o1_g = None
            self._o2_g = None
            self._rr_g = None
            self._gen_g = None
            self._of_g = None
            self._ybuf_g = None
            self._orb_g = None
            self._qsq_g = self._ksq_g
            guard_wqh = []
            guard_wx = self.tmark()
            x_raw = [None] * NT
            nqg = 1 if self.upto in ("l0a_h1", "l0a_1qg") else NQG
            for qg in range(nqg):
                ph = self.phase(f"l0a_qg{qg}")
                ms_qfm = [None] * ND
                for half in range(2):
                    ph.op("gpsimd", lambda e, h=half: e.dma_start(
                        wqh[:, 0:4096].rearrange("p (c f) -> p c f", c=8),
                        wq_d[:, h * 512:(h + 1) * 512].rearrange("(c p) f -> p c f", p=128)),
                        deps=guard_wqh, inc=True, inc_sem=kb.sems["dgpsimd"], inc_n=16)
                    ms_wq = ph.qtotal("gpsimd")
                    for ii in range(4):
                        i = qg * 4 + ii
                        deps0 = self.ms_xfmb + [ms_wq] + list(guard_gen)
                        for c in range(ND):
                            lhs = self.xfmb_chunk(c)[:, i * 128:(i + 1) * 128]
                            rhs = wqh[:, c * 512:(c + 1) * 512]
                            ph.op("tensor", lambda e, l=lhs, r=rhs, s=(c == 0), t=(c == ND - 1):
                                  e.matmul(ps_gen[:], l, r, start=s, stop=t),
                                  deps=deps0 if c == 0 else (), inc=(c == ND - 1))
                        ms_mm = self.tmark()
                        ms_eq = ph.op("vector", lambda e: e.tensor_copy(qkv_tm[:, 0:512], ps_gen[:]),
                                      deps=[ms_mm, self._rope_guard], inc=True)
                        guard_gen = [ms_eq]
                        ms_rope = self.emit_rope_l0(ph, qkv_tm, qkv2, 0, 16, i, cos_s, sin_s, rt1, rt2,
                                                    deps=[ms_eq] + ms_rope_tab)
                        msq_q = ph.op("scalar", lambda e: e.activation(qsqb[:, 0:512], qkv2[:, 0:512],
                                                                       AF.Square),
                                      deps=[ms_rope, self._qsq_g], inc=True)
                        mrd_q = ph.op("vector", lambda e, half=half: e.reduce_sum(
                            qn2[:, half * 16:half * 16 + 16],
                            qsqb.ap(0, [[64, 8], [32, 2], [1, 32]]), axis=AX.X),
                            deps=[msq_q], inc=True)
                        if ii == 0:
                            mrd_q = ph.op("vector", lambda e, half=half: e.tensor_copy(
                                qn2m[:, half * 16:half * 16 + 16], qn2[:, half * 16:half * 16 + 16]),
                                deps=[mrd_q, self._qsq_g], inc=True)
                        else:
                            mrd_q = ph.op("vector", lambda e, half=half: e.tensor_tensor(
                                qn2m[:, half * 16:half * 16 + 16], qn2m[:, half * 16:half * 16 + 16],
                                qn2[:, half * 16:half * 16 + 16], ALU.max), deps=[mrd_q], inc=True)
                        self._qsq_g = mrd_q
                        tms = None
                        for cl in range(4):
                            d0 = gen_b[:, cl * 128:cl * 128 + 128]
                            tms = ph.op("tensor", lambda e, d=d0, cl=cl: e.transpose(
                                d, qkv2[:, cl * 128:(cl + 1) * 128], self.identb[:]),
                                deps=([ms_rope, msq_q] + list(guard_gen)) if cl == 0 else (),
                                inc=(cl == 3))
                        for cl in range(4):
                            c = half * 4 + cl
                            d0 = gen_b[:, cl * 128:cl * 128 + 128]
                            dst = Qfm[:, c * 512 + ii * 128: c * 512 + ii * 128 + 128]
                            ems = ph.op("vector", lambda e, d=dst, s=d0: e.tensor_copy(d, s),
                                        deps=[tms], inc=True)
                            ms_qfm[c] = ems
                        guard_gen = [ems]
                    guard_wqh = [self.tmark()]
                kmax = (qg + 1) * 512
                nkt = kmax // 128
                t1q = ph.op("tensor", lambda e: e.transpose(ps_rr[0:32, 0:128], qn2m[:, 0:32],
                                                            self.identf[:]),
                            deps=[self._qsq_g, self.ms_identf, self._rr_g], inc=True)
                r1q = ph.op("vector", lambda e: e.reduce_max(
                    qn2[0:32, 0:1], bass.AP(ps_rr, 0, [[512, 32], [1, 128]]), axis=AX.X),
                    deps=[t1q], inc=True)
                t2q = ph.op("tensor", lambda e: e.transpose(ps_rr[0:1, 0:32], qn2[0:32, 0:1],
                                                            self.identf[0:32, 0:32]), deps=[r1q], inc=True)
                nq1 = ph.op("vector", lambda e: e.tensor_copy(nrow[0:1, 0:32], ps_rr[0:1, 0:32]),
                            deps=[t2q], inc=True)
                nq2 = ph.op("vector", lambda e: e.tensor_tensor(
                    nrow[0:1, 0:32], nrow[0:1, 0:32],
                    nrow.ap(32, [[2, 4], [0, 4], [1, 2]], pcount=1), ALU.mult),
                    deps=[nq1, ms_krow], inc=True)
                nq3 = ph.op("scalar", lambda e: e.activation(nrow[0:1, 0:32], nrow[0:1, 0:32], AF.Sqrt),
                            deps=[nq2], inc=True)
                nq4 = ph.op("vector", lambda e: e.tensor_scalar_mul(nrow[0:1, 0:32], nrow[0:1, 0:32],
                                                                   -scale), deps=[nq3], inc=True)
                nq5 = ph.op("vector", lambda e: e.tensor_copy(orb[0:1, 0:32], nrow[0:1, 0:32]),
                            deps=[nq4], inc=True)
                nq6 = ph.op("tensor", lambda e: e.matmul(ps_rr[:, 0:32], self.onesrow[0:1, :],
                                                         orb[0:1, 0:32], start=True, stop=True),
                            deps=[nq5, self.ms_ones, t2q], inc=True)
                ms_nmb = ph.op("vector", lambda e: e.tensor_copy(nmb[:, 0:32], ps_rr[:, 0:32]),
                               deps=[nq6], inc=True)
                self._rr_g = ms_nmb
                self._gen_g = guard_gen[-1] if guard_gen else None
                nheads = 1 if self.upto == "l0a_h1" else 16
                pending_norm = None
                for h in range(nheads):
                    qchunk, qb, kvh = h // 2, 64 * (h % 2), h // 4
                    ntl = 2 * nkt
                    tiles = [(kt, s) for kt in range(nkt) for s in (0, 1)]
                    msx = [None] * ntl
                    last_av = [None, None]

                    def emit_tile(j, h=h, qchunk=qchunk, qb=qb, kvh=kvh, tiles=tiles, msx=msx):
                        kt, s12 = tiles[j]
                        r = j % 4
                        lhs = Krep[qb + 32 * s12: qb + 32 * s12 + 32,
                                   kvh * T + kt * 128: kvh * T + kt * 128 + 128]
                        rhs = Qfm[qb + 32 * s12: qb + 32 * s12 + 32,
                                  qchunk * 512: qchunk * 512 + 512]
                        mm = ph.op("tensor", lambda e, l=lhs, rr=rhs, b=ps_r[r], tp=(qb + 32 * s12, 0):
                                   e.matmul(b[:], l, rr, start=True, stop=True, tile_position=tp),
                                   deps=[ms_qfm[qchunk], ms_ktile[kt], self._ring_g[r]], inc=True)
                        dep_e = mm
                        ktg = kt - qg * 4
                        if ktg >= 0:
                            w = (ktg + 1) * 128
                            dep_e = ph.op("vector", lambda e, b=ps_r[r], w=w: e.tensor_add(
                                b[:, 0:w], b[:, 0:w], maskcat[:, 512 - w:512]),
                                deps=[mm, ms_mc], inc=True)
                        msx[j] = ph.op("scalar", lambda e, b=ps_r[r], r=r, col=h * 2 + tiles[j][1]:
                                       e.activation(PT[:, r * 512:(r + 1) * 512], b[:], AF.Exp,
                                                    bias=nmb[:, col:col + 1],
                                                    scale=self.fc(0.1767766952966369)),
                                       deps=[dep_e, ms_nmb, self._pt_g[r]], inc=True)
                        self._ring_g[r] = msx[j]

                    def emit_av(j, h=h, kvh=kvh, tiles=tiles, msx=msx, last_av=last_av, nkt=nkt):
                        kt, s12 = tiles[j]
                        r = j % 4
                        ob = ps_o1 if s12 == 0 else ps_o2
                        og = self._o1_g if s12 == 0 else self._o2_g
                        ms_av = ph.op("tensor", lambda e, kt=kt, r=r, b=ob, kvh=kvh:
                                      e.matmul(b[0:65, :], Vtm.ap(kt * 260 + kvh * 65, [[1, 65]]),
                                               PT[:, r * 512:(r + 1) * 512],
                                               start=(kt == 0), stop=(kt == nkt - 1)),
                                      deps=[msx[j], ms_vtile[kt]] + ([ms_vones, og] if kt == 0 else []),
                                      inc=True)
                        self._pt_g[r] = ms_av
                        last_av[s12] = ms_av

                    def make_norm(h=h, qb=qb, last_av=last_av):
                        st8 = {}

                        def normA():
                            ev1 = ph.op("vector", lambda e: e.tensor_copy(of1[0:65, 0:512], ps_o1[0:65, :]),
                                        deps=[last_av[0], self._of_g], inc=True)
                            ev2 = ph.op("vector", lambda e: e.tensor_copy(of2[0:65, 0:512], ps_o2[0:65, :]),
                                        deps=[last_av[1], self._of_g], inc=True)
                            self._o1_g = ev1
                            self._o2_g = ev2
                            rc1 = ph.op("vector", lambda e: e.reciprocal(of1[64:65, 0:512], of1[64:65, 0:512]),
                                        deps=[ev1], inc=True)
                            rc2 = ph.op("vector", lambda e: e.reciprocal(of2[64:65, 0:512], of2[64:65, 0:512]),
                                        deps=[ev2], inc=True)
                            cb1 = ph.op("vector", lambda e: e.tensor_copy(orb[64:65, 0:512], of1[64:65, 0:512]),
                                        deps=[rc1, self._orb_g], inc=True)
                            cb2 = ph.op("vector", lambda e: e.tensor_copy(orb[64:65, 512:1024], of2[64:65, 0:512]),
                                        deps=[rc2], inc=True)
                            st8["bc1"] = ph.op("tensor", lambda e: e.matmul(
                                ps_rr[0:64, :], ones2[64:65, 0:64], orb[64:65, 0:512],
                                start=True, stop=True, tile_position=(64, 0)),
                                deps=[cb1, self._rr_g], inc=True)
                            st8["bc2"] = ph.op("tensor", lambda e: e.matmul(
                                ps_gen[0:64, :], ones2[64:65, 0:64], orb[64:65, 512:1024],
                                start=True, stop=True, tile_position=(64, 0)),
                                deps=[cb2, self._gen_g], inc=True)
                            st8["ev"] = (ev1, ev2)
                            st8["cb"] = (cb1, cb2)

                        def normB():
                            ev1, ev2 = st8["ev"]
                            ya = ph.op("vector", lambda e: e.tensor_tensor(
                                ybuf[0:64, 0:512], of1[0:64, 0:512], ps_rr[0:64, :], ALU.mult),
                                deps=[st8["bc1"], ev1, self._ybuf_g], inc=True)
                            yb_ = ph.op("vector", lambda e, h=h: e.scalar_tensor_tensor(
                                ysub[0:64, 0:512], of2[0:64, 0:512], lamb[0:64, h:h + 1],
                                ps_gen[0:64, :], ALU.mult, ALU.mult),
                                deps=[st8["bc2"], ev2, ms_lamb], inc=True)
                            yc = ph.op("vector", lambda e: e.tensor_tensor(
                                ybuf[0:64, 0:512], ybuf[0:64, 0:512], ysub[0:64, 0:512], ALU.subtract),
                                deps=[ya, yb_], inc=True)
                            sq = ph.op("scalar", lambda e: e.activation(ysq[0:64, 0:512], ybuf[0:64, 0:512],
                                                                       AF.Square), deps=[yc], inc=True)
                            st8["rs"] = ph.op("tensor", lambda e: e.matmul(
                                ps_rr[0:1, :], self.onesb[0:64, 0:1], ysq[0:64, 0:512],
                                start=True, stop=True),
                                deps=[sq, self.ms_ones, ya], inc=True)
                            st8["yc"] = yc
                            st8["yb_"] = yb_
                            self._of_g = yc

                        def normC():
                            st_ = ph.op("scalar", lambda e: e.activation(
                                o_row[0:1, 0:512], ps_rr[0:1, :], AF.Sqrt,
                                bias=self.fc(1e-6, 1), scale=self.fc(0.015625, 1)),
                                deps=[st8["rs"]], inc=True)
                            rc = ph.op("vector", lambda e: e.reciprocal(o_row[0:1, 0:512], o_row[0:1, 0:512]),
                                       deps=[st_], inc=True)
                            cbr = ph.op("vector", lambda e: e.tensor_copy(orb[0:1, 512:1024], o_row[0:1, 0:512]),
                                        deps=[rc, st8["cb"][1]], inc=True)
                            self._orb_g = cbr
                            bcr = ph.op("tensor", lambda e: e.matmul(
                                ps_gen[0:64, :], ones2[0:1, 0:64], orb[0:1, 512:1024],
                                start=True, stop=True),
                                deps=[cbr, st8["yb_"]], inc=True)
                            ap_ = ph.op("vector", lambda e, h=h: e.scalar_tensor_tensor(
                                yfm[0:64, h * 512:h * 512 + 512], ybuf[0:64, 0:512],
                                rmsw_s[0:64, 0:1], ps_gen[0:64, :], ALU.mult, ALU.mult),
                                deps=[bcr, ms_rw, st8["yc"]], inc=True)
                            self._rr_g = st_
                            self._gen_g = ap_
                            self._ybuf_g = ap_
                            self._yfm_last = ap_
                        return [normA, normB, normC]

                    for j in range(min(4, ntl)):
                        emit_tile(j)
                    if pending_norm is not None:
                        pending_norm[0]()
                    for j in range(4, ntl):
                        emit_av(j - 4)
                        emit_tile(j)
                        if pending_norm is not None:
                            if j == 5:
                                pending_norm[1]()
                            elif j == min(7, ntl - 1):
                                pending_norm[2]()
                                pending_norm = None
                    if pending_norm is not None:
                        pending_norm[1]()
                        pending_norm[2]()
                        pending_norm = None
                    for j in range(max(0, ntl - 4), ntl):
                        emit_av(j)
                    pending_norm = make_norm()
                pending_norm[0]()
                pending_norm[1]()
                pending_norm[2]()
                pending_norm = None
                guard_gen = [self._gen_g]
                yv = (kb.sems["vector"], kb.sems["vector"].n)
                ys = (kb.sems["scalar"], kb.sems["scalar"].n)
                for half in range(2):
                    ph.op("gpsimd", lambda e, h=half: e.dma_start(
                        wx.ap(0, [[512, 16], [1, 512]], pcount=64),
                        bass.AP(wout_d.tensor, wout_d.offset + h * 512,
                                [[16384, 64], [1024, 16], [1, 512]])),
                        deps=[guard_wx], inc=True, inc_sem=kb.sems["dgpsimd"], inc_n=16)
                    ms_wo = ph.qtotal("gpsimd")
                    for ii in range(4):
                        qt = qg * 4 + ii
                        deps0 = [yv, ys, ms_wo] + list(guard_gen)
                        for c in range(16):
                            lhs = yfm[0:64, c * 512 + ii * 128: c * 512 + ii * 128 + 128]
                            rhs = wx.ap(c * 512, [[1, 512]], pcount=64)
                            ph.op("tensor", lambda e, l=lhs, r=rhs, s=(c == 0), t=(c == 15):
                                  e.matmul(ps_gen[:], l, r, start=s, stop=t),
                                  deps=deps0 if c == 0 else (), inc=(c == 15))
                        ms_mm = self.tmark()
                        dstx = self.Xtm[:, qt * 1024 + half * 512: qt * 1024 + half * 512 + 512]
                        ms_res = ph.op("vector", lambda e, d=dstx: e.tensor_add(d, d, ps_gen[:]),
                                       deps=[ms_mm, self.ms_xtm[qt]], inc=True)
                        guard_gen = [ms_res]
                        if half == 1:
                            x_raw[qt] = ms_res
                    guard_wx = self.tmark()
            for i in range(NT):
                if x_raw[i] is None:
                    x_raw[i] = self.ms_xtm[i] if isinstance(self.ms_xtm, list) else self.ms_xtm
            self.ms_x_raw = x_raw

    def p_l0_ffn(self):
        T, NT = self.T, self.NT
        nc, kb = self.nc, self.kb
        NE = 34

        moe_d = self.dram_in("moe_w", [NE, 1572864])
        gw_d = self.dram_in("gw_t", [1024, 32])
        gb_d = self.dram_in("gbias", [1, 32])

        self.arena_reset()
        gw_s = self.aB(256)
        gb_sb = self.aB(32)
        logit = self.aB(512)
        mw = [self.aB(12288), self.aB(12288)]
        sgil = self.aB(1024)
        htm = self.aB(1024)
        hfm = self.aB(1024)
        gb_s = self.aF(32)
        gb_b = self.aF(32)
        SR = self.aF(NT * 32)
        SC = self.aF(NT * 32)
        EQ = self.aF(NT * 32)
        WA = self.aF(NT * 32)
        g64a = self.aF(NT * 4)
        g64b = self.aF(NT * 4)
        g16a = self.aF(NT)
        g16b = self.aF(NT)

        with contextlib.ExitStack() as pes:
            E = pes.enter_context
            pst = lambda n, s, d=F32: E(nc.psum_tensor(n, list(s), d))
            psA = pst("psA", [128, 512])
            psB = pst("psB", [128, 512])
            psC = pst("psC", [128, 512])
            psD = pst("psD", [128, 512])
            psA2 = pst("psA2", [128, 512])
            psB2 = pst("psB2", [128, 512])
            psC2 = pst("psC2", [128, 512])
            psD2 = pst("psD2", [128, 512])
            pAs = [psA, psA2]
            pBs = [psB, psB2]
            pCs = [psC, psC2]
            pDs = [psD, psD2]
            psR = psC2
            rtr_b = psD2[:].bitcast(BF16)
            slot_sem = [Sem(pes.enter_context(nc.semaphore("mw_s0"))),
                        Sem(pes.enter_context(nc.semaphore("mw_s1")))]

            ph = self.phase("l0f_router")
            ph.dma("gpsimd", gw_s[:, 0:256].rearrange("p (c f) -> p c f", c=8),
                   gw_d.rearrange("(c p) f -> p c f", p=128))
            ms_ld = ph.qtotal("gpsimd")
            ph.dma("sync", gb_s[0:1, 0:32], gb_d)
            ms_gb = ph.qtotal("sync")
            ms_c = ph.op("scalar", lambda e: e.copy(gb_sb[0:1, 0:32], gb_s[0:1, 0:32]), deps=[ms_gb], inc=True)
            ms_b = ph.op("tensor", lambda e: e.matmul(psR[:, 0:32], self.onesrow[0:1, :],
                                                      gb_sb[0:1, 0:32], start=True, stop=True),
                         deps=[ms_c, self.ms_ones], inc=True)
            ms_gbb = ph.op("vector", lambda e: e.tensor_copy(gb_b[:, 0:32], psR[:, 0:32]), deps=[ms_b], inc=True)
            guard_r = [ms_gbb]
            self._sig_guard = None
            for tg in range(NT // 4):
                deps0 = self.ms_xfmb + [ms_ld] + guard_r
                for c in range(ND):
                    lhs = gw_s[:, c * 32:(c + 1) * 32]
                    rhs = self.xfmb_chunk(c)[:, tg * 512:(tg + 1) * 512]
                    ph.op("tensor", lambda e, l=lhs, r=rhs, st=(c == 0), sp=(c == ND - 1):
                          e.matmul(psR[0:32, :], l, r, start=st, stop=sp),
                          deps=deps0 if c == 0 else (), inc=(c == ND - 1))
                ms_mm = self.tmark()
                ms_ev = ph.op("vector", lambda e: e.tensor_copy(logit[0:32, 0:512], psR[0:32, :]),
                              deps=[ms_mm, self._sig_guard], inc=True)
                for ii in range(4):
                    i = tg * 4 + ii
                    tms = ph.op("tensor", lambda e, ii=ii: e.transpose(
                        rtr_b[:, 0:32], logit[0:32, ii * 128:(ii + 1) * 128], self.identb[0:32, 0:32]),
                        deps=[ms_ev, self._sig_guard], inc=True)
                    ms_sg = ph.op("scalar", lambda e, i=i: e.activation(
                        SR[:, i * 32:(i + 1) * 32], rtr_b[:, 0:32], AF.Sigmoid), deps=[tms], inc=True)
                    self._sig_guard = ms_sg
                guard_r = [ms_ev]

            ph = self.phase("l0f_route")
            NE32 = NT * 32
            vv = lambda t: t[:, 0:NE32]
            grp8 = lambda t: t.ap(0, [[32, NT], [8, 4], [1, 8]])
            te = lambda t: t.ap(0, [[32, NT], [1, 32]])
            ms0 = ph.op("vector", lambda e: e.tensor_tensor(
                te(SC), te(SR), gb_b.ap(0, [[0, NT], [1, 32]]), ALU.add),
                deps=[(kb.sems["scalar"], kb.sems["scalar"].n)], inc=True)
            ms1 = ph.op("vector", lambda e: e.reduce_max(
                g64a[:, 0:NT * 4], grp8(SC), axis=AX.X), deps=[ms0], inc=True)
            ms2 = ph.op("vector", lambda e: e.tensor_tensor(
                grp8(EQ), grp8(SC), g64a.ap(0, [[4, NT], [1, 4], [0, 8]]), ALU.is_equal),
                deps=[ms1], inc=True)
            ms3 = ph.op("vector", lambda e: e.scalar_tensor_tensor(
                vv(EQ), vv(EQ), -1.0e30, vv(SC), ALU.mult, ALU.add), deps=[ms2], inc=True)
            ms4 = ph.op("vector", lambda e: e.reduce_max(
                g64b[:, 0:NT * 4], grp8(EQ), axis=AX.X), deps=[ms3], inc=True)
            ms5 = ph.op("vector", lambda e: e.tensor_add(
                g64a[:, 0:NT * 4], g64a[:, 0:NT * 4], g64b[:, 0:NT * 4]), deps=[ms4, ms1], inc=True)
            ms6 = ph.op("vector", lambda e: e.reduce_max(
                g16a[:, 0:NT], g64a.ap(0, [[4, NT], [1, 4]]), axis=AX.X), deps=[ms5], inc=True)
            ms7 = ph.op("vector", lambda e: e.tensor_tensor(
                g64b.ap(0, [[4, NT], [1, 4]]), g64a.ap(0, [[4, NT], [1, 4]]),
                g16a.ap(0, [[1, NT], [0, 4]]), ALU.is_equal), deps=[ms6, ms5], inc=True)
            ms8 = ph.op("vector", lambda e: e.scalar_tensor_tensor(
                g64b[:, 0:NT * 4], g64b[:, 0:NT * 4], -1.0e30, g64a[:, 0:NT * 4],
                ALU.mult, ALU.add), deps=[ms7], inc=True)
            ms9 = ph.op("vector", lambda e: e.reduce_max(
                g16b[:, 0:NT], g64b.ap(0, [[4, NT], [1, 4]]), axis=AX.X), deps=[ms8], inc=True)
            ms10 = ph.op("vector", lambda e: e.tensor_tensor(
                g64b.ap(0, [[4, NT], [1, 4]]), g64a.ap(0, [[4, NT], [1, 4]]),
                g16b.ap(0, [[1, NT], [0, 4]]), ALU.is_ge), deps=[ms9, ms5], inc=True)
            msk = ph.op("vector", lambda e: e.tensor_tensor(
                grp8(EQ), grp8(SC), g64b.ap(0, [[4, NT], [1, 4], [0, 8]]), ALU.mult),
                deps=[ms10, ms0], inc=True)
            cpy = ph.op("vector", lambda e: e.tensor_copy(vv(SC), vv(EQ)), deps=[msk], inc=True)
            prev = cpy
            for it in range(7):
                mA = ph.op("vector", lambda e: e.reduce_max(
                    g64a[:, 0:NT], te(SC), axis=AX.X), deps=[prev], inc=True)
                mB = ph.op("vector", lambda e: e.tensor_tensor(
                    te(WA), te(SC), g64a.ap(0, [[1, NT], [0, 32]]), ALU.is_equal),
                    deps=[mA], inc=True)
                prev = ph.op("vector", lambda e: e.scalar_tensor_tensor(
                    vv(SC), vv(WA), -1.0e30, vv(SC), ALU.mult, ALU.add), deps=[mB], inc=True)
            mt8 = ph.op("vector", lambda e: e.reduce_max(
                g16a[:, 0:NT], te(SC), axis=AX.X), deps=[prev], inc=True)
            msel = ph.op("vector", lambda e: e.tensor_tensor(
                te(EQ), te(EQ), g16a.ap(0, [[1, NT], [0, 32]]), ALU.is_ge), deps=[mt8, msk], inc=True)
            mwr = ph.op("vector", lambda e: e.tensor_mul(vv(WA), vv(EQ), vv(SR)), deps=[msel], inc=True)
            mws = ph.op("vector", lambda e: e.reduce_sum(
                g16b[:, 0:NT], te(WA), axis=AX.X), deps=[mwr], inc=True)
            mw1 = ph.op("vector", lambda e: e.tensor_scalar_add(g16b[:, 0:NT], g16b[:, 0:NT], 1e-20),
                        deps=[mws], inc=True)
            mw2 = ph.op("vector", lambda e: e.reciprocal(g16a[:, 0:NT], g16b[:, 0:NT]),
                        deps=[mw1], inc=True)
            mw3 = ph.op("vector", lambda e: e.tensor_scalar_mul(g16a[:, 0:NT], g16a[:, 0:NT], 2.5),
                        deps=[mw2], inc=True)
            mwa = ph.op("vector", lambda e: e.tensor_tensor(
                te(WA), te(WA), g16a.ap(0, [[1, NT], [0, 32]]), ALU.mult), deps=[mw3], inc=True)

            ph = self.phase("l0f_main")
            slot_ms = [None, None]
            slot_guard = [None, None]
            pending_dn = None
            self._moe_last_dn = None
            x_add_ms = [self.ms_xtm[i] if isinstance(self.ms_xtm, list) else self.ms_xtm
                        for i in range(NT)]
            GOF, UOF, DOF = 0, 8 * 512, 8 * 512 * 2
            guardA = [None, None]
            guardB = [None, None]
            guardC = [None, None]
            guardD = [None, None]
            sil_g = [None, None]
            htm_g = [None, None]
            hfm_g = [None, None]
            for e in range(NE):
                sl = e % 2
                src = moe_d[e:e + 1, :]
                ph.op("gpsimd", lambda ee, s=src, sl=sl: ee.dma_start(
                    mw[sl][:, GOF:GOF + 8 * 512].rearrange("p (c f) -> p c f", c=8),
                    s[:, 0:524288].rearrange("o (c p f) -> (o p) c f", c=8, p=128)),
                    deps=[slot_guard[sl]], inc=True, inc_sem=slot_sem[sl], inc_n=16)
                ph.op("gpsimd", lambda ee, s=src, sl=sl: ee.dma_start(
                    mw[sl][:, UOF:UOF + 8 * 512].rearrange("p (c f) -> p c f", c=8),
                    s[:, 524288:1048576].rearrange("o (c p f) -> (o p) c f", c=8, p=128)),
                    inc=True, inc_sem=slot_sem[sl], inc_n=16)
                ph.op("gpsimd", lambda ee, s=src, sl=sl: ee.dma_start(
                    mw[sl][:, DOF:DOF + 4 * 1024].rearrange("p (c f) -> p c f", c=4),
                    s[:, 1048576:1572864].rearrange("o (c p f) -> (o p) c f", c=4, p=128)),
                    inc=True, inc_sem=slot_sem[sl], inc_n=16)
                slot_ms[sl] = (slot_sem[sl], slot_sem[sl].n)
                last_mm = None
                for i in range(NT):
                    s2 = i % 2
                    pA, pB, pC, pD = pAs[s2], pBs[s2], pCs[s2], pDs[s2]
                    pTb = pA[:].bitcast(BF16)
                    for c in range(ND):
                        ph.op("tensor", lambda ee, c=c, i=i, sl=sl, b=pA, st=(c == 0), sp=(c == ND - 1):
                              ee.matmul(b[:], self.xfmb_chunk(c)[:, i * 128:(i + 1) * 128],
                                        mw[sl][:, GOF + c * 512:GOF + (c + 1) * 512], start=st, stop=sp),
                              deps=(self.ms_xfmb + [slot_ms[sl], guardA[s2]]) if c == 0 else (),
                              inc=(c == ND - 1))
                    ms_g = self.tmark()
                    for c in range(ND):
                        ph.op("tensor", lambda ee, c=c, i=i, sl=sl, b=pB, st=(c == 0), sp=(c == ND - 1):
                              ee.matmul(b[:], self.xfmb_chunk(c)[:, i * 128:(i + 1) * 128],
                                        mw[sl][:, UOF + c * 512:UOF + (c + 1) * 512], start=st, stop=sp),
                              deps=[guardB[s2]] if c == 0 else (), inc=(c == ND - 1))
                    ms_u = self.tmark()
                    ms_sil = ph.op("scalar", lambda ee, b=pA, s2=s2: ee.activation(
                        sgil[:, s2 * 512:s2 * 512 + 512], b[:], AF.Silu),
                        deps=[ms_g, sil_g[s2]], inc=True)
                    wcol = (WA[:, i * 32 + e:i * 32 + e + 1] if e < 32 else self.fc(1.0))
                    hdeps = [ms_u, ms_sil, htm_g[s2]] + ([mwa] if e < 32 else [])
                    ms_h = ph.op("vector", lambda ee, w=wcol, b=pB, s2=s2: ee.scalar_tensor_tensor(
                        htm[:, s2 * 512:s2 * 512 + 512], b[:], w, sgil[:, s2 * 512:s2 * 512 + 512],
                        ALU.mult, ALU.mult), deps=hdeps, inc=True)
                    guardB[s2] = ms_h
                    sil_g[s2] = ms_h
                    if pending_dn is not None:
                        pending_dn()
                        pending_dn = None
                    tms = None
                    for q in range(4):
                        tms = ph.op("tensor", lambda ee, q=q, pTb=pTb, s2=s2: ee.transpose(
                            pTb[:, q * 128:(q + 1) * 128],
                            htm[:, s2 * 512 + q * 128:s2 * 512 + (q + 1) * 128], self.identb[:]),
                            deps=[ms_h, ms_sil] if q == 0 else (), inc=(q == 3))
                    htm_g[s2] = tms
                    ms_he = ph.op("vector", lambda ee, pTb=pTb, s2=s2: ee.tensor_copy(
                        hfm[:, s2 * 512:s2 * 512 + 512], pTb[:, 0:512]),
                        deps=[tms, hfm_g[s2]], inc=True)
                    guardA[s2] = ms_he

                    def pending_dn(i=i, sl=sl, s2=s2, pC=pC, pD=pD, ms_he=ms_he):
                        for fh, bank, gg in ((0, pC, guardC), (1, pD, guardD)):
                            for ic in range(4):
                                ph.op("tensor", lambda ee, ic=ic, fh=fh, sl=sl, b=bank, s2=s2,
                                      st=(ic == 0), sp=(ic == 3):
                                      ee.matmul(b[:], hfm[:, s2 * 512 + ic * 128:s2 * 512 + (ic + 1) * 128],
                                                mw[sl][:, DOF + ic * 1024 + fh * 512:DOF + ic * 1024 + fh * 512 + 512],
                                                start=st, stop=sp),
                                      deps=[ms_he, gg[s2]] if ic == 0 else (), inc=(ic == 3))
                        ms_dn = self.tmark()
                        hfm_g[s2] = ms_dn
                        self._moe_last_dn = ms_dn
                        a1 = ph.op("vector", lambda ee, i=i, b=pC: ee.tensor_add(
                            self.Xtm[:, i * 1024:i * 1024 + 512], self.Xtm[:, i * 1024:i * 1024 + 512],
                            b[:]), deps=[ms_dn, x_add_ms[i]], inc=True)
                        guardC[s2] = a1
                        a2 = ph.op("vector", lambda ee, i=i, b=pD: ee.tensor_add(
                            self.Xtm[:, i * 1024 + 512:i * 1024 + 1024], self.Xtm[:, i * 1024 + 512:i * 1024 + 1024],
                            b[:]), deps=[ms_dn, a1], inc=True)
                        guardD[s2] = a2
                        x_add_ms[i] = a2
                pending_dn()
                pending_dn = None
                slot_guard[sl] = self._moe_last_dn
            self.ms_x_raw = x_add_ms


    def emit_rope_l1(self, ph, src, dst, ctab, stab, rt1, rt2, deps):
        ev = lambda t, o: t.ap(o, [[2, 256]])
        cosap = ctab[:, 0:256]
        sinap = stab[:, 0:256]
        r1 = rt1[:, 0:256]
        r2 = rt2[:, 0:256]
        g = self._rope_guard
        m1 = ph.op("vector", lambda e: e.tensor_tensor(r1, ev(src, 0), cosap, ALU.mult), deps=list(deps) + [g], inc=True)
        m2 = ph.op("gpsimd", lambda e: e.tensor_tensor(r2, ev(src, 1), sinap, ALU.mult), deps=list(deps) + [g], inc=True)
        m3 = ph.op("vector", lambda e: e.tensor_tensor(ev(dst, 0), r1, r2, ALU.subtract), deps=[m1, m2], inc=True)
        m4 = ph.op("vector", lambda e: e.tensor_tensor(r1, ev(src, 0), sinap, ALU.mult), deps=[m3], inc=True)
        m5 = ph.op("gpsimd", lambda e: e.tensor_tensor(r2, ev(src, 1), cosap, ALU.mult), deps=[m3], inc=True)
        m6 = ph.op("vector", lambda e: e.tensor_tensor(ev(dst, 1), r1, r2, ALU.add), deps=[m4, m5], inc=True)
        self._rope_guard = m6
        return m6

    def p_l1_attn(self):
        T, NT, NQG = self.T, self.NT, self.NQG
        nc, kb = self.nc, self.kb
        l1s = 0.031008683647302115

        wdq_d = self.dram_in("l1_wdq", [1024, 512])
        wdkv_d = self.dram_in("l1_wdkv", [1024, 512])
        wuq_d = self.dram_in("l1_wuq", [512, 1024])
        wqr_d = self.dram_in("l1_wqr", [512, 1024])
        wkr_d = self.dram_in("l1_wkr", [1024, 1024])
        wuk_d = self.dram_in("l1_wuk", [512, 1024])
        wuv_d = self.dram_in("l1_wuv", [512, 1024])
        wo_d = self.dram_in("l1_wo", [1024, 1024])
        ab_d = self.dram_in("l1ab", [8, 1024])
        cos_d = self.dram_in("l1cos", [T, 512])
        sin_d = self.dram_in("l1sin", [T, 512])

        self.arena_reset()
        ctq = self.aB(4 * T)
        ctkv = self.aB(4 * T)
        Kfm = self.aB(4 * T)
        Vtm = self.aB(NT * 512)
        PT = self.aB(1024)
        Qfm = self.aB(2048)
        yfm = self.aB(2048)
        qc_s = self.aB(512)
        qr_s = self.aB(512)
        qr2 = self.aB(512)
        wA = self.aB(4096)
        wB = self.aB(2048)
        wC = self.aB(2048)
        bb1 = self.aB(512)
        bb2 = self.aB(512)
        brow = self.aB(544)
        bb3 = brow.sub(0, 512)
        rstdb = qr2
        bcol = self.aF(16)
        rt1 = self.aF(256)
        rt2 = self.aF(256)
        u1 = self.aF(1536)
        sqs = u1.sub(0, 512)
        ctab = u1.sub(512, 256)
        stab = u1.sub(768, 256)
        o_row = u1.sub(1024, 512)
        kn2 = self.aF(128)
        qn2 = self.aF(32)
        kqm = self.aF(16)
        nmb = self.aF(8)
        mrow = self.aF(16)

        self._dbg = dict(ctq=ctq.base, ctkv=ctkv.base, Kfm=Kfm.base, Vtm=Vtm.base,
                         Qfm=Qfm.base, yfm=yfm.base, qn2=qn2.base, kn2=kn2.base,
                         nmb=nmb.base, PT=PT.base, mrow=mrow.base, kqm=kqm.base)
        with contextlib.ExitStack() as pes:
            E = pes.enter_context
            pst = lambda n, s, d=F32: E(nc.psum_tensor(n, list(s), d))
            ps_gen = pst("l1_gen", [128, 512])
            ps_st = [pst("l1_st0", [128, 512]), pst("l1_st1", [128, 512])]
            ps_o = pst("l1_o", [128, 512])
            ps_ob = [ps_o, pst("l1_o1", [128, 512])]
            ps_rr = pst("l1_rr", [128, 512])
            ps_nb = pst("l1_nb", [128, 512])
            gen_b = ps_gen[:].bitcast(BF16)

            brows = u1

            def bias_load(ph, specs):
                for j, (row, cols) in enumerate(specs):
                    n = cols.stop - cols.start
                    ph.dma("sync", brows[0:1, j * 512:j * 512 + n], ab_d[row:row + 1, cols])
                return ph.qtotal("sync")

            def bias_bcast(ph, j, n, dstb, ms_rows):
                m2 = ph.op("scalar", lambda e: e.copy(brow[0:1, 0:n], brows[0:1, j * 512:j * 512 + n]),
                           deps=[ms_rows, self._brow_guard], inc=True)
                m3 = ph.op("tensor", lambda e: e.matmul(ps_rr[:, 0:n], self.onesrow[0:1, :],
                                                        brow[0:1, 0:n], start=True, stop=True),
                           deps=[m2, self.ms_ones, self._rr_guard], inc=True)
                m4 = ph.op("vector", lambda e: e.tensor_copy(dstb[:, 0:n], ps_rr[:, 0:n]),
                           deps=[m3], inc=True)
                self._rr_guard = m4
                self._brow_guard = m3
                return m4

            self._rr_guard = None
            self._rope_guard = None
            self._brow_guard = None
            self._st_guard = [None, None]
            self._ps_o_guard = [None, None]
            self._orow_guard = None
            self._nb_guard = None
            self._rstd_guard = None

            ph = self.phase("l1a_pre")
            for c in range(4):
                ph.dma("sync", bcol[:, c:c + 1], ab_d[0:1, c * 128:(c + 1) * 128].rearrange("o (p q) -> (o p) q", q=1))
                ph.dma("sync", bcol[:, 4 + c:5 + c], ab_d[1:2, c * 128:(c + 1) * 128].rearrange("o (p q) -> (o p) q", q=1))
            ms_rows = bias_load(ph, [(7, slice(0, 512)), (7, slice(512, 1024))])
            ms_bcol = ms_rows
            mwo = bias_bcast(ph, 0, 512, bb1, ms_rows)
            mwo2 = bias_bcast(ph, 1, 512, bb2, ms_rows)
            x_ms = list(self.ms_xtm) if isinstance(self.ms_xtm, list) else [self.ms_xtm] * NT
            for i in range(NT):
                a1 = ph.op("vector", lambda e, i=i: e.tensor_add(
                    self.Xtm[:, i * 1024:i * 1024 + 512], self.Xtm[:, i * 1024:i * 1024 + 512], bb1[:, 0:512]),
                    deps=[mwo, x_ms[i]], inc=True)
                a2 = ph.op("vector", lambda e, i=i: e.tensor_add(
                    self.Xtm[:, i * 1024 + 512:i * 1024 + 1024], self.Xtm[:, i * 1024 + 512:i * 1024 + 1024], bb2[:, 0:512]),
                    deps=[mwo2, a1], inc=True)
                x_ms[i] = a2
            x_add_ms = x_ms
            if self.upto == "l1s_pre":
                self.ms_x_raw = x_add_ms
                return

            for which, wsrc, dst, bofs in (("q", wdq_d, ctq, 0), ("kv", wdkv_d, ctkv, 4)):
                ph = self.phase(f"l1a_ct{which}")
                ph.dma("gpsimd", wA[:, 0:4096].rearrange("p (c f) -> p c f", c=8),
                       wsrc.rearrange("(c p) f -> p c f", p=128))
                ms_w = ph.qtotal("gpsimd")
                guard = []
                for cqc in range(4):
                    for tg in range(NQG):
                        deps0 = self.ms_xfmb + [ms_w, ms_bcol] + guard
                        for dc in range(ND):
                            lhs = wA[:, dc * 512 + cqc * 128: dc * 512 + cqc * 128 + 128]
                            rhs = self.xfmb_chunk(dc)[:, tg * 512:(tg + 1) * 512]
                            ph.op("tensor", lambda e, l=lhs, r=rhs, st=(dc == 0), sp=(dc == ND - 1):
                                  e.matmul(ps_gen[:], l, r, start=st, stop=sp),
                                  deps=deps0 if dc == 0 else (), inc=(dc == ND - 1))
                        ms_mm = self.tmark()
                        ms_e = ph.op("vector", lambda e, cqc=cqc, tg=tg, d=dst, b=bofs: e.tensor_scalar(
                            d[:, cqc * T + tg * 512: cqc * T + tg * 512 + 512], ps_gen[:],
                            bcol[:, b + cqc:b + cqc + 1], None, ALU.add), deps=[ms_mm], inc=True)
                        guard = [ms_e]
                ms_ct = guard[0]
            if self.upto == "l1s_ct":
                self.ms_x_raw = x_add_ms
                return

            for hp in range(2):
                f0 = hp * 512
                ph = self.phase(f"l1a_kv{hp}")
                ph.dma("gpsimd", wA[:, 0:4096].rearrange("p (c f) -> p c f", c=8),
                       wkr_d[:, f0:f0 + 512].rearrange("(c p) f -> p c f", p=128),
                       deps=[self.tmark()])
                ph.dma("gpsimd", wB[:, 0:2048].rearrange("p (c f) -> p c f", c=4),
                       wuk_d[:, f0:f0 + 512].rearrange("(c p) f -> p c f", p=128))
                ph.dma("gpsimd", wC[:, 0:2048].rearrange("p (c f) -> p c f", c=4),
                       wuv_d[:, f0:f0 + 512].rearrange("(c p) f -> p c f", p=128))
                ms_w = ph.qtotal("gpsimd")
                ms_rows = bias_load(ph, [(4, slice(f0, f0 + 512)), (5, slice(f0, f0 + 512)),
                                         (6, slice(f0, f0 + 512))])
                mb_kr = bias_bcast(ph, 0, 512, bb1, ms_rows)
                mb_uk = bias_bcast(ph, 1, 512, bb2, ms_rows)
                mb_uv = bias_bcast(ph, 2, 512, bb3, ms_rows)
                guard_gen = [mb_uv]
                guard_K = []
                ms_tab = None
                for i in range(NT):
                    ph.dma("sync", ctab[:, 0:256], cos_d[i * 128:(i + 1) * 128, hp * 256:hp * 256 + 256],
                           deps=[self._rope_guard, mb_uv])
                    ph.dma("sync", stab[:, 0:256], sin_d[i * 128:(i + 1) * 128, hp * 256:hp * 256 + 256])
                    ms_tab = ph.qtotal("sync")
                    deps0 = self.ms_xfmb + [ms_w] + guard_gen
                    for dc in range(ND):
                        lhs = self.xfmb_chunk(dc)[:, i * 128:(i + 1) * 128]
                        rhs = wA[:, dc * 512:(dc + 1) * 512]
                        ph.op("tensor", lambda e, l=lhs, r=rhs, st=(dc == 0), sp=(dc == ND - 1):
                              e.matmul(ps_gen[:], l, r, start=st, stop=sp),
                              deps=deps0 if dc == 0 else (), inc=(dc == ND - 1))
                    ms_mm = self.tmark()
                    ms_kr = ph.op("vector", lambda e: e.tensor_add(qr_s[:, 0:512], ps_gen[:], bb1[:, 0:512]),
                                  deps=[ms_mm, mb_kr, self._rope_guard], inc=True)
                    guard_gen = [ms_kr]
                    if self.upto == "l1s_kva":
                        mv = ms_kr
                        ms_vtile = mv
                        continue
                    ms_rope = self.emit_rope_l1(ph, qr_s, qr2, ctab, stab, rt1, rt2, deps=[ms_kr, ms_tab])
                    if self.upto == "l1s_kvb":
                        mv = ms_rope
                        ms_vtile = mv
                        continue
                    deps0 = [ms_ct, ms_w] + guard_gen
                    for cqc in range(4):
                        lhs = ctkv[:, cqc * T + i * 128: cqc * T + i * 128 + 128]
                        rhs = wB[:, cqc * 512:(cqc + 1) * 512]
                        ph.op("tensor", lambda e, l=lhs, r=rhs, st=(cqc == 0), sp=(cqc == 3):
                              e.matmul(ps_gen[:], l, r, start=st, stop=sp),
                              deps=deps0 if cqc == 0 else (), inc=(cqc == 3))
                    ms_mm = self.tmark()
                    m1 = ph.op("vector", lambda e: e.tensor_add(qc_s[:, 0:512], ps_gen[:], bb2[:, 0:512]),
                               deps=[ms_mm, mb_uk], inc=True)
                    m2 = ph.op("vector", lambda e: e.tensor_add(qc_s[:, 0:512], qc_s[:, 0:512], qr2[:, 0:512]),
                               deps=[m1, ms_rope], inc=True)
                    guard_gen = [m2]
                    if self.upto == "l1s_kvc":
                        mv = m2
                        ms_vtile = mv
                        continue
                    msq = ph.op("scalar", lambda e: e.activation(sqs[:, 0:512], qc_s[:, 0:512], AF.Square),
                                deps=[m2], inc=True)
                    mkn = ph.op("vector", lambda e, i=i: e.reduce_sum(
                        kn2[:, i * 8:(i + 1) * 8], sqs.ap(0, [[64, 8], [1, 64]]), axis=AX.X),
                        deps=[msq], inc=True)
                    tms = None
                    for cl in range(4):
                        d0 = gen_b[:, cl * 128:cl * 128 + 128]
                        tms = ph.op("tensor", lambda e, d=d0, cl=cl: e.transpose(
                            d, qc_s[:, cl * 128:(cl + 1) * 128], self.identb[:]),
                            deps=([m2, msq] + guard_gen) if cl == 0 else (), inc=(cl == 3))
                    evs = None
                    for cl in range(4):
                        d0 = gen_b[:, cl * 128:cl * 128 + 128]
                        evs = ph.op("vector", lambda e, d=d0, cl=cl, i=i: e.tensor_copy(
                            Kfm[:, cl * T + i * 128: cl * T + i * 128 + 128], d), deps=[tms], inc=True)
                    guard_gen = [evs]
                    guard_K = [evs]
                    if self.upto == "l1s_kvd":
                        mv = evs
                        ms_vtile = mv
                        continue
                    deps0 = [ms_ct, ms_w] + guard_gen
                    for cqc in range(4):
                        lhs = ctkv[:, cqc * T + i * 128: cqc * T + i * 128 + 128]
                        rhs = wC[:, cqc * 512:(cqc + 1) * 512]
                        ph.op("tensor", lambda e, l=lhs, r=rhs, st=(cqc == 0), sp=(cqc == 3):
                              e.matmul(ps_gen[:], l, r, start=st, stop=sp),
                              deps=deps0 if cqc == 0 else (), inc=(cqc == 3))
                    ms_mm = self.tmark()
                    mv = ph.op("vector", lambda e, i=i: e.tensor_add(
                        Vtm[:, i * 512:(i + 1) * 512], ps_gen[:], bb3[:, 0:512]),
                        deps=[ms_mm, mb_uv], inc=True)
                    guard_gen = [mv]
                    ms_vtile = mv
                    if self.upto == "l1s_kv1t":
                        break
                if self.upto in ("l1s_kva", "l1s_kvb", "l1s_kvc", "l1s_kvd", "l1s_kve"):
                    self.ms_x_raw = x_add_ms
                    return
                ms_kn_all = mkn
                ms_v_all = mv
                nk = NT * 8
                t1 = ph.op("tensor", lambda e, nk=nk: e.transpose(ps_gen[0:nk, 0:128], kn2[:, 0:nk], self.identf[:]),
                           deps=[ms_kn_all, self.ms_identf] + guard_gen, inc=True)
                r1m = ph.op("vector", lambda e, nk=nk: e.reduce_max(kqm[0:nk, 8:9],
                            bass.AP(ps_gen, 0, [[512, nk], [1, 128]]), axis=AX.X),
                            deps=[t1], inc=True)
                t2 = ph.op("tensor", lambda e, nk=nk: e.transpose(ps_gen[0:1, 0:nk], kqm[0:nk, 8:9],
                                                                  self.identf[0:nk, 0:nk]),
                           deps=[r1m], inc=True)
                r2m = ph.op("vector", lambda e: e.reduce_max(
                    mrow[0:1, 0:8], bass.AP(ps_gen, 0, [[512, 1], [1, 8], [8, NT]]), axis=AX.X),
                    deps=[t2], inc=True)

                ph = self.phase(f"l1a_w{hp}")
                ph.dma("gpsimd", wB[:, 0:2048].rearrange("p (c f) -> p c f", c=4),
                       wuq_d[:, f0:f0 + 512].rearrange("(c p) f -> p c f", p=128),
                       deps=[self.tmark()])
                ph.dma("gpsimd", wC[:, 0:2048].rearrange("p (c f) -> p c f", c=4),
                       wqr_d[:, f0:f0 + 512].rearrange("(c p) f -> p c f", p=128))
                ph.dma("gpsimd", wA[:, 0:4096].rearrange("p (c f) -> p c f", c=4),
                       wo_d[f0:f0 + 512, :].rearrange("(c p) f -> p c f", p=128))
                ms_w = ph.qtotal("gpsimd")
                ms_rows = bias_load(ph, [(2, slice(f0, f0 + 512)), (3, slice(f0, f0 + 512))])
                mb_uq = bias_bcast(ph, 0, 512, bb1, ms_rows)
                mb_qr = bias_bcast(ph, 1, 512, bb2, ms_rows)
                mcast = ph.op("vector", lambda e: e.tensor_copy(brow[0:1, 0:8], mrow[0:1, 0:8]),
                              deps=[r2m], inc=True)
                mkb = ph.op("tensor", lambda e: e.matmul(ps_rr[:, 0:8], self.onesrow[0:1, :],
                                                         brow[0:1, 0:8], start=True, stop=True),
                            deps=[mcast, self.ms_ones, self._rr_guard], inc=True)
                mkbe = ph.op("vector", lambda e: e.tensor_copy(kqm[:, 0:8], ps_rr[:, 0:8]),
                             deps=[mkb], inc=True)
                self._rr_guard = mkbe
                if self.upto in ("l1s_kv", "l1s_kv1t") or (self.upto == "l1s_hp1kv" and hp == 1):
                    self.ms_x_raw = x_add_ms
                    return
                _micro = int(self.upto[8:]) if self.upto.startswith("l1s_hp1m") else 99

                def _mgate(n):
                    if hp == 1 and _micro <= n:
                        self.ms_x_raw = x_add_ms
                        return True
                    return False

                for qg in range(NQG):
                    if qg > 0:
                        ph = self.phase(f"l1a_w{hp}g{qg}")
                    ms_qfm = None
                    guard_gen2 = [mkbe]
                    for ii in range(4):
                        i = qg * 4 + ii
                        if self.upto == "l1s_hp1v1" and hp == 1:
                            ph.dma("gpsimd", ctab[:, 0:256], cos_d[i * 128:(i + 1) * 128, hp * 256:hp * 256 + 256],
                                   deps=[self._rope_guard, mb_qr])
                            ph.dma("gpsimd", stab[:, 0:256], sin_d[i * 128:(i + 1) * 128, hp * 256:hp * 256 + 256])
                            self.ms_x_raw = x_add_ms
                            return
                        if self.upto == "l1s_hp1v2" and hp == 1:
                            ph.dma("sync", ctab[:, 0:256], cos_d[i * 128:(i + 1) * 128, hp * 256:hp * 256 + 256])
                            ph.dma("sync", stab[:, 0:256], sin_d[i * 128:(i + 1) * 128, hp * 256:hp * 256 + 256])
                            self.ms_x_raw = x_add_ms
                            return
                        if self.upto == "l1s_hp1v3" and hp == 1:
                            ph.dma("sync", ctab[:, 0:256], cos_d[i * 128:(i + 1) * 128, hp * 256:hp * 256 + 256],
                                   deps=[self._rope_guard, mb_qr])
                            self.ms_x_raw = x_add_ms
                            return
                        ph.dma("sync", ctab[:, 0:256], cos_d[i * 128:(i + 1) * 128, hp * 256:hp * 256 + 256],
                               deps=[self._rope_guard, mb_qr])
                        ph.dma("sync", stab[:, 0:256], sin_d[i * 128:(i + 1) * 128, hp * 256:hp * 256 + 256])
                        ms_tab = ph.qtotal("sync")
                        if _mgate(1):
                            return
                        deps0 = [ms_ct, ms_w] + guard_gen2
                        for cqc in range(4):
                            lhs = ctq[:, cqc * T + i * 128: cqc * T + i * 128 + 128]
                            rhs = wC[:, cqc * 512:(cqc + 1) * 512]
                            ph.op("tensor", lambda e, l=lhs, r=rhs, st=(cqc == 0), sp=(cqc == 3):
                                  e.matmul(ps_gen[:], l, r, start=st, stop=sp),
                                  deps=deps0 if cqc == 0 else (), inc=(cqc == 3))
                        ms_mm = self.tmark()
                        ms_qr = ph.op("vector", lambda e: e.tensor_add(qr_s[:, 0:512], ps_gen[:], bb2[:, 0:512]),
                                      deps=[ms_mm, mb_qr, self._rope_guard], inc=True)
                        guard_gen2 = [ms_qr]
                        if _mgate(2):
                            return
                        ms_rope = self.emit_rope_l1(ph, qr_s, qr2, ctab, stab, rt1, rt2, deps=[ms_qr, ms_tab])
                        if _mgate(3):
                            return
                        deps0 = [ms_ct] + guard_gen2
                        for cqc in range(4):
                            lhs = ctq[:, cqc * T + i * 128: cqc * T + i * 128 + 128]
                            rhs = wB[:, cqc * 512:(cqc + 1) * 512]
                            ph.op("tensor", lambda e, l=lhs, r=rhs, st=(cqc == 0), sp=(cqc == 3):
                                  e.matmul(ps_gen[:], l, r, start=st, stop=sp),
                                  deps=deps0 if cqc == 0 else (), inc=(cqc == 3))
                        ms_mm = self.tmark()
                        m1 = ph.op("vector", lambda e: e.tensor_add(qc_s[:, 0:512], ps_gen[:], bb1[:, 0:512]),
                                   deps=[ms_mm, mb_uq], inc=True)
                        m2 = ph.op("vector", lambda e: e.tensor_add(qc_s[:, 0:512], qc_s[:, 0:512], qr2[:, 0:512]),
                                   deps=[m1, ms_rope], inc=True)
                        guard_gen2 = [m2]
                        if _mgate(4):
                            return
                        msq = ph.op("scalar", lambda e: e.activation(sqs[:, 0:512], qc_s[:, 0:512], AF.Square),
                                    deps=[m2], inc=True)
                        mqn = ph.op("vector", lambda e, ii=ii: e.reduce_sum(
                            qn2[:, ii * 8:(ii + 1) * 8], sqs.ap(0, [[64, 8], [1, 64]]), axis=AX.X),
                            deps=[msq], inc=True)
                        tms = None
                        for cl in range(4):
                            d0 = gen_b[:, cl * 128:cl * 128 + 128]
                            tms = ph.op("tensor", lambda e, d=d0, cl=cl: e.transpose(
                                d, qc_s[:, cl * 128:(cl + 1) * 128], self.identb[:]),
                                deps=([m2, msq] + guard_gen2) if cl == 0 else (), inc=(cl == 3))
                        evs = None
                        for cl in range(4):
                            d0 = gen_b[:, cl * 128:cl * 128 + 128]
                            evs = ph.op("vector", lambda e, d=d0, cl=cl, ii=ii: e.tensor_copy(
                                Qfm[:, cl * 512 + ii * 128: cl * 512 + ii * 128 + 128], d), deps=[tms], inc=True)
                        guard_gen2 = [evs]
                        ms_qfm = evs
                        if self.upto.startswith("l1s_hp1q") and hp == 1 and ii + 1 >= int(self.upto[8:]):
                            self.ms_x_raw = x_add_ms
                            return
                    if self.upto == "l1s_wq" or (self.upto == "l1s_hp1wq" and hp == 1) or \
                       (self.upto.startswith("l1s_qb") and qg + 1 >= int(self.upto[6:])):
                        self.ms_x_raw = x_add_ms
                        return
                    tq1 = ph.op("tensor", lambda e: e.transpose(ps_gen[0:32, 0:128], qn2[:, 0:32], self.identf[:]),
                                deps=[mqn, self.ms_identf] + guard_gen2, inc=True)
                    rq1 = ph.op("vector", lambda e: e.reduce_max(
                        kqm[0:32, 9:10], bass.AP(ps_gen, 0, [[512, 32], [1, 128]]), axis=AX.X),
                        deps=[tq1], inc=True)
                    tq2 = ph.op("tensor", lambda e: e.transpose(ps_gen[0:1, 0:32], kqm[0:32, 9:10], self.identf[0:32, 0:32]),
                                deps=[rq1], inc=True)
                    rq2 = ph.op("vector", lambda e: e.reduce_max(
                        mrow[0:1, 8:16], bass.AP(ps_gen, 0, [[512, 1], [1, 8], [8, 4]]), axis=AX.X),
                        deps=[tq2], inc=True)
                    n1 = ph.op("vector", lambda e: e.tensor_mul(mrow[0:1, 8:16], mrow[0:1, 8:16], mrow[0:1, 0:8]),
                               deps=[rq2, r2m], inc=True)
                    n2 = ph.op("scalar", lambda e: e.activation(mrow[0:1, 8:16], mrow[0:1, 8:16], AF.Sqrt),
                               deps=[n1], inc=True)
                    n3 = ph.op("vector", lambda e: e.tensor_scalar_mul(mrow[0:1, 8:16], mrow[0:1, 8:16], -l1s),
                               deps=[n2], inc=True)
                    n4 = ph.op("vector", lambda e: e.tensor_copy(brow[0:1, 8:16], mrow[0:1, 8:16]),
                               deps=[n3], inc=True)
                    n5 = ph.op("tensor", lambda e: e.matmul(ps_rr[:, 8:16], self.onesrow[0:1, :],
                                                            brow[0:1, 8:16], start=True, stop=True),
                               deps=[n4, self._rr_guard], inc=True)
                    n6 = ph.op("vector", lambda e: e.tensor_copy(nmb[:, 0:8], ps_rr[:, 8:16]),
                               deps=[n5], inc=True)
                    self._rr_guard = n6
                    pending_norm = None
                    for hl in range(8):
                        qb = 64 * (hl % 2)
                        kch = hl // 2
                        ms_av = None
                        guard_pt = [None, None]
                        ms_exp = [None] * NT
                        def emit_mss(kt, hl=hl, qb=qb, kch=kch, ms_exp=ms_exp, guard_pt=guard_pt):
                            lhs = Kfm[qb:qb + 64, kch * T + kt * 128: kch * T + kt * 128 + 128]
                            rhs = Qfm[qb:qb + 64, kch * 512: kch * 512 + 512]
                            mss = ph.op("tensor", lambda e, l=lhs, r=rhs, b=ps_st[kt % 2], tp=(qb, 0):
                                        e.matmul(b[:], l, r, start=True, stop=True, tile_position=tp),
                                        deps=[ms_qfm, guard_K[0], self._st_guard[kt % 2]], inc=True)
                            ms_exp[kt] = ph.op("scalar", lambda e, b=ps_st[kt % 2], kt=kt, hl=hl:
                                        e.activation(PT[:, (kt % 2) * 512:(kt % 2) * 512 + 512], b[:],
                                                     AF.Exp, bias=nmb[:, hl:hl + 1], scale=self.fc(l1s)),
                                        deps=[mss, n6, guard_pt[kt % 2]], inc=True)
                            self._st_guard[kt % 2] = ms_exp[kt]
                        emit_mss(0)
                        if NT > 1:
                            emit_mss(1)
                        if pending_norm is not None:
                            pending_norm()
                            pending_norm = None
                        for kt in range(NT):
                            msx = ms_exp[kt]
                            lmm = ph.op("tensor", lambda e, kt=kt: e.matmul(
                                ps_rr[0:1, :], self.onesb[:, 0:1], PT[:, (kt % 2) * 512:(kt % 2) * 512 + 512],
                                start=(kt == 0), stop=(kt == NT - 1)),
                                deps=[msx, self.ms_ones, self._rr_guard if kt == 0 else None], inc=True)
                            ms_av = ph.op("tensor", lambda e, kt=kt, hl=hl, qb=qb: e.matmul(
                                ps_ob[hl % 2][qb:qb + 64, :], Vtm[:, kt * 512 + hl * 64: kt * 512 + hl * 64 + 64],
                                PT[:, (kt % 2) * 512:(kt % 2) * 512 + 512],
                                start=(kt == 0), stop=(kt == NT - 1)),
                                deps=[msx, ms_v_all, self._ps_o_guard[hl % 2] if kt == 0 else None], inc=True)
                            guard_pt[kt % 2] = ms_av
                            if kt + 2 < NT:
                                emit_mss(kt + 2)

                        def pending_norm(hl=hl, qb=qb, lmm=lmm, ms_av=ms_av):
                            e2 = ph.op("vector", lambda e: e.reciprocal(o_row[0:1, 0:512], ps_rr[0:1, :]),
                                       deps=[lmm, self._orow_guard], inc=True)
                            self._rr_guard = e2
                            e3 = ph.op("vector", lambda e: e.tensor_copy(brow[0:1, 16:528], o_row[0:1, 0:512]),
                                       deps=[e2, self._brow_guard], inc=True)
                            self._orow_guard = e3
                            e4 = ph.op("tensor", lambda e, qb=qb: e.matmul(ps_nb[qb:qb + 64, :], self.onesrow[0:1, 0:64],
                                                                           brow[0:1, 16:528], start=True, stop=True),
                                       deps=[e3, self._nb_guard], inc=True)
                            self._brow_guard = e4
                            e5 = ph.op("scalar", lambda e, qb=qb: e.copy(rstdb[qb:qb + 64, 0:512], ps_nb[qb:qb + 64, :]),
                                       deps=[e4, self._rstd_guard], inc=True)
                            self._nb_guard = e5
                            ydst = yfm[qb:qb + 64, (hl // 2) * 512:(hl // 2) * 512 + 512]
                            e6 = ph.op("vector", lambda e, d=ydst, qb=qb, hl=hl: e.tensor_mul(
                                d, ps_ob[hl % 2][qb:qb + 64, :], rstdb[qb:qb + 64, 0:512]), deps=[e5, ms_av], inc=True)
                            self._rstd_guard = e6
                            self._ps_o_guard[hl % 2] = e6
                    pending_norm()
                    if self.upto in ("l1s_hp1wh1", "l1s_wh1"):
                        self.ms_x_raw = x_add_ms
                        return
                    yv = (kb.sems["vector"], kb.sems["vector"].n)
                    for ii in range(4):
                        qt = qg * 4 + ii
                        for fh in range(2):
                            deps0 = [yv, ms_w] + guard_gen2
                            for c in range(4):
                                lhs = yfm[:, c * 512 + ii * 128: c * 512 + ii * 128 + 128]
                                rhs = wA[:, c * 1024 + fh * 512: c * 1024 + fh * 512 + 512]
                                ph.op("tensor", lambda e, l=lhs, r=rhs, st=(c == 0), sp=(c == 3):
                                      e.matmul(ps_gen[:], l, r, start=st, stop=sp),
                                      deps=deps0 if c == 0 else (), inc=(c == 3))
                            ms_mm = self.tmark()
                            dstx = self.Xtm[:, qt * 1024 + fh * 512: qt * 1024 + fh * 512 + 512]
                            ms_res = ph.op("vector", lambda e, d=dstx: e.tensor_add(d, d, ps_gen[:]),
                                           deps=[ms_mm, x_add_ms[qt]], inc=True)
                            guard_gen2 = [ms_res]
                            x_add_ms[qt] = ms_res
                    if self.upto == "l1s_wo1" or (self.upto == "l1s_hp1w1" and hp == 1) or \
                       (self.upto.startswith("l1s_wg") and qg + 1 >= int(self.upto[6:])):
                        self.ms_x_raw = x_add_ms
                        return
                if self.upto == "l1s_whp0":
                    self.ms_x_raw = x_add_ms
                    return
            self.ms_x_raw = x_add_ms


    def p_l1_ffn(self):
        T, NT, NQG = self.T, self.NT, self.NQG
        nc, kb = self.nc, self.kb
        NW = NQG
        HID = 2816
        NI = HID // 128

        wqkv_d = self.dram_in("f_wqkv", [2, 1024, 1536])
        wo_d = self.dram_in("f_wo", [2, 512, 1024])
        w1_d = self.dram_in("f_w1", [2, 1024, 2 * HID])
        w2_d = self.dram_in("f_w2", [2, HID, 1024])
        b1_d = self.dram_in("f_b1", [2, 2 * HID])
        b2_d = self.dram_in("f_b2", [2, 1024])
        lng_d = self.dram_in("f_lng", [2, 1024])
        lnb_d = self.dram_in("f_lnb", [2, 1024])
        qs_d = self.dram_in("f_qs", [2, 64])
        ks_d = self.dram_in("f_ks", [2, 64])
        xq1_d = self.dram_in("xq1", [T, 64])
        xq2_d = self.dram_in("xq2", [T, 64])
        xk1_d = self.dram_in("xk1", [T, 64])
        xk2_d = self.dram_in("xk2", [T, 64])

        self.arena_reset()
        xpt = self.aB(1024)
        sqb = self.aB(512)
        t1b = self.aB(512)
        qsb = self.aB(64)
        ksb = self.aB(64)
        mb_ = self.aB(512)
        rb_ = self.aB(512)
        xln = [self.aB(4096), self.aB(4096)]
        w1s = [self.aB(1024), self.aB(1024), self.aB(1024), self.aB(1024)]
        ggb = self.aB(512)
        dtmp = self.aB(512)
        browf = self.aB(544)
        wqk = self.aB(8192)
        qtm = self.aB(512)
        ktm = self.aB(512)
        q3b = self.aB(512)
        Pb = self.aB(768)
        PTb = self.aB(768)
        yfm = self.aB(2048)
        wos = self.aB(2048)
        w2r = wqk.sub(0, 11264)
        Qfm = [self.aB(2048), self.aB(2048)]
        Kfm = [self.aB(2048), self.aB(2048)]
        Vtm = [self.aB(2048), self.aB(2048)]
        h4a = Qfm[0].sub(0, 11264)
        b1c = self.aF(48)
        b2c = self.aF(8)
        lgc = self.aF(8)
        lbc = self.aF(8)
        ssf = self.aF(16)
        lrow = self.aF(512)
        vrow = self.aF(512)
        nmx = self.aF(8)
        bsave = self.aF(4)
        bsave2 = self.aF(4)

        with contextlib.ExitStack() as pes:
            E = pes.enter_context
            pst = lambda n, s, d=F32: E(nc.psum_tensor(n, list(s), d))
            psQK = pst("f_qk", [128, 1024])
            psS = pst("f_s", [128, 640])
            psO = pst("f_o", [128, 512])
            psG = pst("f_g", [128, 512])
            gen_b = psG[:].bitcast(BF16)
            wsem = [Sem(pes.enter_context(nc.semaphore("fw_s0"))),
                    Sem(pes.enter_context(nc.semaphore("fw_s1"))),
                    Sem(pes.enter_context(nc.semaphore("fw_s2"))),
                    Sem(pes.enter_context(nc.semaphore("fw_s3")))]

            x_ms = list(self.ms_xtm) if isinstance(self.ms_xtm, list) else [self.ms_xtm] * NT
            g = dict(gen=None, wqk=None, psqk=None, psqk2=None, sqb=None, q3=None,
                     psS=None, Pb=None, PT=None, psO=None, t1=None, stream=None,
                     xpt=None, dtmp=None, gg=None, h4=None)
            g["w1"] = [None, None, None, None]
            g["w2"] = [None, None]
            g["gg0"] = None
            g["gg1"] = None
            g["dn"] = [None, None, None, None]

            def colload(ph, dst, dsrc, row, n):
                for cix in range(n):
                    ph.dma("sync", dst[:, cix:cix + 1],
                           dsrc[row:row + 1, cix * 128:(cix + 1) * 128].rearrange("o (p q) -> (o p) q", q=1))

            for l in range(2):
                ph = self.phase(f"f{l}_setup")
                colload(ph, b1c, b1_d, l, 44)
                colload(ph, b2c, b2_d, l, 8)
                colload(ph, lgc, lng_d, l, 8)
                colload(ph, lbc, lnb_d, l, 8)
                ph.dma("sync", lrow[0:1, 0:64], qs_d[l:l + 1, :])
                ph.dma("sync", lrow[0:1, 64:128], ks_d[l:l + 1, :])
                ms_cl = ph.qtotal("sync")
                c1 = ph.op("scalar", lambda e: e.copy(browf[0:1, 0:128], lrow[0:1, 0:128]),
                           deps=[ms_cl], inc=True)
                c2 = ph.op("tensor", lambda e: e.matmul(psG[:, 0:128], self.onesrow[0:1, :],
                                                        browf[0:1, 0:128], start=True, stop=True),
                           deps=[c1, self.ms_ones, g["gen"]], inc=True)
                c3 = ph.op("vector", lambda e: e.tensor_copy(qsb[:, 0:64], psG[:, 0:64]), deps=[c2], inc=True)
                c4 = ph.op("vector", lambda e: e.tensor_copy(ksb[:, 0:64], psG[:, 64:128]), deps=[c2], inc=True)
                g["gen"] = c4
                g["xpt"] = c4
                ms_scales = c4

                def qkv_build(ph, grp):
                    sl = grp % 2
                    ph.op("gpsimd", lambda e, l=l: e.dma_start(
                        wqk[:, 0:8192].rearrange("p (c f) -> p c f", c=8),
                        wqkv_d[l, :, 0:1024].rearrange("(c p) f -> p c f", p=128)),
                        deps=[g["wqk"]], inc=True, inc_sem=wsem[3], inc_n=16)
                    ms_qk = (wsem[3], wsem[3].n)
                    for j, tab in enumerate((xq1_d, xq2_d, xk1_d, xk2_d)):
                        ph.dma("gpsimd", xpt[:, j * 256:(j + 1) * 256].rearrange("p (a c) -> p a c", c=64),
                               tab[grp * 512:(grp + 1) * 512, :].rearrange("(a p) c -> p a c", p=128),
                               deps=[g["xpt"]] if j == 0 else ())
                    ms_xpt = ph.qtotal("gpsimd")
                    kevac = None
                    ms_v_l = None
                    for ii in range(4):
                        i = grp * 4 + ii
                        deps0 = [ms_qk, g["psqk"]]
                        for c in range(ND):
                            lhs = self.xfmb_chunk(c, shifted=(c >= 4))[:, i * 128:(i + 1) * 128]
                            ph.op("tensor", lambda e, lh=lhs, c=c, st=(c == 0), sp=(c == ND - 1):
                                  e.matmul(psQK[:, 0:512], lh, wqk[:, c * 1024:c * 1024 + 512], start=st, stop=sp),
                                  deps=deps0 if c == 0 else (), inc=(c == ND - 1))
                        ms_q = self.tmark()
                        for c in range(ND):
                            lhs = self.xfmb_chunk(c, shifted=(c >= 4))[:, i * 128:(i + 1) * 128]
                            ph.op("tensor", lambda e, lh=lhs, c=c, st=(c == 0), sp=(c == ND - 1):
                                  e.matmul(psQK[:, 512:1024], lh, wqk[:, c * 1024 + 512:(c + 1) * 1024],
                                           start=st, stop=sp),
                                  deps=[g["psqk2"]] if c == 0 else (), inc=(c == ND - 1))
                        ms_k = self.tmark()
                        eq = ph.op("vector", lambda e: e.tensor_copy(qtm[:, 0:512], psQK[:, 0:512]),
                                   deps=[ms_q, g["q3"]], inc=True)
                        ek = ph.op("vector", lambda e: e.tensor_copy(ktm[:, 0:512], psQK[:, 512:1024]),
                                   deps=[ms_k], inc=True)
                        g["psqk"] = eq
                        g["psqk2"] = ek
                        ms_tr_prev = None
                        for which, buf, scb, x1o, x2o in (("q", qtm, qsb, 0, 256), ("k", ktm, ksb, 512, 768)):
                            dep_in = eq if which == "q" else ek
                            msq_ = ph.op("scalar", lambda e, b=buf: e.activation(sqb[:, 0:512], b[:, 0:512], AF.Square),
                                         deps=[dep_in, g["sqb"]], inc=True)
                            mss = ph.op("vector", lambda e: e.reduce_sum(
                                ssf[:, 0:8], sqb.ap(0, [[64, 8], [1, 64]]), axis=AX.X), deps=[msq_], inc=True)
                            g["sqb"] = mss
                            ms1 = ph.op("scalar", lambda e: e.activation(ssf[:, 8:16], ssf[:, 0:8], AF.Sqrt),
                                        deps=[mss], inc=True)
                            ms2 = ph.op("vector", lambda e: e.tensor_scalar_max(ssf[:, 8:16], ssf[:, 8:16], 1e-12),
                                        deps=[ms1], inc=True)
                            ms3 = ph.op("vector", lambda e: e.reciprocal(ssf[:, 8:16], ssf[:, 8:16]),
                                        deps=[ms2], inc=True)
                            m4 = ph.op("vector", lambda e, b=buf: e.tensor_tensor(
                                t1b.ap(0, [[64, 8], [1, 64]]), b.ap(0, [[64, 8], [1, 64]]),
                                ssf.ap(8, [[1, 8], [0, 64]]), ALU.mult), deps=[ms3, dep_in], inc=True)
                            m5 = ph.op("vector", lambda e, sb=scb: e.tensor_tensor(
                                t1b.ap(0, [[64, 8], [1, 64]]), t1b.ap(0, [[64, 8], [1, 64]]),
                                sb.ap(0, [[0, 8], [1, 64]]), ALU.mult), deps=[m4, ms_scales], inc=True)
                            m6 = ph.op("gpsimd", lambda e, o=x1o, ii=ii: e.tensor_tensor(
                                q3b.ap(0, [[64, 8], [1, 64]]), t1b.ap(0, [[64, 8], [1, 64]]),
                                xpt.ap(o + ii * 64, [[0, 8], [1, 64]]), ALU.mult),
                                deps=[m5, ms_xpt, g["q3"]], inc=True)
                            m7 = ph.op("vector", lambda e, o=x2o, ii=ii: e.tensor_tensor(
                                sqb.ap(0, [[64, 8], [1, 64]]),
                                t1b.ap(32, [[64, 8], [-32, 2], [1, 32]]),
                                xpt.ap(o + ii * 64, [[0, 8], [0, 2], [1, 32]]), ALU.mult),
                                deps=[m5, ms_xpt, mss], inc=True)
                            m8 = ph.op("vector", lambda e: e.tensor_add(q3b[:, 0:512], q3b[:, 0:512], sqb[:, 0:512]),
                                       deps=[m6, m7], inc=True)
                            g["sqb"] = m8
                            g["q3"] = m8
                            dstv = Qfm[sl] if which == "q" else Kfm[sl]
                            tms = None
                            for cl in range(4):
                                tms = ph.op("tensor", lambda e, cl=cl: e.transpose(
                                    gen_b[:, cl * 128:(cl + 1) * 128], q3b[:, cl * 128:(cl + 1) * 128],
                                    self.identb[:]),
                                    deps=([m8, g["gen"], ms_tr_prev]) if cl == 0 else (), inc=(cl == 3))
                            evs = None
                            for cl in range(4):
                                evs = ph.op("vector", lambda e, cl=cl, ii=ii, d=dstv: e.tensor_copy(
                                    d[:, cl * 512 + ii * 128: cl * 512 + ii * 128 + 128],
                                    gen_b[:, cl * 128:(cl + 1) * 128]), deps=[tms], inc=True)
                            g["gen"] = evs
                            ms_tr_prev = tms
                            if which == "k":
                                kevac = evs
                    ph.op("gpsimd", lambda e, l=l: e.dma_start(
                        wqk[:, 0:4096].rearrange("p (c f) -> p c f", c=8),
                        wqkv_d[l, :, 1024:1536].rearrange("(c p) f -> p c f", p=128)),
                        deps=[self.tmark()], inc=True, inc_sem=wsem[3], inc_n=16)
                    ms_vw = (wsem[3], wsem[3].n)
                    for ii in range(4):
                        i = grp * 4 + ii
                        deps0 = [ms_vw, g["psqk"]]
                        for c in range(ND):
                            lhs = self.xfmb_chunk(c, shifted=(c >= 4))[:, i * 128:(i + 1) * 128]
                            ph.op("tensor", lambda e, lh=lhs, c=c, st=(c == 0), sp=(c == ND - 1):
                                  e.matmul(psQK[:, 0:512], lh, wqk[:, c * 512:(c + 1) * 512], start=st, stop=sp),
                                  deps=deps0 if c == 0 else (), inc=(c == ND - 1))
                        ms_mm = self.tmark()
                        mv = ph.op("vector", lambda e, ii=ii, sl=sl: e.tensor_copy(
                            Vtm[sl][:, ii * 512:(ii + 1) * 512], psQK[:, 0:512]), deps=[ms_mm], inc=True)
                        g["psqk"] = mv
                        ms_v_l = mv
                    g["wqk"] = self.tmark()
                    return kevac, ms_v_l

                ph = self.phase(f"f{l}_qkv0")
                kv_ms = [None] * NW
                kv_ms[0] = qkv_build(ph, 0)
                for grp in range(NW):
                    ph = self.phase(f"f{l}_w{grp}")
                    sl, slp = grp % 2, (grp - 1) % 2
                    for hl in range(8):
                        qb = 64 * (hl % 2)
                        kch = hl // 2
                        ms_oacc = None
                        for qq in range(4):
                            pieces = []
                            if grp > 0:
                                pieces.append((slp, qq, 4 - qq))
                            pieces.append((sl, 0, qq + 1))
                            col = 0
                            deps_mm = [kv_ms[grp][0], kv_ms[grp][1], g["psS"]]
                            if grp > 0:
                                deps_mm += [kv_ms[grp - 1][0], kv_ms[grp - 1][1]]
                            for (pslot, t0, ntl) in pieces:
                                lhs = Qfm[sl].ap(kch * 512 + qq * 128, [[1, 128]], pbase=qb, pcount=64)
                                rem = ntl * 128
                                roff = 0
                                while rem > 0:
                                    seg = min(rem, 512 - (col % 512))
                                    rhs = Kfm[pslot].ap(kch * 512 + t0 * 128 + roff,
                                                        [[1, seg]], pbase=qb, pcount=64)
                                    ph.op("tensor", lambda e, lh=lhs, r=rhs, c0=col, n=seg, tp=(qb, 0):
                                          e.matmul(psS[:, c0:c0 + n], lh, r, start=True, stop=True, tile_position=tp),
                                          deps=deps_mm if col == 0 else (), inc=True)
                                    col += seg
                                    roff += seg
                                    rem -= seg
                            ms_smm = self.tmark()
                            kc = col
                            ms_mk = ms_smm
                            if grp > 0:
                                ms_mk = ph.op("vector", lambda e: e.tensor_add(
                                    psS[:, 0:128], psS[:, 0:128], self.masklo[:]),
                                    deps=[ms_smm, self.ms_masklo], inc=True)
                            ms_mk = ph.op("vector", lambda e, kc=kc: e.tensor_add(
                                psS[:, kc - 128:kc], psS[:, kc - 128:kc], self.maskup[:]),
                                deps=[ms_mk, self.ms_maskup], inc=True)
                            ms_mx = ph.op("vector", lambda e, kc=kc: e.reduce_max(
                                nmx[:, 0:1], psS[:, 0:kc], axis=AX.X), deps=[ms_mk], inc=True)
                            ms_nm = ph.op("vector", lambda e: e.tensor_scalar_mul(nmx[:, 1:2], nmx[:, 0:1], -8.0),
                                          deps=[ms_mx], inc=True)
                            ms_ex = ph.op("scalar", lambda e, kc=kc: e.activation(
                                Pb[:, 0:kc], psS[:, 0:kc], AF.Exp, bias=nmx[:, 1:2],
                                scale=self.fc(8.0), accum_out=nmx[:, 2:3]),
                                deps=[ms_nm, g["Pb"]], inc=True)
                            g["psS"] = ms_ex
                            ms_rc = ph.op("vector", lambda e: e.reciprocal(nmx[:, 3:4], nmx[:, 2:3]),
                                          deps=[ms_ex], inc=True)
                            ms_pn = ph.op("vector", lambda e, kc=kc: e.tensor_scalar(
                                Pb[:, 0:kc], Pb[:, 0:kc], nmx[:, 3:4], None, ALU.mult),
                                deps=[ms_rc], inc=True)
                            nkt = kc // 128
                            tms = None
                            for kt in range(nkt):
                                tms = ph.op("tensor", lambda e, kt=kt: e.transpose(
                                    gen_b[:, kt * 128:(kt + 1) * 128], Pb[:, kt * 128:(kt + 1) * 128],
                                    self.identb[:]),
                                    deps=([ms_pn, g["gen"]]) if kt == 0 else (), inc=(kt == nkt - 1))
                            ev = ph.op("vector", lambda e, kc=kc: e.tensor_copy(PTb[:, 0:kc], gen_b[:, 0:kc]),
                                       deps=[tms, g["PT"]], inc=True)
                            g["gen"] = ev
                            g["Pb"] = ev
                            ms_av = None
                            for kt in range(nkt):
                                gkt = (4 * (grp - 1) + qq + kt) if grp > 0 else kt
                                vslot = (gkt // 4) % 2
                                vloc = gkt % 4
                                ms_av = ph.op("tensor", lambda e, kt=kt, vs=vslot, vl=vloc, hl=hl, qb=qb, qq=qq,
                                              st=(kt == 0), sp=(kt == nkt - 1):
                                              e.matmul(psO[qb:qb + 64, qq * 128:(qq + 1) * 128],
                                                       Vtm[vs][:, vl * 512 + hl * 64: vl * 512 + hl * 64 + 64],
                                                       PTb[:, kt * 128:(kt + 1) * 128], start=st, stop=sp),
                                              deps=[ev, g["psO"] if (kt == 0 and qq == 0) else None],
                                              inc=True)
                            ms_oacc = ms_av
                            g["PT"] = ms_av
                        ey = ph.op("vector", lambda e, qb=qb, hl=hl: e.tensor_copy(
                            yfm[qb:qb + 64, (hl // 2) * 512:(hl // 2) * 512 + 512], psO[qb:qb + 64, :]),
                            deps=[ms_oacc], inc=True)
                        g["psO"] = ey
                    yv = (kb.sems["vector"], kb.sems["vector"].n)
                    for half in range(2):
                        ph.op("gpsimd", lambda e, l=l, h=half: e.dma_start(
                            wos[:, 0:2048].rearrange("p (c f) -> p c f", c=4),
                            wo_d[l, :, h * 512:(h + 1) * 512].rearrange("(c p) f -> p c f", p=128)),
                            deps=[self.tmark()], inc=True, inc_sem=wsem[2], inc_n=16)
                        ms_wo = (wsem[2], wsem[2].n)
                        for fc in range(4):
                            f = half * 4 + fc
                            deps0 = [yv, ms_wo, g["gen"]]
                            for ic in range(4):
                                ph.op("tensor", lambda e, ic=ic, fc=fc, st=(ic == 0), sp=(ic == 3):
                                      e.matmul(psG[:], wos[:, ic * 512 + fc * 128: ic * 512 + fc * 128 + 128],
                                               yfm[:, ic * 512:(ic + 1) * 512], start=st, stop=sp),
                                      deps=deps0 if ic == 0 else (), inc=(ic == 3))
                            ms_mm = self.tmark()
                            if f >= 4:
                                dst = self.Xfmb.ap(f * (T + 1) + 1 + grp * 512, [[1, 511]])
                                ms_r = ph.op("vector", lambda e, d=dst: e.tensor_tensor(d, d, psG[:, 0:511], ALU.add),
                                             deps=[ms_mm, g["stream"]], inc=True)
                                ms_r = ph.op("vector", lambda e, f=f: e.tensor_copy(
                                    bsave[:, f - 4:f - 3], psG[:, 511:512]), deps=[ms_r], inc=True)
                            else:
                                dst = self.Xfmb.ap(f * (T + 1) + 1 + grp * 512, [[1, 512]])
                                ms_r = ph.op("vector", lambda e, d=dst: e.tensor_tensor(d, d, psG[:], ALU.add),
                                             deps=[ms_mm, g["stream"]], inc=True)
                            g["gen"] = ms_r
                            g["stream"] = ms_r
                    if grp + 1 < NW:
                        kv_ms[grp + 1] = qkv_build(ph, grp + 1)
                    ms_pe_done = self.tmark()
                    ms_ap = None
                    for f in range(4, 8):
                        dstb = self.Xfmb.ap(f * (T + 1) + 1 + grp * 512 + 511, [[1, 1]])
                        ms_ap = ph.op("vector", lambda e, d=dstb, f=f: e.tensor_add(d, d, bsave[:, f - 4:f - 3]),
                                      deps=([ms_pe_done, g["stream"]] if f == 4 else ()), inc=(f == 7))
                    g["stream"] = ms_ap

                def ln_build(ph, grp, sl, xg):
                    for c in range(ND):
                        src = self.xfmb_chunk(c, shifted=(c >= 4))[:, grp * 512:(grp + 1) * 512]
                        ph.op("tensor", lambda e, sr=src, st=(c == 0), sp=(c == ND - 1):
                              e.matmul(psG[0:1, :], self.onesb[:], sr, start=st, stop=sp),
                              deps=([g["stream"], self.ms_ones, g["gen"]]) if c == 0 else (),
                              inc=(c == ND - 1))
                    ms_s1 = self.tmark()
                    r1 = ph.op("vector", lambda e: e.tensor_scalar_mul(lrow[0:1, 0:512], psG[0:1, :], 1.0 / 1024),
                               deps=[ms_s1], inc=True)
                    for c in range(ND):
                        src = self.xfmb_chunk(c, shifted=(c >= 4))[:, grp * 512:(grp + 1) * 512]
                        mq = ph.op("scalar", lambda e, sr=src: e.activation(
                            sqb[:, 0:512], sr, AF.Square), deps=[g["stream"], g["sqb"]], inc=True)
                        mm2 = ph.op("tensor", lambda e, st=(c == 0), sp=(c == ND - 1):
                                    e.matmul(psG[0:1, :], self.onesb[:], sqb[:, 0:512], start=st, stop=sp),
                                    deps=[mq] + ([r1] if c == 0 else []), inc=True)
                        g["sqb"] = mm2
                    ms_s2 = self.tmark()
                    rA = ph.op("vector", lambda e: e.tensor_mul(vrow[0:1, 0:512], lrow[0:1, 0:512], lrow[0:1, 0:512]),
                               deps=[r1], inc=True)
                    rB = ph.op("vector", lambda e: e.scalar_tensor_tensor(
                        vrow[0:1, 0:512], psG[0:1, :], 1.0 / 1024, vrow[0:1, 0:512], ALU.mult, ALU.subtract),
                        deps=[rA, ms_s2], inc=True)
                    rC = ph.op("scalar", lambda e: e.activation(vrow[0:1, 0:512], vrow[0:1, 0:512], AF.Sqrt,
                                                                bias=self.fc(1e-5, 1)), deps=[rB], inc=True)
                    rD = ph.op("vector", lambda e: e.reciprocal(vrow[0:1, 0:512], vrow[0:1, 0:512]),
                               deps=[rC], inc=True)
                    rE = ph.op("vector", lambda e: e.tensor_copy(browf[0:1, 0:512], lrow[0:1, 0:512]),
                               deps=[r1, rA], inc=True)
                    rF = ph.op("tensor", lambda e: e.matmul(psG[:], self.onesrow[0:1, :], browf[0:1, 0:512],
                                                            start=True, stop=True),
                               deps=[rE, rB, g["gen"]], inc=True)
                    rG = ph.op("vector", lambda e: e.tensor_copy(mb_[:, 0:512], psG[:]), deps=[rF], inc=True)
                    rH = ph.op("vector", lambda e: e.tensor_copy(browf[0:1, 0:512], vrow[0:1, 0:512]),
                               deps=[rD, rF], inc=True)
                    rI = ph.op("tensor", lambda e: e.matmul(psG[:], self.onesrow[0:1, :], browf[0:1, 0:512],
                                                            start=True, stop=True),
                               deps=[rH, rG], inc=True)
                    rJ = ph.op("vector", lambda e: e.tensor_copy(rb_[:, 0:512], psG[:]), deps=[rI], inc=True)
                    g["gen"] = rJ
                    last = None
                    for c in range(ND):
                        src = self.xfmb_chunk(c, shifted=(c >= 4))[:, grp * 512:(grp + 1) * 512]
                        a1 = ph.op("vector", lambda e, sr=src: e.tensor_tensor(
                            t1b[:, 0:512], sr, mb_[:, 0:512], ALU.subtract),
                            deps=[rG, g["t1"]] + ([xg] if c == 0 else []), inc=True)
                        a2 = ph.op("vector", lambda e: e.tensor_mul(t1b[:, 0:512], t1b[:, 0:512], rb_[:, 0:512]),
                                   deps=[a1, rJ], inc=True)
                        a3 = ph.op("vector", lambda e, c=c, sl=sl: e.tensor_scalar(
                            xln[sl][:, c * 512:(c + 1) * 512], t1b[:, 0:512],
                            lgc[:, c:c + 1], lbc[:, c:c + 1], ALU.mult, ALU.add),
                            deps=[a2], inc=True)
                        g["t1"] = a3
                        last = a3
                    return last

                NIH = NI // 2
                qkb = psQK[:].bitcast(BF16)
                psb = psS[:].bitcast(BF16)
                pob = psO[:].bitcast(BF16)
                regs = [psQK[:, 0:512], psQK[:, 512:1024], psS[:, 0:512], psO[:, 0:512]]
                regs_b = [qkb[:, 0:512], qkb[:, 1024:1536], psb[:, 0:512], pob[:, 0:512]]
                g.setdefault("bk", [None, None, None, None])
                g.setdefault("w2r", None)
                g["w1"] = [None, None, None, None]
                w2sem = wsem[2]
                ln_done = {}
                ms_h2 = None

                ph = self.phase(f"f{l}_lnA")
                ln_done[0] = ln_build(ph, 0, 0, None)
                if NW > 1:
                    ln_done[1] = ln_build(ph, 1, 1, None)
                NM = (NW + 1) // 2
                for m in range(NM):
                    units = [(gg_, s_) for gg_ in range(2 * m, min(2 * m + 2, NW)) for s_ in (0, 1)]
                    U = len(units)
                    ph = self.phase(f"f{l}_f{m}")
                    if DBG_LN_SEQ and m == 1:
                        if NW > 2:
                            ln_done[2] = ln_build(ph, 2, 0, None)
                        if NW > 3:
                            ln_done[3] = ln_build(ph, 3, 1, None)
                        ms_apb = None
                        for f_ in range(4, 8):
                            dstb = self.Xfmb.ap(f_ * (T + 1) + 1 + 1023, [[1, 1]])
                            ms_apb = ph.op("vector", lambda e, d=dstb, f_=f_: e.tensor_add(
                                d, d, bsave2[:, f_ - 4:f_ - 3]),
                                deps=([ln_done[2], ln_done.get(3), g["stream"]] if f_ == 4 else ()),
                                inc=(f_ == 7))
                        g["stream"] = ms_apb
                    for h2 in range(2):
                        msw = [None, None, None, None]
                        for jl in range(NIH):
                            j = h2 * NIH + jl
                            for ab in range(2):
                                sw = (j % 2) * 2 + ab
                                off = (HID if ab else 0) + j * 128
                                ph.op("gpsimd", lambda e, l=l, o=off, sw=sw: e.dma_start(
                                    w1s[sw][:, 0:1024].rearrange("p (c f) -> p c f", c=8),
                                    w1_d[l, :, o:o + 128].rearrange("(c p) f -> p c f", p=128)),
                                    deps=[g["w1"][sw]], inc=True, inc_sem=wsem[sw], inc_n=16)
                                msw[sw] = (wsem[sw], wsem[sw].n)
                            if jl == 1 and h2 == 0:
                                pass
                            swa, swg = (j % 2) * 2, (j % 2) * 2 + 1
                            for u in range(U):
                                gg_, s_ = units[u]
                                par = u % 2
                                rG, rA = 1 + 2 * par, 0 + 2 * par
                                xsl, t0loc = gg_ - 2 * m, s_ * 256
                                deps0 = [msw[swg], ln_done.get(gg_), g["bk"][rG]]
                                for c in range(ND):
                                    ph.op("tensor", lambda e, c=c, sw=swg, rr=regs[rG], xsl=xsl, t0=t0loc,
                                          st=(c == 0), sp=(c == ND - 1):
                                          e.matmul(rr[:, 0:256] if False else bass.AP(rr.tensor, rr.offset, [rr.ap[0], [1, 256]]),
                                                   w1s[sw][:, c * 128:(c + 1) * 128],
                                                   xln[xsl][:, c * 512 + t0:c * 512 + t0 + 256],
                                                   start=st, stop=sp),
                                          deps=deps0 if c == 0 else (), inc=(c == ND - 1))
                                ms_g = self.tmark()
                                deps0 = [msw[swa], g["bk"][rA]]
                                for c in range(ND):
                                    ph.op("tensor", lambda e, c=c, sw=swa, rr=regs[rA], xsl=xsl, t0=t0loc,
                                          st=(c == 0), sp=(c == ND - 1):
                                          e.matmul(bass.AP(rr.tensor, rr.offset, [rr.ap[0], [1, 256]]),
                                                   w1s[sw][:, c * 128:(c + 1) * 128],
                                                   xln[xsl][:, c * 512 + t0:c * 512 + t0 + 256],
                                                   start=st, stop=sp),
                                          deps=deps0 if c == 0 else (), inc=(c == ND - 1))
                                ms_a = self.tmark()
                                mgg = ph.op("scalar", lambda e, j=j, rr=regs[rG], par=par: e.activation(
                                    ggb[:, par * 256:par * 256 + 256],
                                    bass.AP(rr.tensor, rr.offset, [rr.ap[0], [1, 256]]),
                                    AF.Gelu, bias=b1c[:, 22 + j:23 + j]),
                                    deps=[ms_g, g["gg%d" % par]], inc=True)
                                g["bk"][rG] = mgg
                                mh4 = ph.op("vector", lambda e, j=j, rr=regs[rA], par=par, jl=jl, u=u, U=U:
                                            e.scalar_tensor_tensor(
                                                h4a[:, (jl * U + u) * 256:(jl * U + u) * 256 + 256],
                                                bass.AP(rr.tensor, rr.offset, [rr.ap[0], [1, 256]]),
                                                b1c[:, j:j + 1], ggb[:, par * 256:par * 256 + 256],
                                                ALU.add, ALU.mult),
                                            deps=[ms_a, mgg, g["h4"], g["w2r"]], inc=True)
                                g["bk"][rA] = mh4
                                g["gg%d" % par] = mh4
                                g["h4"] = mh4
                            g["w1"][swa] = self.tmark()
                            g["w1"][swg] = g["w1"][swa]
                        ms_up_done = self.tmark()
                        for icl in range(NIH):
                            ph.op("gpsimd", lambda e, l=l, icl=icl, h2=h2: e.dma_start(
                                w2r[:, icl * 1024:(icl + 1) * 1024],
                                w2_d[l, h2 * NIH * 128 + icl * 128: h2 * NIH * 128 + (icl + 1) * 128, :]),
                                deps=[g["w2r"]] if icl == 0 else (), inc=True,
                                inc_sem=kb.sems["dgpsimd"], inc_n=16)
                        ms_w2r = ph.qtotal("gpsimd")
                        if m == 0 and h2 == 1 and not DBG_LN_SEQ:
                            if NW > 2:
                                ln_done[2] = ln_build(ph, 2, 0, ms_up_done)
                            if NW > 3:
                                ln_done[3] = ln_build(ph, 3, 1, ms_up_done)
                        for u in range(U):
                            gg_, s_ = units[u]
                            tb = gg_ * 512 + s_ * 256
                            for icl in range(NIH):
                                for t2 in range(2):
                                    for fh in range(2):
                                        bi = t2 * 2 + fh
                                        ph.op("tensor", lambda e, icl=icl, t2=t2, fh=fh, u=u, U=U, b=regs[t2 * 2 + fh],
                                              st=(icl == 0), sp=(icl == NIH - 1):
                                              e.matmul(b, h4a[:, (icl * U + u) * 256 + t2 * 128:
                                                              (icl * U + u) * 256 + t2 * 128 + 128],
                                                       w2r[:, icl * 1024 + fh * 512:icl * 1024 + fh * 512 + 512],
                                                       start=st, stop=sp),
                                              deps=([ms_w2r, g["h4"], g["bk"][bi]] if icl == 0 else ()),
                                              inc=(icl == NIH - 1))
                            ms_dn = self.tmark()
                            for t2 in range(2):
                                for fh in range(2):
                                    bi = t2 * 2 + fh
                                    ec = ph.op("vector", lambda e, b=regs[bi]: e.tensor_copy(dtmp[:, 0:512], b),
                                               deps=[ms_dn, g["dtmp"]], inc=True)
                                    tms = None
                                    for fc in range(4):
                                        tms = ph.op("tensor", lambda e, fc=fc, bb=regs_b[bi]: e.transpose(
                                            bb[:, fc * 128:(fc + 1) * 128], dtmp[:, fc * 128:(fc + 1) * 128],
                                            self.identb[:]), deps=([ec] if fc == 0 else ()),
                                            inc=(fc == 3))
                                    g["dtmp"] = tms
                                    last2 = None
                                    for fc in range(4):
                                        f = fh * 4 + fc
                                        defer = (NW > 2 and m == 0 and gg_ == 1 and s_ == 1
                                                 and t2 == 1 and f >= 4)
                                        w = 127 if defer else 128
                                        dst = self.Xfmb.ap(f * (T + 1) + 1 + tb + t2 * 128, [[1, w]])
                                        bb = regs_b[bi]
                                        if h2 == 0:
                                            last2 = ph.op("vector", lambda e, d=dst, f=f, fc=fc, bb=bb, w=w:
                                                          e.scalar_tensor_tensor(
                                                              d, bass.AP(bb.tensor, bb.offset + fc * 128, [bb.ap[0], [1, w]]),
                                                              b2c[:, f:f + 1], d, ALU.add, ALU.add),
                                                          deps=[tms, g["stream"]], inc=True)
                                        else:
                                            last2 = ph.op("vector", lambda e, d=dst, fc=fc, bb=bb, w=w:
                                                          e.tensor_add(d, d,
                                                                       bass.AP(bb.tensor, bb.offset + fc * 128, [bb.ap[0], [1, w]])),
                                                          deps=[tms, g["stream"]], inc=True)
                                        if defer:
                                            if h2 == 0:
                                                last2 = ph.op("vector", lambda e, f=f, fc=fc, bb=bb: e.tensor_scalar(
                                                    bsave2[:, f - 4:f - 3],
                                                    bass.AP(bb.tensor, bb.offset + fc * 128 + 127, [bb.ap[0], [1, 1]]),
                                                    b2c[:, f:f + 1], None, ALU.add),
                                                    deps=[last2], inc=True)
                                            else:
                                                last2 = ph.op("vector", lambda e, f=f, fc=fc, bb=bb: e.tensor_add(
                                                    bsave2[:, f - 4:f - 3], bsave2[:, f - 4:f - 3],
                                                    bass.AP(bb.tensor, bb.offset + fc * 128 + 127, [bb.ap[0], [1, 1]])),
                                                    deps=[last2], inc=True)
                                    g["bk"][bi] = last2
                                    g["stream"] = last2
                        g["w2r"] = self.tmark()
                    if NW > 2 and m == 0 and not DBG_LN_SEQ:
                        ms_apb = None
                        for f in range(4, 8):
                            dstb = self.Xfmb.ap(f * (T + 1) + 1 + 1023, [[1, 1]])
                            ms_apb = ph.op("vector", lambda e, d=dstb, f=f: e.tensor_add(
                                d, d, bsave2[:, f - 4:f - 3]),
                                deps=([ln_done[2], ln_done.get(3), g["stream"]] if f == 4 else ()),
                                inc=(f == 7))
                        g["stream"] = ms_apb
            ph = self.phase("f_final")
            x_ms2 = [None] * NT
            for i in range(NT):
                for half in range(2):
                    tms = None
                    for cl in range(4):
                        c = half * 4 + cl
                        src = self.Xfmb.ap(c * (T + 1) + 1 + i * 128, [[1, 128]])
                        tms = ph.op("tensor", lambda e, sr=src, cl=cl: e.transpose(
                            gen_b[:, cl * 128:(cl + 1) * 128], sr, self.identb[:]),
                            deps=([g["stream"], g["gen"]]) if cl == 0 else (), inc=(cl == 3))
                    a = ph.op("vector", lambda e, i=i, h=half: e.tensor_add(
                        self.Xtm[:, i * 1024 + h * 512: i * 1024 + h * 512 + 512],
                        self.Xtm[:, i * 1024 + h * 512: i * 1024 + h * 512 + 512],
                        gen_b[:, 0:512]), deps=[tms, x_ms[i], x_ms2[i]], inc=True)
                    g["gen"] = a
                    x_ms2[i] = a
            self.ms_x_raw = x_ms2

    def p_ln(self, tag, eps=1e-5):
        NT = self.NT
        ph = self.phase("ln_" + tag)
        ms_stats = []
        prev = None
        for i in range(NT):
            dep = [self.ms_x_raw[i] if isinstance(self.ms_x_raw, list) else self.ms_x_raw, prev]
            mcol = self.stat_m[:, i:i + 1]
            scol = self.stat_s[:, i:i + 1]
            m1 = ph.op("scalar", lambda e, i=i, m=mcol: e.activation(
                self.xtmb_scr[:], self.xtm_tile(i), AF.Identity, accum_out=m), deps=dep, inc=True)
            m2 = ph.op("scalar", lambda e, i=i, s=scol: e.activation(
                self.xtmb_scr[:], self.xtm_tile(i), AF.Square, accum_out=s), deps=dep + [m1], inc=True)
            prev = m2
            ms_stats.append(m2)
            ms_stats.append(m1)
        self._scr_guard = prev
        mm = ph.op("vector", lambda e: e.tensor_scalar_mul(
            self.stat_m[:, 0:NT], self.stat_m[:, 0:NT], 1.0 / 1024), deps=ms_stats, inc=True)
        m2 = ph.op("vector", lambda e: e.tensor_mul(
            self.stat_r[:, 0:NT], self.stat_m[:, 0:NT], self.stat_m[:, 0:NT]), deps=[mm], inc=True)
        m3 = ph.op("vector", lambda e: e.scalar_tensor_tensor(
            self.stat_s[:, 0:NT], self.stat_s[:, 0:NT], 1.0 / 1024, self.stat_r[:, 0:NT],
            ALU.mult, ALU.subtract), deps=[m2], inc=True)
        m4 = ph.op("scalar", lambda e, eps=eps: e.activation(
            self.stat_s[:, 0:NT], self.stat_s[:, 0:NT], AF.Sqrt, bias=self.fc(eps)), deps=[m3], inc=True)
        m5 = ph.op("vector", lambda e: e.reciprocal(self.stat_r[:, 0:NT], self.stat_s[:, 0:NT]),
                   deps=[m4], inc=True)
        ms_app = []
        for i in range(NT):
            ms = ph.op("vector", lambda e, i=i: e.tensor_scalar(
                self.xtm_tile(i), self.xtm_tile(i), self.stat_m[:, i:i + 1], self.stat_r[:, i:i + 1],
                ALU.subtract, ALU.mult), deps=[m5], inc=True)
            ms_app.append(ms)
        self.ms_xtm = ms_app
        with contextlib.ExitStack() as pes:
            trb = [pes.enter_context(self.nc.psum_tensor("trn0" + tag, [128, 128], BF16)),
                   pes.enter_context(self.nc.psum_tensor("trn1" + tag, [128, 128], BF16))]
            self.rebuild_xfmb(ph, ms_app, trb)

    def p_store(self):
        NT = self.NT
        out = self.dram_out("out", [self.T, 1024])
        ph = self.phase("store")
        last = []
        for i in range(NT):
            dep = [self.ms_xtm[i] if isinstance(self.ms_xtm, list) else self.ms_xtm]
            last.append(ph.dma("sync", out[i * 128:(i + 1) * 128, :], self.xtm_tile(i), deps=dep))
        ph.wait("sync", last)

    STAGE_ORDER = ("load", "l0_attn", "ln0a", "l0_ffn", "ln0f", "l1_attn", "ln1a",
                   "l1_ffn", "ln1f")

    def build(self):
        u = self.upto
        if u == "all":
            level = 99
        elif u.startswith("l0a_"):
            level = 1
        elif u == "l0_attn":
            level = 2
        elif u == "l0_ffn":
            level = 4
        elif u.startswith("l1s_") or u == "l1a_raw":
            level = 5
        elif u == "l1_attn":
            level = 6
        elif u == "l1_ffn":
            level = 8
        elif u == "load":
            level = 0
        else:
            raise ValueError(u)
        self.alloc_stream()
        self.p_load()
        self.ms_x_raw = self.ms_xtm
        if level >= 1:
            self.p_l0_attn()
        if level >= 2:
            self.p_ln("0a")
        if level >= 3:
            self._sig_guard = None
            self.p_l0_ffn()
        if level >= 4:
            self.p_ln("0f")
        if level >= 5:
            self.p_l1_attn()
        if level >= 6:
            self.p_ln("1a")
        if level >= 7:
            self.p_l1_ffn()
        if level >= 8:
            self.p_ln("1f")
        if level in (1, 5, 7):
            self.ms_xtm = self.ms_x_raw
        self.p_store()
        return self.finish()


def _make_host_arrays(inputs, T):
    consts = host_consts(T)
    w = host_weights(inputs)
    shared = {}
    for k, v in consts.items():
        shared[k] = np.ascontiguousarray(v.astype(np.float32))
    for k, v in w.items():
        shared[k] = np.ascontiguousarray(v)
    return shared


def kernel(**inputs):
    T = 2048
    shared = _make_host_arrays(inputs, T)
    x = np.ascontiguousarray(np.asarray(inputs["L0_attn_x"], dtype=np.float32))
    b = Builder(T, "all")
    nc = b.build()
    in_maps = []
    for ci in range(8):
        m = dict(shared)
        m["x"] = np.ascontiguousarray(x[ci])
        in_maps.append(m)
    res = run_bass_kernel_spmd(nc, in_maps, core_ids=list(range(8)))
    return np.stack([r["out"] for r in res.results], axis=0).astype(np.float32)


# version 11 (final)
# speedup vs baseline: 1.1929x; 1.1929x over previous
import contextlib
import math

import numpy as np
import ml_dtypes

import concourse.bass as bass
import concourse.mybir as mybir
from concourse.bass_utils import run_bass_kernel_spmd

F32 = mybir.dt.float32
BF16 = mybir.dt.bfloat16
AF = mybir.ActivationFunctionType
ALU = mybir.AluOpType
AX = mybir.AxisListType
BF16NP = ml_dtypes.bfloat16

D = 1024
ND = 8
NEG = -1.0e30
DBG_LN_SEQ = False

ENGS = ("tensor", "vector", "scalar", "gpsimd", "sync")


def _copy(e, d, s):
    return e.copy(d, s) if hasattr(e, "copy") and not hasattr(e, "tensor_copy") else e.tensor_copy(d, s)


class Sem:
    __slots__ = ("h", "n")

    def __init__(self, h):
        self.h = h
        self.n = 0


class KB:
    def __init__(self, nc, es):
        self.nc = nc
        self.es = es
        self.sems = {}
        for e in ENGS:
            self.sems[e] = Sem(es.enter_context(nc.semaphore("sem_" + e)))
        for q in ("dsync", "dgpsimd", "dscalar"):
            self.sems[q] = Sem(es.enter_context(nc.semaphore(q)))


class Phase:

    def __init__(self, kb, name=""):
        self.kb = kb
        self.name = name
        self.ops = {e: [] for e in ENGS}
        self.waited = {e: {} for e in ENGS}

    def op(self, eng, build, deps=(), inc=False, inc_sem=None, inc_n=1):
        best = {}
        for ms in deps:
            if ms is None:
                continue
            sem, val = ms
            if best.get(id(sem), (None, -1))[1] < val:
                best[id(sem)] = (sem, val)
        todo = []
        w = self.waited[eng]
        for sem, val in best.values():
            if w.get(id(sem), -1) < val:
                w[id(sem)] = val
                todo.append((sem, val))
        out_ms = None
        if inc:
            sem = inc_sem if inc_sem is not None else self.kb.sems[eng]
            sem.n += inc_n
            out_ms = (sem, sem.n)

        def closure(e, todo=todo, build=build, out_ms=out_ms, inc_n=inc_n):
            for (s, v) in todo:
                e.wait_ge(s.h, v)
            r = build(e)
            if out_ms is not None:
                r.then_inc(out_ms[0].h, inc_n)
            return r

        self.ops[eng].append(closure)
        return out_ms

    def dma(self, queue, dst, src, deps=()):
        sem = self.kb.sems["d" + queue]
        return self.op(queue, lambda e, d=dst, s=src: e.dma_start(d, s),
                       deps=deps, inc=True, inc_sem=sem, inc_n=16)

    def qtotal(self, queue):
        sem = self.kb.sems["d" + queue]
        return (sem, sem.n)

    def wait(self, eng, deps):
        self.op(eng, lambda e: None, deps=deps)

    def realize(self):
        nc = self.kb.nc
        with contextlib.ExitStack() as _ns:
            if self.name:
                _ns.enter_context(nc.named_scope(self.name))
            with nc.Block(self.name or None) as blk:
                for e in ENGS:
                    if self.ops[e]:
                        def emit(obj, ops=self.ops[e]):
                            for f in ops:
                                f(obj)
                        getattr(blk, e)(emit)


def host_consts(T):
    c = {}
    c["identb"] = np.eye(128, dtype=np.float32)
    i = np.arange(128)[:, None]
    j = np.arange(128)[None, :]
    c["maskup"] = np.where(j > i, np.float32(NEG), np.float32(0.0)).astype(np.float32)
    c["masklo"] = np.where(j < i, np.float32(NEG), np.float32(0.0)).astype(np.float32)
    inv = (1.0 / 10000.0 ** (np.arange(0, 32, 2, dtype=np.float32) / np.float32(32.0))).astype(np.float32)
    t = np.arange(T, dtype=np.float32)
    fr = np.outer(t, inv).astype(np.float32)
    c["l0cos"] = np.cos(fr).astype(np.float32)
    c["l0sin"] = np.sin(fr).astype(np.float32)
    ids = np.arange(512, dtype=np.float32)
    theta = np.power(np.float32(10000.0), np.float32(-2.0) * ids) / np.float32(512.0)
    r = (t[:, None] * theta[None, :]).astype(np.float32)
    c["l1cos"] = np.cos(r).astype(np.float32)
    c["l1sin"] = np.sin(r).astype(np.float32)
    DH, WIN = 64, 512
    half_idx = np.arange(0, DH, 2, dtype=np.float32)
    inv_freq = (1.0 / 10000.0 ** (half_idx / np.float32(DH))).astype(np.float32)
    tt = np.arange(T, dtype=np.float32)
    fr2 = (tt[:, None] * inv_freq[None, :]).astype(np.float32)
    fr2 = np.concatenate([fr2, fr2], axis=-1)
    base = ((half_idx + 0.4 * DH) / (1.4 * DH)).astype(np.float32)
    power = ((tt - T // 2) / np.float32(WIN // 2)).astype(np.float32)
    sc = (base[None, :] ** power[:, None]).astype(np.float32)
    sc = np.concatenate([sc, sc], axis=-1)
    cosf, sinf = np.cos(fr2).astype(np.float32), np.sin(fr2).astype(np.float32)
    sgn = np.concatenate([-np.ones(32, np.float32), np.ones(32, np.float32)])
    c["xq1"] = (cosf * sc).astype(np.float32)
    c["xq2"] = (sinf * sc * sgn[None, :]).astype(np.float32)
    c["xk1"] = (cosf / sc).astype(np.float32)
    c["xk2"] = (sinf / sc * sgn[None, :]).astype(np.float32)
    return c


def host_weights(inp):
    w = {}
    ct = lambda a: np.ascontiguousarray(np.asarray(a, dtype=np.float32))
    w["wq_t"] = ct(np.asarray(inp["L0_attn_Wq"]).T)
    w["wkv_t"] = ct(np.asarray(inp["L0_attn_Wkv"]).T)
    _wo3 = np.asarray(inp["L0_attn_Wout"]).T
    w["wout3"] = ct(np.ascontiguousarray(_wo3.reshape(16, 64, 1024).transpose(1, 0, 2).reshape(64, 16384)))
    w["lam"] = ct(np.asarray(inp["L0_attn_lambda_init"]).reshape(1, 16))
    w["rmsw"] = ct(np.asarray(inp["L0_attn_rms_weight"]).reshape(64, 1))
    if "L0_ffn_expert_gate" not in inp:
        return w
    eg = np.asarray(inp["L0_ffn_expert_gate"], dtype=np.float32)
    eu = np.asarray(inp["L0_ffn_expert_up"], dtype=np.float32)
    ed = np.asarray(inp["L0_ffn_expert_down"], dtype=np.float32)
    sg = np.asarray(inp["L0_ffn_shared_gate"], dtype=np.float32).T
    su = np.asarray(inp["L0_ffn_shared_up"], dtype=np.float32).T
    sd = np.asarray(inp["L0_ffn_shared_down"], dtype=np.float32).T
    rows = []
    for e in range(32):
        rows.append(np.concatenate([eg[e].T.ravel(), eu[e].T.ravel(), ed[e].T.ravel()]))
    for h in range(2):
        rows.append(np.concatenate([sg[:, h * 512:(h + 1) * 512].ravel(),
                                    su[:, h * 512:(h + 1) * 512].ravel(),
                                    sd[h * 512:(h + 1) * 512, :].ravel()]))
    w["moe_w"] = ct(np.stack(rows))
    w["gw_t"] = ct(np.asarray(inp["L0_ffn_gate_w"], dtype=np.float32).T)
    w["gbias"] = ct(np.asarray(inp["L0_ffn_gate_bias"], dtype=np.float32).reshape(1, 32))
    if "L1_attn_Wd_q_w" not in inp:
        return w
    for k_src, k_dst in (("Wd_q_w", "l1_wdq"), ("Wd_kv_w", "l1_wdkv"), ("Wu_q_w", "l1_wuq"),
                         ("Wq_r_w", "l1_wqr"), ("Wk_r_w", "l1_wkr"), ("Wu_k_w", "l1_wuk"),
                         ("Wu_v_w", "l1_wuv"), ("Wo_w", "l1_wo")):
        w[k_dst] = ct(np.asarray(inp["L1_attn_" + k_src], dtype=np.float32))
    bb = np.zeros((8, 1024), dtype=np.float32)
    bb[0, 0:512] = np.asarray(inp["L1_attn_Wd_q_b"], dtype=np.float32)
    bb[1, 0:512] = np.asarray(inp["L1_attn_Wd_kv_b"], dtype=np.float32)
    bb[2] = np.asarray(inp["L1_attn_Wu_q_b"], dtype=np.float32)
    bb[3] = np.asarray(inp["L1_attn_Wq_r_b"], dtype=np.float32)
    bb[4] = np.asarray(inp["L1_attn_Wk_r_b"], dtype=np.float32)
    bb[5] = np.asarray(inp["L1_attn_Wu_k_b"], dtype=np.float32)
    bb[6] = np.asarray(inp["L1_attn_Wu_v_b"], dtype=np.float32)
    bb[7] = np.asarray(inp["L1_attn_Wo_b"], dtype=np.float32)
    w["l1ab"] = ct(bb)
    if "L1_ffn_wqkv" not in inp:
        return w
    HID = 2730
    HP = 2816
    w1 = np.asarray(inp["L1_ffn_w1"], dtype=np.float32)
    w1p = np.zeros((2, 1024, 2 * HP), dtype=np.float32)
    w1p[:, :, 0:HID] = w1[:, :, 0:HID]
    w1p[:, :, HP:HP + HID] = w1[:, :, HID:2 * HID]
    b1 = np.asarray(inp["L1_ffn_b1"], dtype=np.float32)
    b1p = np.zeros((2, 2 * HP), dtype=np.float32)
    b1p[:, 0:HID] = b1[:, 0:HID]
    b1p[:, HP:HP + HID] = b1[:, HID:2 * HID]
    w2 = np.asarray(inp["L1_ffn_w2"], dtype=np.float32)
    w2p = np.zeros((2, HP, 1024), dtype=np.float32)
    w2p[:, 0:HID, :] = w2
    w["f_w1"] = ct(w1p)
    w["f_b1"] = ct(b1p)
    w["f_w2"] = ct(w2p)
    w["f_wqkv"] = ct(np.asarray(inp["L1_ffn_wqkv"], dtype=np.float32))
    w["f_wo"] = ct(np.asarray(inp["L1_ffn_wo"], dtype=np.float32))
    w["f_b2"] = ct(np.asarray(inp["L1_ffn_b2"], dtype=np.float32))
    w["f_lng"] = ct(np.asarray(inp["L1_ffn_ln_g"], dtype=np.float32))
    w["f_lnb"] = ct(np.asarray(inp["L1_ffn_ln_b"], dtype=np.float32))
    w["f_qs"] = ct(np.asarray(inp["L1_ffn_q_scale"], dtype=np.float32))
    w["f_ks"] = ct(np.asarray(inp["L1_ffn_k_scale"], dtype=np.float32))
    return w


class AV:

    __slots__ = ("t", "pitch", "base", "w")

    def __init__(self, t, pitch, base, w):
        self.t, self.pitch, self.base, self.w = t, pitch, base, w

    def __getitem__(self, key):
        ps, cs = key
        p0 = ps.start or 0
        p1 = 128 if ps.stop is None else ps.stop
        c0 = cs.start or 0
        c1 = self.w if cs.stop is None else cs.stop
        return bass.AP(self.t, p0 * self.pitch + self.base + c0,
                       [[self.pitch, p1 - p0], [1, c1 - c0]])

    def ap(self, off, dims, pbase=0, pcount=128):
        return bass.AP(self.t, pbase * self.pitch + self.base + off,
                       [[self.pitch, pcount]] + [list(d) for d in dims])

    def sub(self, off, w):
        return AV(self.t, self.pitch, self.base + off, w)


class Builder:
    def __init__(self, T=2048, upto="all"):
        assert T % 512 == 0
        self.T = T
        self.NT = T // 128
        self.NQG = self.NT // 4
        self.upto = upto
        self.nc = bass.Bass("TRN2", target_bir_lowering=False)
        self.es = contextlib.ExitStack()
        self.kb = KB(self.nc, self.es)
        self.phases = []

    def dram_in(self, name, shape, dt=F32):
        return self.nc.dram_tensor(name, list(shape), dt, kind="ExternalInput").ap()

    def dram_out(self, name, shape, dt=F32):
        return self.nc.dram_tensor(name, list(shape), dt, kind="ExternalOutput").ap()

    def sb(self, name, shape, dt):
        return self.es.enter_context(self.nc.sbuf_tensor(name, list(shape), dt))

    def phase(self, name=""):
        p = Phase(self.kb, name)
        self.phases.append(p)
        return p

    def tmark(self):
        return (self.kb.sems["tensor"], self.kb.sems["tensor"].n)

    def finish(self):
        for p in self.phases:
            p.realize()
        self.es.close()
        return self.nc

    FCVALS = (1e-6, 1e-5, 0.1767766952966369, 0.015625, 1.0, 0.031008683647302115, 8.0)

    def fc(self, v, p=128):
        k = self._fcmap[float(v)]
        return self.fct[0:p, k:k + 1]

    ARENA_B = 66480
    ARENA_F = 2560

    def arena_reset(self, keep_xfmb=True):
        self._ab = self._xfmb_resv if keep_xfmb else 0
        self._af = 0

    def aB(self, w):
        w = (w + 15) & ~15
        v = AV(self.arena_b, self.ARENA_B, self._ab, w)
        self._ab += w
        assert self._ab <= self.ARENA_B, f"bf16 arena overflow: {self._ab}"
        return v

    def aF(self, w):
        w = (w + 7) & ~7
        v = AV(self.arena_f, self.ARENA_F, self._af, w)
        self._af += w
        assert self._af <= self.ARENA_F, f"f32 arena overflow: {self._af}"
        return v

    def alloc_stream(self):
        self.fct = self.sb("fct_sb", [128, len(self.FCVALS)], F32)
        self._fcmap = {float(v): i for i, v in enumerate(self.FCVALS)}
        self.arena_b = self.sb("arena_b_sb", [128, self.ARENA_B], BF16)
        self.arena_f = self.sb("arena_f_sb", [128, self.ARENA_F], F32)
        self._xfmb_w = ND * (self.T + 1)
        self._xfmb_resv = (self._xfmb_w + 15) & ~15
        self.Xfmb = AV(self.arena_b, self.ARENA_B, 0, self._xfmb_w)
        self.arena_reset()
        T, NT = self.T, self.NT
        self.Xtm = self.sb("Xtm_sb", [128, NT * 1024], F32)
        self.identb = self.sb("identb_sb", [128, 128], BF16)
        self.identf = self.sb("identf_sb", [128, 128], F32)
        self.maskup = self.sb("maskup_sb", [128, 128], BF16)
        self.masklo = self.sb("masklo_sb", [128, 128], BF16)
        self.onesb = self.sb("onesb_sb", [128, 1], BF16)
        self.onesrow = self.sb("onesrow_sb", [1, 128], BF16)
        self.stat_m = self.sb("stat_m_sb", [128, NT], F32)
        self.stat_s = self.sb("stat_s_sb", [128, NT], F32)
        self.stat_r = self.sb("stat_r_sb", [128, NT], F32)
        self.xtmb_scr = self.sb("xtmb_scr_sb", [128, 1024], BF16)

    def xfmb_chunk(self, c, shifted=False):
        T = self.T
        off = c * (T + 1) + (0 if shifted else 1)
        return self.Xfmb.ap(off, [[1, T]])

    def xtm_tile(self, i):
        return self.Xtm[:, i * 1024:(i + 1) * 1024]

    def rebuild_xfmb(self, ph, x_deps, trb):
        NT, T = self.NT, self.T
        xb = self.xtmb_scr
        for c in range(ND):
            ph.op("vector", lambda e, c=c: e.memset(
                self.Xfmb.ap(c * (T + 1), [[1, 1]]), 0.0))
        guard = [None, None]
        prev_t = getattr(self, "_scr_guard", None)
        last_v = None
        last_s = None
        for i in range(NT):
            dep = [x_deps[i] if isinstance(x_deps, list) else x_deps, prev_t]
            cms = ph.op("scalar", lambda e, i=i: e.copy(xb[:], self.xtm_tile(i)), deps=dep, inc=True)
            for c in range(ND):
                bank = trb[c % 2]
                tms = ph.op("tensor",
                            lambda e, b=bank, c=c: e.transpose(b[:], xb[:, c * 128:(c + 1) * 128], self.identb[:]),
                            deps=[cms, self.ms_identb, guard[c % 2]], inc=True)
                prev_t = tms
                eng = "vector" if c % 2 == 0 else "scalar"
                dst = self.Xfmb.ap(c * (T + 1) + 1 + i * 128, [[1, 128]])
                ems = ph.op(eng, lambda e, b=bank, d=dst: _copy(e, d, b[:]), deps=[tms], inc=True)
                guard[c % 2] = ems
                if eng == "vector":
                    last_v = ems
                else:
                    last_s = ems
        self.ms_xfmb = [last_v, last_s]

    def p_load(self):
        T, NT = self.T, self.NT
        x = self.dram_in("x", [T, 1024])
        identb_d = self.dram_in("identb", [128, 128])
        maskup_d = self.dram_in("maskup", [128, 128])
        masklo_d = self.dram_in("masklo", [128, 128])

        ph = self.phase("load")
        for i in range(NT):
            ph.dma("sync", self.xtm_tile(i), x[i * 128:(i + 1) * 128, :])
        self.ms_identb = ph.dma("gpsimd", self.identb[:], identb_d)
        self.ms_identf = ph.dma("sync", self.identf[:], identb_d)
        ph.dma("gpsimd", self.maskup[:], maskup_d)
        ph.dma("gpsimd", self.masklo[:], masklo_d)
        ms_gp_all = ph.qtotal("gpsimd")
        ms_sync_all = ph.qtotal("sync")
        lds = [ms_sync_all] * NT
        self.ms_maskup = ms_gp_all
        self.ms_masklo = ms_gp_all
        self.ms_identb = ms_gp_all
        self.ms_identf = ms_sync_all
        for v, k in self._fcmap.items():
            ph.op("vector", lambda e, v=v, k=k: e.memset(self.fct[:, k:k + 1], v))
        ph.op("vector", lambda e: e.memset(self.onesb[:], 1.0), inc=True)
        ph.op("vector", lambda e: e.memset(self.onesrow[:], 1.0), inc=True)
        self.ms_ones = (self.kb.sems["vector"], self.kb.sems["vector"].n)
        self.ms_xtm = lds
        with contextlib.ExitStack() as pes:
            trb = [pes.enter_context(self.nc.psum_tensor("trl0", [128, 128], BF16)),
                   pes.enter_context(self.nc.psum_tensor("trl1", [128, 128], BF16))]
            self.rebuild_xfmb(ph, lds, trb)

    def emit_rope_l0(self, ph, src, dst, col0, ngroups, ttile, cos_s, sin_s, rt1, rt2, deps):
        ev = lambda t, o: t.ap(col0 + o, [[32, ngroups], [2, 16]])
        cosap = cos_s.ap(ttile * 16, [[0, ngroups], [1, 16]])
        sinap = sin_s.ap(ttile * 16, [[0, ngroups], [1, 16]])
        r1 = rt1.ap(0, [[16, ngroups], [1, 16]])
        r2 = rt2.ap(0, [[16, ngroups], [1, 16]])
        g = self._rope_guard
        m1 = ph.op("vector", lambda e: e.tensor_tensor(r1, ev(src, 0), cosap, ALU.mult), deps=list(deps) + [g], inc=True)
        m2 = ph.op("gpsimd", lambda e: e.tensor_tensor(r2, ev(src, 1), sinap, ALU.mult), deps=list(deps) + [g], inc=True)
        m3 = ph.op("vector", lambda e: e.tensor_tensor(ev(dst, 0), r1, r2, ALU.subtract), deps=[m1, m2], inc=True)
        m4 = ph.op("vector", lambda e: e.tensor_tensor(r1, ev(src, 0), sinap, ALU.mult), deps=[m3], inc=True)
        m5 = ph.op("gpsimd", lambda e: e.tensor_tensor(r2, ev(src, 1), cosap, ALU.mult), deps=[m3], inc=True)
        m6 = ph.op("vector", lambda e: e.tensor_tensor(ev(dst, 1), r1, r2, ALU.add), deps=[m4, m5], inc=True)
        self._rope_guard = m6
        return m6

    def p_l0_attn(self):
        T, NT, NQG = self.T, self.NT, self.NQG
        nc, kb = self.nc, self.kb
        scale = 1.0 / math.sqrt(32.0)

        wq_d = self.dram_in("wq_t", [1024, 1024])
        wkv_d = self.dram_in("wkv_t", [1024, 512])
        wout_d = self.dram_in("wout3", [64, 16384])
        lam_d = self.dram_in("lam", [1, 16])
        rmsw_d = self.dram_in("rmsw", [64, 1])
        cos_d = self.dram_in("l0cos", [T, 16])
        sin_d = self.dram_in("l0sin", [T, 16])

        self.arena_reset()
        wqh = self.aB(4096)
        wx = self.aB(8192)
        qkv_tm = self.aB(1024)
        qkv2 = self.aB(1024)
        rt1 = self.aB(512)
        rt2 = self.aB(512)
        Krep = self.aB(4 * T)
        Vtm = self.aB(NT * 260)
        Qfm = self.aB(4096)
        PT = self.aB(2048)
        maskcat = self.aB(512)
        qsqb = self.aB(1024)
        ybuf = self.aB(512)
        ysub = self.aB(512)
        ones2 = self.aB(64)
        yfm = self.aB(8192)
        ysq = self.aB(512)
        lam_sb = self.aB(16)
        orb = self.aB(1024)
        lam_s = self.aF(16)
        lamb = self.aF(16)
        rmsw_s = self.aF(8)
        cos_s = self.aF(NT * 16)
        sin_s = self.aF(NT * 16)
        o_row = self.aF(512)
        of1 = self.aF(512)
        of2 = self.aF(512)
        qn2 = self.aF(32)
        qn2m = self.aF(32)
        kn2a = self.aF(8)
        nmb = self.aF(32)
        nrow = self.aF(40)

        with contextlib.ExitStack() as pes:
            E = pes.enter_context
            pst = lambda n, s, d=F32: E(nc.psum_tensor(n, list(s), d))
            ps_gen = pst("ps_gen", [128, 512])
            ps_r = [pst(f"ps_r{j}", [128, 512]) for j in range(4)]
            ps_o1 = pst("ps_o1", [128, 512])
            ps_o2 = pst("ps_o2", [128, 512])
            ps_rr = pst("ps_rr", [128, 512])
            gen_b = ps_gen[:].bitcast(BF16)

            ph = self.phase("l0a_load")
            ph.dma("gpsimd", wx[:, 0:4096].rearrange("p (c f) -> p c f", c=8),
                   wkv_d.rearrange("(c p) f -> p c f", p=128))
            ms_wkv = ph.qtotal("gpsimd")
            ph.dma("sync", lam_s[0:1, 0:16], lam_d)
            ph.dma("sync", rmsw_s[0:64, 0:1], rmsw_d)
            ph.dma("sync", rmsw_s[64:128, 0:1], rmsw_d)
            ph.dma("sync", cos_s[:, 0:NT * 16].rearrange("p (a j) -> p a j", j=16),
                   cos_d.rearrange("(a p) j -> p a j", p=128))
            ph.dma("sync", sin_s[:, 0:NT * 16].rearrange("p (a j) -> p a j", j=16),
                   sin_d.rearrange("(a p) j -> p a j", p=128))
            ms_sl = ph.qtotal("sync")
            ms_lam = ms_rw = ms_cos = ms_sin = ms_sl
            ms_sig = ph.op("scalar", lambda e: e.activation(lam_sb[0:1, 0:16], lam_s[0:1, 0:16], AF.Sigmoid),
                           deps=[ms_lam], inc=True)
            ms_bcl = ph.op("tensor", lambda e: e.matmul(ps_gen[:, 0:16], self.onesrow[0:1, :],
                                                        lam_sb[0:1, 0:16], start=True, stop=True),
                           deps=[ms_sig, self.ms_ones], inc=True)
            ms_lamb = ph.op("vector", lambda e: e.tensor_copy(lamb[:, 0:16], ps_gen[:, 0:16]),
                            deps=[ms_bcl], inc=True)
            ms_rope_tab = [ms_cos, ms_sin]
            self._rope_guard = None
            mc1 = ph.op("vector", lambda e: e.memset(maskcat[:, 0:512], NEG), inc=True)
            ms_mc = ph.op("vector", lambda e: e.tensor_copy(maskcat[:, 384:512], self.masklo[:]),
                          deps=[mc1, self.ms_masklo], inc=True)
            ms_o2 = ph.op("vector", lambda e: e.memset(ones2[:, 0:64], 1.0), inc=True)
            ms_vones = ph.op("gpsimd", lambda e: e.memset(Vtm.ap(64, [[65, 4 * NT], [1, 1]]), 1.0),
                             inc=True)

            ph = self.phase("l0a_kv")
            self._ksq_g = None
            ms_ktile = [None] * NT
            ms_vtile = [None] * NT
            guard_gen = [ms_lamb]
            nkv = NT
            if self.upto.startswith("l0a_proj") and self.upto != "l0a_proj":
                nkv = int(self.upto[len("l0a_proj"):])
            for i in range(nkv):
                deps0 = self.ms_xfmb + [ms_wkv] + list(guard_gen)
                for c in range(ND):
                    lhs = self.xfmb_chunk(c)[:, i * 128:(i + 1) * 128]
                    rhs = wx[:, c * 512:(c + 1) * 512]
                    ph.op("tensor", lambda e, l=lhs, r=rhs, s=(c == 0), t=(c == ND - 1):
                          e.matmul(ps_gen[:], l, r, start=s, stop=t),
                          deps=deps0 if c == 0 else (), inc=(c == ND - 1))
                ms_mm = self.tmark()
                ms_ek = ph.op("vector", lambda e: e.tensor_copy(qkv_tm[:, 0:256], ps_gen[:, 0:256]),
                              deps=[ms_mm], inc=True)
                ms_ev = ph.op("vector", lambda e, i=i: e.tensor_copy(
                    Vtm.ap(i * 260, [[65, 4], [1, 64]]),
                    bass.AP(ps_gen, 256, [[512, 128], [64, 4], [1, 64]])),
                    deps=[ms_mm] + ([ms_vones] if i == 0 else []), inc=True)
                ms_vtile[i] = ms_ev
                msq_k = ph.op("scalar", lambda e: e.activation(qsqb[:, 0:256], qkv_tm[:, 0:256], AF.Square),
                              deps=[ms_ek, self._ksq_g], inc=True)
                mkr = ph.op("vector", lambda e: e.reduce_sum(
                    qn2[:, 0:8], qsqb.ap(0, [[64, 4], [32, 2], [1, 32]]), axis=AX.X),
                    deps=[msq_k], inc=True)
                if i == 0:
                    mkr = ph.op("vector", lambda e: e.tensor_copy(kn2a[:, 0:8], qn2[:, 0:8]),
                                deps=[mkr], inc=True)
                else:
                    mkr = ph.op("vector", lambda e: e.tensor_tensor(
                        kn2a[:, 0:8], kn2a[:, 0:8], qn2[:, 0:8], ALU.max), deps=[mkr], inc=True)
                self._ksq_g = mkr
                if self.upto.startswith("l0a_proj"):
                    guard_gen = [ms_ek, ms_ev]
                    continue
                ms_rope = self.emit_rope_l0(ph, qkv_tm, qkv2, 0, 8, i, cos_s, sin_s, rt1, rt2,
                                            deps=[ms_ek] + ms_rope_tab)
                guard_tr = [ms_ek, ms_ev]
                tms = None
                for g in range(4):
                    d0 = gen_b[:, g * 128:(g + 1) * 128]
                    ph.op("tensor", lambda e, d=d0, g=g: e.transpose(
                        d[0:64, :], qkv2[:, g * 64:(g + 1) * 64], self.identb[:]),
                        deps=([ms_rope, self.ms_identb] + guard_tr) if g == 0 else ())
                    tms = ph.op("tensor", lambda e, d=d0, g=g: e.transpose(
                        d[64:128, :], qkv2[:, g * 64:(g + 1) * 64], self.identb[:]),
                        inc=(g == 3))
                last_e = None
                for g in range(4):
                    d0 = gen_b[:, g * 128:(g + 1) * 128]
                    ph.op("vector", lambda e, d=d0, g=g, i=i: e.tensor_copy(
                        Krep[0:64, g * T + i * 128: g * T + i * 128 + 128], d[0:64, :]),
                        deps=[tms], inc=True)
                    last_e = ph.op("vector", lambda e, d=d0, g=g, i=i: e.tensor_copy(
                        Krep[64:128, g * T + i * 128: g * T + i * 128 + 128], d[64:128, :]),
                        deps=[tms], inc=True)
                guard_gen = [last_e]
                ms_ktile[i] = last_e
            tkq = ph.op("tensor", lambda e: e.transpose(ps_rr[0:8, 0:128], kn2a[:, 0:8], self.identf[:]),
                        deps=[self._ksq_g, self.ms_identf], inc=True)
            rkq = ph.op("vector", lambda e: e.reduce_max(
                qn2m[0:8, 0:1], bass.AP(ps_rr, 0, [[512, 8], [1, 128]]), axis=AX.X),
                deps=[tkq], inc=True)
            tk2 = ph.op("tensor", lambda e: e.transpose(ps_rr[0:1, 0:8], qn2m[0:8, 0:1],
                                                        self.identf[0:8, 0:8]), deps=[rkq], inc=True)
            ms_krow = ph.op("vector", lambda e: e.tensor_copy(nrow[0:1, 32:40], ps_rr[0:1, 0:8]),
                            deps=[tk2], inc=True)
            if self.upto.startswith("l0a_proj") or self.upto in ("l0a_rope", "l0a_kv"):
                return

            self._ring_g = [None] * 4
            self._pt_g = [None] * 4
            self._o1_g = None
            self._o2_g = None
            self._rr_g = None
            self._gen_g = None
            self._of_g = None
            self._ybuf_g = None
            self._orb_g = None
            self._qsq_g = self._ksq_g
            self._ropeg = [self._rope_guard, self._rope_guard]
            self._q2g = [None, None]
            guard_wqh = []
            guard_wx = self.tmark()
            x_raw = [None] * NT
            nqg = 1 if self.upto in ("l0a_h1", "l0a_1qg") else NQG
            for qg in range(nqg):
                ph = self.phase(f"l0a_qg{qg}")
                ms_qfm = [None] * ND
                for half in range(2):
                    ph.op("gpsimd", lambda e, h=half: e.dma_start(
                        wqh[:, 0:4096].rearrange("p (c f) -> p c f", c=8),
                        wq_d[:, h * 512:(h + 1) * 512].rearrange("(c p) f -> p c f", p=128)),
                        deps=guard_wqh, inc=True, inc_sem=kb.sems["dgpsimd"], inc_n=16)
                    ms_wq = ph.qtotal("gpsimd")
                    for ii in range(4):
                        i = qg * 4 + ii
                        deps0 = self.ms_xfmb + [ms_wq] + list(guard_gen)
                        for c in range(ND):
                            lhs = self.xfmb_chunk(c)[:, i * 128:(i + 1) * 128]
                            rhs = wqh[:, c * 512:(c + 1) * 512]
                            ph.op("tensor", lambda e, l=lhs, r=rhs, s=(c == 0), t=(c == ND - 1):
                                  e.matmul(ps_gen[:], l, r, start=s, stop=t),
                                  deps=deps0 if c == 0 else (), inc=(c == ND - 1))
                        ms_mm = self.tmark()
                        par = ii % 2
                        self._rope_guard = self._ropeg[par]
                        ms_eq = ph.op("vector", lambda e, par=par: e.tensor_copy(
                            qkv_tm[:, par * 512:par * 512 + 512], ps_gen[:]),
                            deps=[ms_mm, self._rope_guard], inc=True)
                        guard_gen = [ms_eq]
                        ms_rope = self.emit_rope_l0(ph, qkv_tm, qkv2, par * 512, 16, i, cos_s, sin_s,
                                                    rt1.sub(par * 256, 256), rt2.sub(par * 256, 256),
                                                    deps=[ms_eq, self._q2g[par]] + ms_rope_tab)
                        self._ropeg[par] = ms_rope
                        msq_q = ph.op("scalar", lambda e, par=par: e.activation(
                            qsqb[:, 0:512], qkv2[:, par * 512:par * 512 + 512], AF.Square),
                                      deps=[ms_rope, self._qsq_g], inc=True)
                        mrd_q = ph.op("vector", lambda e, half=half: e.reduce_sum(
                            qn2[:, half * 16:half * 16 + 16],
                            qsqb.ap(0, [[64, 8], [32, 2], [1, 32]]), axis=AX.X),
                            deps=[msq_q], inc=True)
                        if ii == 0:
                            mrd_q = ph.op("vector", lambda e, half=half: e.tensor_copy(
                                qn2m[:, half * 16:half * 16 + 16], qn2[:, half * 16:half * 16 + 16]),
                                deps=[mrd_q, self._qsq_g], inc=True)
                        else:
                            mrd_q = ph.op("vector", lambda e, half=half: e.tensor_tensor(
                                qn2m[:, half * 16:half * 16 + 16], qn2m[:, half * 16:half * 16 + 16],
                                qn2[:, half * 16:half * 16 + 16], ALU.max), deps=[mrd_q], inc=True)
                        self._qsq_g = mrd_q
                        tms = None
                        for cl in range(4):
                            d0 = gen_b[:, cl * 128:cl * 128 + 128]
                            tms = ph.op("tensor", lambda e, d=d0, cl=cl, par=par: e.transpose(
                                d, qkv2[:, par * 512 + cl * 128:par * 512 + (cl + 1) * 128], self.identb[:]),
                                deps=([ms_rope, msq_q] + list(guard_gen)) if cl == 0 else (),
                                inc=(cl == 3))
                        self._q2g[par] = tms
                        for cl in range(4):
                            c = half * 4 + cl
                            d0 = gen_b[:, cl * 128:cl * 128 + 128]
                            dst = Qfm[:, c * 512 + ii * 128: c * 512 + ii * 128 + 128]
                            ems = ph.op("vector", lambda e, d=dst, s=d0: e.tensor_copy(d, s),
                                        deps=[tms], inc=True)
                            ms_qfm[c] = ems
                        guard_gen = [ems]
                    guard_wqh = [self.tmark()]
                kmax = (qg + 1) * 512
                nkt = kmax // 128
                t1q = ph.op("tensor", lambda e: e.transpose(ps_rr[0:32, 0:128], qn2m[:, 0:32],
                                                            self.identf[:]),
                            deps=[self._qsq_g, self.ms_identf, self._rr_g], inc=True)
                r1q = ph.op("vector", lambda e: e.reduce_max(
                    qn2[0:32, 0:1], bass.AP(ps_rr, 0, [[512, 32], [1, 128]]), axis=AX.X),
                    deps=[t1q], inc=True)
                t2q = ph.op("tensor", lambda e: e.transpose(ps_rr[0:1, 0:32], qn2[0:32, 0:1],
                                                            self.identf[0:32, 0:32]), deps=[r1q], inc=True)
                nq1 = ph.op("vector", lambda e: e.tensor_copy(nrow[0:1, 0:32], ps_rr[0:1, 0:32]),
                            deps=[t2q], inc=True)
                nq2 = ph.op("vector", lambda e: e.tensor_tensor(
                    nrow[0:1, 0:32], nrow[0:1, 0:32],
                    nrow.ap(32, [[2, 4], [0, 4], [1, 2]], pcount=1), ALU.mult),
                    deps=[nq1, ms_krow], inc=True)
                nq3 = ph.op("scalar", lambda e: e.activation(nrow[0:1, 0:32], nrow[0:1, 0:32], AF.Sqrt),
                            deps=[nq2], inc=True)
                nq4 = ph.op("vector", lambda e: e.tensor_scalar_mul(nrow[0:1, 0:32], nrow[0:1, 0:32],
                                                                   -scale), deps=[nq3], inc=True)
                nq5 = ph.op("vector", lambda e: e.tensor_copy(orb[0:1, 0:32], nrow[0:1, 0:32]),
                            deps=[nq4], inc=True)
                nq6 = ph.op("tensor", lambda e: e.matmul(ps_rr[:, 0:32], self.onesrow[0:1, :],
                                                         orb[0:1, 0:32], start=True, stop=True),
                            deps=[nq5, self.ms_ones, t2q], inc=True)
                ms_nmb = ph.op("vector", lambda e: e.tensor_copy(nmb[:, 0:32], ps_rr[:, 0:32]),
                               deps=[nq6], inc=True)
                self._rr_g = ms_nmb
                self._gen_g = guard_gen[-1] if guard_gen else None
                nheads = 1 if self.upto == "l0a_h1" else 16
                pending_norm = None
                for h in range(nheads):
                    qchunk, qb, kvh = h // 2, 64 * (h % 2), h // 4
                    ntl = 2 * nkt
                    tiles = [(kt, s) for kt in range(nkt) for s in (0, 1)]
                    msx = [None] * ntl
                    last_av = [None, None]

                    def emit_tile(j, h=h, qchunk=qchunk, qb=qb, kvh=kvh, tiles=tiles, msx=msx):
                        kt, s12 = tiles[j]
                        r = j % 4
                        lhs = Krep[qb + 32 * s12: qb + 32 * s12 + 32,
                                   kvh * T + kt * 128: kvh * T + kt * 128 + 128]
                        rhs = Qfm[qb + 32 * s12: qb + 32 * s12 + 32,
                                  qchunk * 512: qchunk * 512 + 512]
                        mm = ph.op("tensor", lambda e, l=lhs, rr=rhs, b=ps_r[r], tp=(qb + 32 * s12, 0):
                                   e.matmul(b[:], l, rr, start=True, stop=True, tile_position=tp),
                                   deps=[ms_qfm[qchunk], ms_ktile[kt], self._ring_g[r]], inc=True)
                        dep_e = mm
                        ktg = kt - qg * 4
                        if ktg >= 0:
                            w = (ktg + 1) * 128
                            dep_e = ph.op("vector", lambda e, b=ps_r[r], w=w: e.tensor_add(
                                b[:, 0:w], b[:, 0:w], maskcat[:, 512 - w:512]),
                                deps=[mm, ms_mc], inc=True)
                        msx[j] = ph.op("scalar", lambda e, b=ps_r[r], r=r, col=h * 2 + tiles[j][1]:
                                       e.activation(PT[:, r * 512:(r + 1) * 512], b[:], AF.Exp,
                                                    bias=nmb[:, col:col + 1],
                                                    scale=self.fc(0.1767766952966369)),
                                       deps=[dep_e, ms_nmb, self._pt_g[r]], inc=True)
                        self._ring_g[r] = msx[j]

                    def emit_av(j, h=h, kvh=kvh, tiles=tiles, msx=msx, last_av=last_av, nkt=nkt):
                        kt, s12 = tiles[j]
                        r = j % 4
                        ob = ps_o1 if s12 == 0 else ps_o2
                        og = self._o1_g if s12 == 0 else self._o2_g
                        ms_av = ph.op("tensor", lambda e, kt=kt, r=r, b=ob, kvh=kvh:
                                      e.matmul(b[0:65, :], Vtm.ap(kt * 260 + kvh * 65, [[1, 65]]),
                                               PT[:, r * 512:(r + 1) * 512],
                                               start=(kt == 0), stop=(kt == nkt - 1)),
                                      deps=[msx[j], ms_vtile[kt]] + ([ms_vones, og] if kt == 0 else []),
                                      inc=True)
                        self._pt_g[r] = ms_av
                        last_av[s12] = ms_av

                    def make_norm(h=h, qb=qb, last_av=last_av):
                        st8 = {}

                        def normA():
                            ev1 = ph.op("vector", lambda e: e.tensor_copy(of1[0:65, 0:512], ps_o1[0:65, :]),
                                        deps=[last_av[0], self._of_g], inc=True)
                            ev2 = ph.op("vector", lambda e: e.tensor_copy(of2[0:65, 0:512], ps_o2[0:65, :]),
                                        deps=[last_av[1], self._of_g], inc=True)
                            self._o1_g = ev1
                            self._o2_g = ev2
                            rc1 = ph.op("vector", lambda e: e.reciprocal(of1[64:65, 0:512], of1[64:65, 0:512]),
                                        deps=[ev1], inc=True)
                            rc2 = ph.op("vector", lambda e: e.reciprocal(of2[64:65, 0:512], of2[64:65, 0:512]),
                                        deps=[ev2], inc=True)
                            cb1 = ph.op("vector", lambda e: e.tensor_copy(orb[64:65, 0:512], of1[64:65, 0:512]),
                                        deps=[rc1, self._orb_g], inc=True)
                            cb2 = ph.op("vector", lambda e: e.tensor_copy(orb[64:65, 512:1024], of2[64:65, 0:512]),
                                        deps=[rc2], inc=True)
                            st8["bc1"] = ph.op("tensor", lambda e: e.matmul(
                                ps_rr[0:64, :], ones2[64:65, 0:64], orb[64:65, 0:512],
                                start=True, stop=True, tile_position=(64, 0)),
                                deps=[cb1, self._rr_g], inc=True)
                            st8["bc2"] = ph.op("tensor", lambda e: e.matmul(
                                ps_gen[0:64, :], ones2[64:65, 0:64], orb[64:65, 512:1024],
                                start=True, stop=True, tile_position=(64, 0)),
                                deps=[cb2, self._gen_g], inc=True)
                            st8["ev"] = (ev1, ev2)
                            st8["cb"] = (cb1, cb2)

                        def normB():
                            ev1, ev2 = st8["ev"]
                            ya = ph.op("vector", lambda e: e.tensor_tensor(
                                ybuf[0:64, 0:512], of1[0:64, 0:512], ps_rr[0:64, :], ALU.mult),
                                deps=[st8["bc1"], ev1, self._ybuf_g], inc=True)
                            yb_ = ph.op("vector", lambda e, h=h: e.scalar_tensor_tensor(
                                ysub[0:64, 0:512], of2[0:64, 0:512], lamb[0:64, h:h + 1],
                                ps_gen[0:64, :], ALU.mult, ALU.mult),
                                deps=[st8["bc2"], ev2, ms_lamb], inc=True)
                            yc = ph.op("vector", lambda e: e.tensor_tensor(
                                ybuf[0:64, 0:512], ybuf[0:64, 0:512], ysub[0:64, 0:512], ALU.subtract),
                                deps=[ya, yb_], inc=True)
                            sq = ph.op("scalar", lambda e: e.activation(ysq[0:64, 0:512], ybuf[0:64, 0:512],
                                                                       AF.Square), deps=[yc], inc=True)
                            st8["rs"] = ph.op("tensor", lambda e: e.matmul(
                                ps_rr[0:1, :], self.onesb[0:64, 0:1], ysq[0:64, 0:512],
                                start=True, stop=True),
                                deps=[sq, self.ms_ones, ya], inc=True)
                            st8["yc"] = yc
                            st8["yb_"] = yb_
                            self._of_g = yc

                        def normC():
                            st_ = ph.op("scalar", lambda e: e.activation(
                                o_row[0:1, 0:512], ps_rr[0:1, :], AF.Sqrt,
                                bias=self.fc(1e-6, 1), scale=self.fc(0.015625, 1)),
                                deps=[st8["rs"]], inc=True)
                            rc = ph.op("vector", lambda e: e.reciprocal(o_row[0:1, 0:512], o_row[0:1, 0:512]),
                                       deps=[st_], inc=True)
                            cbr = ph.op("vector", lambda e: e.tensor_copy(orb[0:1, 512:1024], o_row[0:1, 0:512]),
                                        deps=[rc, st8["cb"][1]], inc=True)
                            self._orb_g = cbr
                            bcr = ph.op("tensor", lambda e: e.matmul(
                                ps_gen[0:64, :], ones2[0:1, 0:64], orb[0:1, 512:1024],
                                start=True, stop=True),
                                deps=[cbr, st8["yb_"]], inc=True)
                            ap_ = ph.op("vector", lambda e, h=h: e.scalar_tensor_tensor(
                                yfm[0:64, h * 512:h * 512 + 512], ybuf[0:64, 0:512],
                                rmsw_s[0:64, 0:1], ps_gen[0:64, :], ALU.mult, ALU.mult),
                                deps=[bcr, ms_rw, st8["yc"]], inc=True)
                            self._rr_g = st_
                            self._gen_g = ap_
                            self._ybuf_g = ap_
                            self._yfm_last = ap_
                        return [normA, normB, normC]

                    for j in range(min(4, ntl)):
                        emit_tile(j)
                    if pending_norm is not None:
                        pending_norm[0]()
                    for j in range(4, ntl):
                        emit_av(j - 4)
                        emit_tile(j)
                        if pending_norm is not None:
                            if j == 5:
                                pending_norm[1]()
                            elif j == min(7, ntl - 1):
                                pending_norm[2]()
                                pending_norm = None
                    if pending_norm is not None:
                        pending_norm[1]()
                        pending_norm[2]()
                        pending_norm = None
                    for j in range(max(0, ntl - 4), ntl):
                        emit_av(j)
                    pending_norm = make_norm()
                pending_norm[0]()
                pending_norm[1]()
                pending_norm[2]()
                pending_norm = None
                guard_gen = [self._gen_g]
                yv = (kb.sems["vector"], kb.sems["vector"].n)
                ys = (kb.sems["scalar"], kb.sems["scalar"].n)
                for half in range(2):
                    ph.op("gpsimd", lambda e, h=half: e.dma_start(
                        wx.ap(0, [[512, 16], [1, 512]], pcount=64),
                        bass.AP(wout_d.tensor, wout_d.offset + h * 512,
                                [[16384, 64], [1024, 16], [1, 512]])),
                        deps=[guard_wx], inc=True, inc_sem=kb.sems["dgpsimd"], inc_n=16)
                    ms_wo = ph.qtotal("gpsimd")
                    for ii in range(4):
                        qt = qg * 4 + ii
                        deps0 = [yv, ys, ms_wo] + list(guard_gen)
                        for c in range(16):
                            lhs = yfm[0:64, c * 512 + ii * 128: c * 512 + ii * 128 + 128]
                            rhs = wx.ap(c * 512, [[1, 512]], pcount=64)
                            ph.op("tensor", lambda e, l=lhs, r=rhs, s=(c == 0), t=(c == 15):
                                  e.matmul(ps_gen[:], l, r, start=s, stop=t),
                                  deps=deps0 if c == 0 else (), inc=(c == 15))
                        ms_mm = self.tmark()
                        dstx = self.Xtm[:, qt * 1024 + half * 512: qt * 1024 + half * 512 + 512]
                        ms_res = ph.op("vector", lambda e, d=dstx: e.tensor_add(d, d, ps_gen[:]),
                                       deps=[ms_mm, self.ms_xtm[qt]], inc=True)
                        guard_gen = [ms_res]
                        if half == 1:
                            x_raw[qt] = ms_res
                    guard_wx = self.tmark()
            for i in range(NT):
                if x_raw[i] is None:
                    x_raw[i] = self.ms_xtm[i] if isinstance(self.ms_xtm, list) else self.ms_xtm
            self.ms_x_raw = x_raw

    def p_l0_ffn(self):
        T, NT = self.T, self.NT
        nc, kb = self.nc, self.kb
        NE = 34

        moe_d = self.dram_in("moe_w", [NE, 1572864])
        gw_d = self.dram_in("gw_t", [1024, 32])
        gb_d = self.dram_in("gbias", [1, 32])

        self.arena_reset()
        gw_s = self.aB(256)
        gb_sb = self.aB(32)
        logit = self.aB(512)
        mw = [self.aB(12288), self.aB(12288)]
        sgil = self.aB(1024)
        htm = self.aB(1024)
        hfm = self.aB(1024)
        gb_s = self.aF(32)
        gb_b = self.aF(32)
        SR = self.aF(NT * 32)
        SC = self.aF(NT * 32)
        EQ = self.aF(NT * 32)
        WA = self.aF(NT * 32)
        g64a = self.aF(NT * 4)
        g64b = self.aF(NT * 4)
        g16a = self.aF(NT)
        g16b = self.aF(NT)

        with contextlib.ExitStack() as pes:
            E = pes.enter_context
            pst = lambda n, s, d=F32: E(nc.psum_tensor(n, list(s), d))
            psA = pst("psA", [128, 512])
            psB = pst("psB", [128, 512])
            psC = pst("psC", [128, 512])
            psD = pst("psD", [128, 512])
            psA2 = pst("psA2", [128, 512])
            psB2 = pst("psB2", [128, 512])
            psC2 = pst("psC2", [128, 512])
            psD2 = pst("psD2", [128, 512])
            pAs = [psA, psA2]
            pBs = [psB, psB2]
            pCs = [psC, psC2]
            pDs = [psD, psD2]
            psR = psC2
            rtr_b = psD2[:].bitcast(BF16)
            slot_sem = [Sem(pes.enter_context(nc.semaphore("mw_s0"))),
                        Sem(pes.enter_context(nc.semaphore("mw_s1")))]

            ph = self.phase("l0f_router")
            ph.dma("gpsimd", gw_s[:, 0:256].rearrange("p (c f) -> p c f", c=8),
                   gw_d.rearrange("(c p) f -> p c f", p=128))
            ms_ld = ph.qtotal("gpsimd")
            ph.dma("sync", gb_s[0:1, 0:32], gb_d)
            ms_gb = ph.qtotal("sync")
            ms_c = ph.op("scalar", lambda e: e.copy(gb_sb[0:1, 0:32], gb_s[0:1, 0:32]), deps=[ms_gb], inc=True)
            ms_b = ph.op("tensor", lambda e: e.matmul(psR[:, 0:32], self.onesrow[0:1, :],
                                                      gb_sb[0:1, 0:32], start=True, stop=True),
                         deps=[ms_c, self.ms_ones], inc=True)
            ms_gbb = ph.op("vector", lambda e: e.tensor_copy(gb_b[:, 0:32], psR[:, 0:32]), deps=[ms_b], inc=True)
            guard_r = [ms_gbb]
            self._sig_guard = None
            for tg in range(NT // 4):
                deps0 = self.ms_xfmb + [ms_ld] + guard_r
                for c in range(ND):
                    lhs = gw_s[:, c * 32:(c + 1) * 32]
                    rhs = self.xfmb_chunk(c)[:, tg * 512:(tg + 1) * 512]
                    ph.op("tensor", lambda e, l=lhs, r=rhs, st=(c == 0), sp=(c == ND - 1):
                          e.matmul(psR[0:32, :], l, r, start=st, stop=sp),
                          deps=deps0 if c == 0 else (), inc=(c == ND - 1))
                ms_mm = self.tmark()
                ms_ev = ph.op("vector", lambda e: e.tensor_copy(logit[0:32, 0:512], psR[0:32, :]),
                              deps=[ms_mm, self._sig_guard], inc=True)
                for ii in range(4):
                    i = tg * 4 + ii
                    tms = ph.op("tensor", lambda e, ii=ii: e.transpose(
                        rtr_b[:, 0:32], logit[0:32, ii * 128:(ii + 1) * 128], self.identb[0:32, 0:32]),
                        deps=[ms_ev, self._sig_guard], inc=True)
                    ms_sg = ph.op("scalar", lambda e, i=i: e.activation(
                        SR[:, i * 32:(i + 1) * 32], rtr_b[:, 0:32], AF.Sigmoid), deps=[tms], inc=True)
                    self._sig_guard = ms_sg
                guard_r = [ms_ev]

            ph = self.phase("l0f_route")
            NE32 = NT * 32
            vv = lambda t: t[:, 0:NE32]
            grp8 = lambda t: t.ap(0, [[32, NT], [8, 4], [1, 8]])
            te = lambda t: t.ap(0, [[32, NT], [1, 32]])
            ms0 = ph.op("vector", lambda e: e.tensor_tensor(
                te(SC), te(SR), gb_b.ap(0, [[0, NT], [1, 32]]), ALU.add),
                deps=[(kb.sems["scalar"], kb.sems["scalar"].n)], inc=True)
            ms1 = ph.op("vector", lambda e: e.reduce_max(
                g64a[:, 0:NT * 4], grp8(SC), axis=AX.X), deps=[ms0], inc=True)
            ms2 = ph.op("vector", lambda e: e.tensor_tensor(
                grp8(EQ), grp8(SC), g64a.ap(0, [[4, NT], [1, 4], [0, 8]]), ALU.is_equal),
                deps=[ms1], inc=True)
            ms3 = ph.op("vector", lambda e: e.scalar_tensor_tensor(
                vv(EQ), vv(EQ), -1.0e30, vv(SC), ALU.mult, ALU.add), deps=[ms2], inc=True)
            ms4 = ph.op("vector", lambda e: e.reduce_max(
                g64b[:, 0:NT * 4], grp8(EQ), axis=AX.X), deps=[ms3], inc=True)
            ms5 = ph.op("vector", lambda e: e.tensor_add(
                g64a[:, 0:NT * 4], g64a[:, 0:NT * 4], g64b[:, 0:NT * 4]), deps=[ms4, ms1], inc=True)
            ms6 = ph.op("vector", lambda e: e.reduce_max(
                g16a[:, 0:NT], g64a.ap(0, [[4, NT], [1, 4]]), axis=AX.X), deps=[ms5], inc=True)
            ms7 = ph.op("vector", lambda e: e.tensor_tensor(
                g64b.ap(0, [[4, NT], [1, 4]]), g64a.ap(0, [[4, NT], [1, 4]]),
                g16a.ap(0, [[1, NT], [0, 4]]), ALU.is_equal), deps=[ms6, ms5], inc=True)
            ms8 = ph.op("vector", lambda e: e.scalar_tensor_tensor(
                g64b[:, 0:NT * 4], g64b[:, 0:NT * 4], -1.0e30, g64a[:, 0:NT * 4],
                ALU.mult, ALU.add), deps=[ms7], inc=True)
            ms9 = ph.op("vector", lambda e: e.reduce_max(
                g16b[:, 0:NT], g64b.ap(0, [[4, NT], [1, 4]]), axis=AX.X), deps=[ms8], inc=True)
            ms10 = ph.op("vector", lambda e: e.tensor_tensor(
                g64b.ap(0, [[4, NT], [1, 4]]), g64a.ap(0, [[4, NT], [1, 4]]),
                g16b.ap(0, [[1, NT], [0, 4]]), ALU.is_ge), deps=[ms9, ms5], inc=True)
            msk = ph.op("vector", lambda e: e.tensor_tensor(
                grp8(EQ), grp8(SC), g64b.ap(0, [[4, NT], [1, 4], [0, 8]]), ALU.mult),
                deps=[ms10, ms0], inc=True)
            cpy = ph.op("vector", lambda e: e.tensor_copy(vv(SC), vv(EQ)), deps=[msk], inc=True)
            prev = cpy
            for it in range(7):
                mA = ph.op("vector", lambda e: e.reduce_max(
                    g64a[:, 0:NT], te(SC), axis=AX.X), deps=[prev], inc=True)
                mB = ph.op("vector", lambda e: e.tensor_tensor(
                    te(WA), te(SC), g64a.ap(0, [[1, NT], [0, 32]]), ALU.is_equal),
                    deps=[mA], inc=True)
                prev = ph.op("vector", lambda e: e.scalar_tensor_tensor(
                    vv(SC), vv(WA), -1.0e30, vv(SC), ALU.mult, ALU.add), deps=[mB], inc=True)
            mt8 = ph.op("vector", lambda e: e.reduce_max(
                g16a[:, 0:NT], te(SC), axis=AX.X), deps=[prev], inc=True)
            msel = ph.op("vector", lambda e: e.tensor_tensor(
                te(EQ), te(EQ), g16a.ap(0, [[1, NT], [0, 32]]), ALU.is_ge), deps=[mt8, msk], inc=True)
            mwr = ph.op("vector", lambda e: e.tensor_mul(vv(WA), vv(EQ), vv(SR)), deps=[msel], inc=True)
            mws = ph.op("vector", lambda e: e.reduce_sum(
                g16b[:, 0:NT], te(WA), axis=AX.X), deps=[mwr], inc=True)
            mw1 = ph.op("vector", lambda e: e.tensor_scalar_add(g16b[:, 0:NT], g16b[:, 0:NT], 1e-20),
                        deps=[mws], inc=True)
            mw2 = ph.op("vector", lambda e: e.reciprocal(g16a[:, 0:NT], g16b[:, 0:NT]),
                        deps=[mw1], inc=True)
            mw3 = ph.op("vector", lambda e: e.tensor_scalar_mul(g16a[:, 0:NT], g16a[:, 0:NT], 2.5),
                        deps=[mw2], inc=True)
            mwa = ph.op("vector", lambda e: e.tensor_tensor(
                te(WA), te(WA), g16a.ap(0, [[1, NT], [0, 32]]), ALU.mult), deps=[mw3], inc=True)

            ph = self.phase("l0f_main")
            slot_ms = [None, None]
            slot_guard = [None, None]
            pending_dn = None
            self._moe_last_dn = None
            x_add_ms = [self.ms_xtm[i] if isinstance(self.ms_xtm, list) else self.ms_xtm
                        for i in range(NT)]
            GOF, UOF, DOF = 0, 8 * 512, 8 * 512 * 2
            guardA = [None, None]
            guardB = [None, None]
            guardC = [None, None]
            guardD = [None, None]
            sil_g = [None, None]
            htm_g = [None, None]
            hfm_g = [None, None]
            for e in range(NE):
                sl = e % 2
                src = moe_d[e:e + 1, :]
                ph.op("gpsimd", lambda ee, s=src, sl=sl: ee.dma_start(
                    mw[sl][:, GOF:GOF + 8 * 512].rearrange("p (c f) -> p c f", c=8),
                    s[:, 0:524288].rearrange("o (c p f) -> (o p) c f", c=8, p=128)),
                    deps=[slot_guard[sl]], inc=True, inc_sem=slot_sem[sl], inc_n=16)
                ph.op("gpsimd", lambda ee, s=src, sl=sl: ee.dma_start(
                    mw[sl][:, UOF:UOF + 8 * 512].rearrange("p (c f) -> p c f", c=8),
                    s[:, 524288:1048576].rearrange("o (c p f) -> (o p) c f", c=8, p=128)),
                    inc=True, inc_sem=slot_sem[sl], inc_n=16)
                ph.op("gpsimd", lambda ee, s=src, sl=sl: ee.dma_start(
                    mw[sl][:, DOF:DOF + 4 * 1024].rearrange("p (c f) -> p c f", c=4),
                    s[:, 1048576:1572864].rearrange("o (c p f) -> (o p) c f", c=4, p=128)),
                    inc=True, inc_sem=slot_sem[sl], inc_n=16)
                slot_ms[sl] = (slot_sem[sl], slot_sem[sl].n)
                last_mm = None
                for i in range(NT):
                    s2 = i % 2
                    pA, pB, pC, pD = pAs[s2], pBs[s2], pCs[s2], pDs[s2]
                    pTb = pA[:].bitcast(BF16)
                    for c in range(ND):
                        ph.op("tensor", lambda ee, c=c, i=i, sl=sl, b=pA, st=(c == 0), sp=(c == ND - 1):
                              ee.matmul(b[:], self.xfmb_chunk(c)[:, i * 128:(i + 1) * 128],
                                        mw[sl][:, GOF + c * 512:GOF + (c + 1) * 512], start=st, stop=sp),
                              deps=(self.ms_xfmb + [slot_ms[sl], guardA[s2]]) if c == 0 else (),
                              inc=(c == ND - 1))
                    ms_g = self.tmark()
                    for c in range(ND):
                        ph.op("tensor", lambda ee, c=c, i=i, sl=sl, b=pB, st=(c == 0), sp=(c == ND - 1):
                              ee.matmul(b[:], self.xfmb_chunk(c)[:, i * 128:(i + 1) * 128],
                                        mw[sl][:, UOF + c * 512:UOF + (c + 1) * 512], start=st, stop=sp),
                              deps=[guardB[s2]] if c == 0 else (), inc=(c == ND - 1))
                    ms_u = self.tmark()
                    ms_sil = ph.op("scalar", lambda ee, b=pA, s2=s2: ee.activation(
                        sgil[:, s2 * 512:s2 * 512 + 512], b[:], AF.Silu),
                        deps=[ms_g, sil_g[s2]], inc=True)
                    wcol = (WA[:, i * 32 + e:i * 32 + e + 1] if e < 32 else self.fc(1.0))
                    hdeps = [ms_u, ms_sil, htm_g[s2]] + ([mwa] if e < 32 else [])
                    ms_h = ph.op("vector", lambda ee, w=wcol, b=pB, s2=s2: ee.scalar_tensor_tensor(
                        htm[:, s2 * 512:s2 * 512 + 512], b[:], w, sgil[:, s2 * 512:s2 * 512 + 512],
                        ALU.mult, ALU.mult), deps=hdeps, inc=True)
                    guardB[s2] = ms_h
                    sil_g[s2] = ms_h
                    if pending_dn is not None:
                        pending_dn()
                        pending_dn = None
                    tms = None
                    for q in range(4):
                        tms = ph.op("tensor", lambda ee, q=q, pTb=pTb, s2=s2: ee.transpose(
                            pTb[:, q * 128:(q + 1) * 128],
                            htm[:, s2 * 512 + q * 128:s2 * 512 + (q + 1) * 128], self.identb[:]),
                            deps=[ms_h, ms_sil] if q == 0 else (), inc=(q == 3))
                    htm_g[s2] = tms
                    ms_he = ph.op("vector", lambda ee, pTb=pTb, s2=s2: ee.tensor_copy(
                        hfm[:, s2 * 512:s2 * 512 + 512], pTb[:, 0:512]),
                        deps=[tms, hfm_g[s2]], inc=True)
                    guardA[s2] = ms_he

                    def pending_dn(i=i, sl=sl, s2=s2, pC=pC, pD=pD, ms_he=ms_he):
                        for fh, bank, gg in ((0, pC, guardC), (1, pD, guardD)):
                            for ic in range(4):
                                ph.op("tensor", lambda ee, ic=ic, fh=fh, sl=sl, b=bank, s2=s2,
                                      st=(ic == 0), sp=(ic == 3):
                                      ee.matmul(b[:], hfm[:, s2 * 512 + ic * 128:s2 * 512 + (ic + 1) * 128],
                                                mw[sl][:, DOF + ic * 1024 + fh * 512:DOF + ic * 1024 + fh * 512 + 512],
                                                start=st, stop=sp),
                                      deps=[ms_he, gg[s2]] if ic == 0 else (), inc=(ic == 3))
                        ms_dn = self.tmark()
                        hfm_g[s2] = ms_dn
                        self._moe_last_dn = ms_dn
                        a1 = ph.op("vector", lambda ee, i=i, b=pC: ee.tensor_add(
                            self.Xtm[:, i * 1024:i * 1024 + 512], self.Xtm[:, i * 1024:i * 1024 + 512],
                            b[:]), deps=[ms_dn, x_add_ms[i]], inc=True)
                        guardC[s2] = a1
                        a2 = ph.op("vector", lambda ee, i=i, b=pD: ee.tensor_add(
                            self.Xtm[:, i * 1024 + 512:i * 1024 + 1024], self.Xtm[:, i * 1024 + 512:i * 1024 + 1024],
                            b[:]), deps=[ms_dn, a1], inc=True)
                        guardD[s2] = a2
                        x_add_ms[i] = a2
                pending_dn()
                pending_dn = None
                slot_guard[sl] = self._moe_last_dn
            self.ms_x_raw = x_add_ms


    def emit_rope_l1(self, ph, src, dst, ctab, stab, rt1, rt2, deps):
        ev = lambda t, o: t.ap(o, [[2, 256]])
        cosap = ctab[:, 0:256]
        sinap = stab[:, 0:256]
        r1 = rt1[:, 0:256]
        r2 = rt2[:, 0:256]
        g = self._rope_guard
        m1 = ph.op("vector", lambda e: e.tensor_tensor(r1, ev(src, 0), cosap, ALU.mult), deps=list(deps) + [g], inc=True)
        m2 = ph.op("gpsimd", lambda e: e.tensor_tensor(r2, ev(src, 1), sinap, ALU.mult), deps=list(deps) + [g], inc=True)
        m3 = ph.op("vector", lambda e: e.tensor_tensor(ev(dst, 0), r1, r2, ALU.subtract), deps=[m1, m2], inc=True)
        m4 = ph.op("vector", lambda e: e.tensor_tensor(r1, ev(src, 0), sinap, ALU.mult), deps=[m3], inc=True)
        m5 = ph.op("gpsimd", lambda e: e.tensor_tensor(r2, ev(src, 1), cosap, ALU.mult), deps=[m3], inc=True)
        m6 = ph.op("vector", lambda e: e.tensor_tensor(ev(dst, 1), r1, r2, ALU.add), deps=[m4, m5], inc=True)
        self._rope_guard = m6
        return m6

    def p_l1_attn(self):
        T, NT, NQG = self.T, self.NT, self.NQG
        nc, kb = self.nc, self.kb
        l1s = 0.031008683647302115

        wdq_d = self.dram_in("l1_wdq", [1024, 512])
        wdkv_d = self.dram_in("l1_wdkv", [1024, 512])
        wuq_d = self.dram_in("l1_wuq", [512, 1024])
        wqr_d = self.dram_in("l1_wqr", [512, 1024])
        wkr_d = self.dram_in("l1_wkr", [1024, 1024])
        wuk_d = self.dram_in("l1_wuk", [512, 1024])
        wuv_d = self.dram_in("l1_wuv", [512, 1024])
        wo_d = self.dram_in("l1_wo", [1024, 1024])
        ab_d = self.dram_in("l1ab", [8, 1024])
        cos_d = self.dram_in("l1cos", [T, 512])
        sin_d = self.dram_in("l1sin", [T, 512])

        self.arena_reset()
        ctq = self.aB(4 * T)
        ctkv = self.aB(4 * T)
        Kfm = self.aB(4 * T)
        Vtm = self.aB(NT * 512)
        PT = self.aB(1024)
        Qfm = self.aB(2048)
        yfm = self.aB(2048)
        qc_s = self.aB(512)
        qr_s = self.aB(512)
        qr2 = self.aB(512)
        wA = self.aB(4096)
        wB = self.aB(2048)
        wC = self.aB(2048)
        bb1 = self.aB(512)
        bb2 = self.aB(512)
        brow = self.aB(544)
        bb3 = brow.sub(0, 512)
        rstdb = qr2
        bcol = self.aF(16)
        rt1 = self.aF(256)
        rt2 = self.aF(256)
        u1 = self.aF(1536)
        sqs = u1.sub(0, 512)
        ctab = u1.sub(512, 256)
        stab = u1.sub(768, 256)
        o_row = u1.sub(1024, 512)
        kn2 = self.aF(128)
        qn2 = self.aF(32)
        kqm = self.aF(16)
        nmb = self.aF(8)
        mrow = self.aF(16)

        self._dbg = dict(ctq=ctq.base, ctkv=ctkv.base, Kfm=Kfm.base, Vtm=Vtm.base,
                         Qfm=Qfm.base, yfm=yfm.base, qn2=qn2.base, kn2=kn2.base,
                         nmb=nmb.base, PT=PT.base, mrow=mrow.base, kqm=kqm.base)
        with contextlib.ExitStack() as pes:
            E = pes.enter_context
            pst = lambda n, s, d=F32: E(nc.psum_tensor(n, list(s), d))
            ps_gen = pst("l1_gen", [128, 512])
            ps_st = [pst("l1_st0", [128, 512]), pst("l1_st1", [128, 512])]
            ps_o = pst("l1_o", [128, 512])
            ps_ob = [ps_o, pst("l1_o1", [128, 512])]
            ps_rr = pst("l1_rr", [128, 512])
            ps_nb = pst("l1_nb", [128, 512])
            gen_b = ps_gen[:].bitcast(BF16)

            brows = u1

            def bias_load(ph, specs):
                for j, (row, cols) in enumerate(specs):
                    n = cols.stop - cols.start
                    ph.dma("sync", brows[0:1, j * 512:j * 512 + n], ab_d[row:row + 1, cols])
                return ph.qtotal("sync")

            def bias_bcast(ph, j, n, dstb, ms_rows):
                m2 = ph.op("scalar", lambda e: e.copy(brow[0:1, 0:n], brows[0:1, j * 512:j * 512 + n]),
                           deps=[ms_rows, self._brow_guard], inc=True)
                m3 = ph.op("tensor", lambda e: e.matmul(ps_rr[:, 0:n], self.onesrow[0:1, :],
                                                        brow[0:1, 0:n], start=True, stop=True),
                           deps=[m2, self.ms_ones, self._rr_guard], inc=True)
                m4 = ph.op("vector", lambda e: e.tensor_copy(dstb[:, 0:n], ps_rr[:, 0:n]),
                           deps=[m3], inc=True)
                self._rr_guard = m4
                self._brow_guard = m3
                return m4

            self._rr_guard = None
            self._rope_guard = None
            self._brow_guard = None
            self._st_guard = [None, None]
            self._ps_o_guard = [None, None]
            self._orow_guard = None
            self._nb_guard = None
            self._rstd_guard = None

            ph = self.phase("l1a_pre")
            for c in range(4):
                ph.dma("sync", bcol[:, c:c + 1], ab_d[0:1, c * 128:(c + 1) * 128].rearrange("o (p q) -> (o p) q", q=1))
                ph.dma("sync", bcol[:, 4 + c:5 + c], ab_d[1:2, c * 128:(c + 1) * 128].rearrange("o (p q) -> (o p) q", q=1))
            ms_rows = bias_load(ph, [(7, slice(0, 512)), (7, slice(512, 1024))])
            ms_bcol = ms_rows
            mwo = bias_bcast(ph, 0, 512, bb1, ms_rows)
            mwo2 = bias_bcast(ph, 1, 512, bb2, ms_rows)
            x_ms = list(self.ms_xtm) if isinstance(self.ms_xtm, list) else [self.ms_xtm] * NT
            for i in range(NT):
                a1 = ph.op("vector", lambda e, i=i: e.tensor_add(
                    self.Xtm[:, i * 1024:i * 1024 + 512], self.Xtm[:, i * 1024:i * 1024 + 512], bb1[:, 0:512]),
                    deps=[mwo, x_ms[i]], inc=True)
                a2 = ph.op("vector", lambda e, i=i: e.tensor_add(
                    self.Xtm[:, i * 1024 + 512:i * 1024 + 1024], self.Xtm[:, i * 1024 + 512:i * 1024 + 1024], bb2[:, 0:512]),
                    deps=[mwo2, a1], inc=True)
                x_ms[i] = a2
            x_add_ms = x_ms
            if self.upto == "l1s_pre":
                self.ms_x_raw = x_add_ms
                return

            for which, wsrc, dst, bofs in (("q", wdq_d, ctq, 0), ("kv", wdkv_d, ctkv, 4)):
                ph = self.phase(f"l1a_ct{which}")
                ph.dma("gpsimd", wA[:, 0:4096].rearrange("p (c f) -> p c f", c=8),
                       wsrc.rearrange("(c p) f -> p c f", p=128))
                ms_w = ph.qtotal("gpsimd")
                guard = []
                for cqc in range(4):
                    for tg in range(NQG):
                        deps0 = self.ms_xfmb + [ms_w, ms_bcol] + guard
                        for dc in range(ND):
                            lhs = wA[:, dc * 512 + cqc * 128: dc * 512 + cqc * 128 + 128]
                            rhs = self.xfmb_chunk(dc)[:, tg * 512:(tg + 1) * 512]
                            ph.op("tensor", lambda e, l=lhs, r=rhs, st=(dc == 0), sp=(dc == ND - 1):
                                  e.matmul(ps_gen[:], l, r, start=st, stop=sp),
                                  deps=deps0 if dc == 0 else (), inc=(dc == ND - 1))
                        ms_mm = self.tmark()
                        ms_e = ph.op("vector", lambda e, cqc=cqc, tg=tg, d=dst, b=bofs: e.tensor_scalar(
                            d[:, cqc * T + tg * 512: cqc * T + tg * 512 + 512], ps_gen[:],
                            bcol[:, b + cqc:b + cqc + 1], None, ALU.add), deps=[ms_mm], inc=True)
                        guard = [ms_e]
                ms_ct = guard[0]
            if self.upto == "l1s_ct":
                self.ms_x_raw = x_add_ms
                return

            for hp in range(2):
                f0 = hp * 512
                ph = self.phase(f"l1a_kv{hp}")
                ph.dma("gpsimd", wA[:, 0:4096].rearrange("p (c f) -> p c f", c=8),
                       wkr_d[:, f0:f0 + 512].rearrange("(c p) f -> p c f", p=128),
                       deps=[self.tmark()])
                ph.dma("gpsimd", wB[:, 0:2048].rearrange("p (c f) -> p c f", c=4),
                       wuk_d[:, f0:f0 + 512].rearrange("(c p) f -> p c f", p=128))
                ph.dma("gpsimd", wC[:, 0:2048].rearrange("p (c f) -> p c f", c=4),
                       wuv_d[:, f0:f0 + 512].rearrange("(c p) f -> p c f", p=128))
                ms_w = ph.qtotal("gpsimd")
                ms_rows = bias_load(ph, [(4, slice(f0, f0 + 512)), (5, slice(f0, f0 + 512)),
                                         (6, slice(f0, f0 + 512))])
                mb_kr = bias_bcast(ph, 0, 512, bb1, ms_rows)
                mb_uk = bias_bcast(ph, 1, 512, bb2, ms_rows)
                mb_uv = bias_bcast(ph, 2, 512, bb3, ms_rows)
                guard_gen = [mb_uv]
                guard_K = []
                ms_tab = None
                for i in range(NT):
                    ph.dma("sync", ctab[:, 0:256], cos_d[i * 128:(i + 1) * 128, hp * 256:hp * 256 + 256],
                           deps=[self._rope_guard, mb_uv])
                    ph.dma("sync", stab[:, 0:256], sin_d[i * 128:(i + 1) * 128, hp * 256:hp * 256 + 256])
                    ms_tab = ph.qtotal("sync")
                    deps0 = self.ms_xfmb + [ms_w] + guard_gen
                    for dc in range(ND):
                        lhs = self.xfmb_chunk(dc)[:, i * 128:(i + 1) * 128]
                        rhs = wA[:, dc * 512:(dc + 1) * 512]
                        ph.op("tensor", lambda e, l=lhs, r=rhs, st=(dc == 0), sp=(dc == ND - 1):
                              e.matmul(ps_gen[:], l, r, start=st, stop=sp),
                              deps=deps0 if dc == 0 else (), inc=(dc == ND - 1))
                    ms_mm = self.tmark()
                    ms_kr = ph.op("vector", lambda e: e.tensor_add(qr_s[:, 0:512], ps_gen[:], bb1[:, 0:512]),
                                  deps=[ms_mm, mb_kr, self._rope_guard], inc=True)
                    guard_gen = [ms_kr]
                    if self.upto == "l1s_kva":
                        mv = ms_kr
                        ms_vtile = mv
                        continue
                    ms_rope = self.emit_rope_l1(ph, qr_s, qr2, ctab, stab, rt1, rt2, deps=[ms_kr, ms_tab])
                    if self.upto == "l1s_kvb":
                        mv = ms_rope
                        ms_vtile = mv
                        continue
                    deps0 = [ms_ct, ms_w] + guard_gen
                    for cqc in range(4):
                        lhs = ctkv[:, cqc * T + i * 128: cqc * T + i * 128 + 128]
                        rhs = wB[:, cqc * 512:(cqc + 1) * 512]
                        ph.op("tensor", lambda e, l=lhs, r=rhs, st=(cqc == 0), sp=(cqc == 3):
                              e.matmul(ps_gen[:], l, r, start=st, stop=sp),
                              deps=deps0 if cqc == 0 else (), inc=(cqc == 3))
                    ms_mm = self.tmark()
                    m1 = ph.op("vector", lambda e: e.tensor_add(qc_s[:, 0:512], ps_gen[:], bb2[:, 0:512]),
                               deps=[ms_mm, mb_uk], inc=True)
                    m2 = ph.op("vector", lambda e: e.tensor_add(qc_s[:, 0:512], qc_s[:, 0:512], qr2[:, 0:512]),
                               deps=[m1, ms_rope], inc=True)
                    guard_gen = [m2]
                    if self.upto == "l1s_kvc":
                        mv = m2
                        ms_vtile = mv
                        continue
                    msq = ph.op("scalar", lambda e: e.activation(sqs[:, 0:512], qc_s[:, 0:512], AF.Square),
                                deps=[m2], inc=True)
                    mkn = ph.op("vector", lambda e, i=i: e.reduce_sum(
                        kn2[:, i * 8:(i + 1) * 8], sqs.ap(0, [[64, 8], [1, 64]]), axis=AX.X),
                        deps=[msq], inc=True)
                    tms = None
                    for cl in range(4):
                        d0 = gen_b[:, cl * 128:cl * 128 + 128]
                        tms = ph.op("tensor", lambda e, d=d0, cl=cl: e.transpose(
                            d, qc_s[:, cl * 128:(cl + 1) * 128], self.identb[:]),
                            deps=([m2, msq] + guard_gen) if cl == 0 else (), inc=(cl == 3))
                    evs = None
                    for cl in range(4):
                        d0 = gen_b[:, cl * 128:cl * 128 + 128]
                        evs = ph.op("vector", lambda e, d=d0, cl=cl, i=i: e.tensor_copy(
                            Kfm[:, cl * T + i * 128: cl * T + i * 128 + 128], d), deps=[tms], inc=True)
                    guard_gen = [evs]
                    guard_K = [evs]
                    if self.upto == "l1s_kvd":
                        mv = evs
                        ms_vtile = mv
                        continue
                    deps0 = [ms_ct, ms_w] + guard_gen
                    for cqc in range(4):
                        lhs = ctkv[:, cqc * T + i * 128: cqc * T + i * 128 + 128]
                        rhs = wC[:, cqc * 512:(cqc + 1) * 512]
                        ph.op("tensor", lambda e, l=lhs, r=rhs, st=(cqc == 0), sp=(cqc == 3):
                              e.matmul(ps_gen[:], l, r, start=st, stop=sp),
                              deps=deps0 if cqc == 0 else (), inc=(cqc == 3))
                    ms_mm = self.tmark()
                    mv = ph.op("vector", lambda e, i=i: e.tensor_add(
                        Vtm[:, i * 512:(i + 1) * 512], ps_gen[:], bb3[:, 0:512]),
                        deps=[ms_mm, mb_uv], inc=True)
                    guard_gen = [mv]
                    ms_vtile = mv
                    if self.upto == "l1s_kv1t":
                        break
                if self.upto in ("l1s_kva", "l1s_kvb", "l1s_kvc", "l1s_kvd", "l1s_kve"):
                    self.ms_x_raw = x_add_ms
                    return
                ms_kn_all = mkn
                ms_v_all = mv
                nk = NT * 8
                t1 = ph.op("tensor", lambda e, nk=nk: e.transpose(ps_gen[0:nk, 0:128], kn2[:, 0:nk], self.identf[:]),
                           deps=[ms_kn_all, self.ms_identf] + guard_gen, inc=True)
                r1m = ph.op("vector", lambda e, nk=nk: e.reduce_max(kqm[0:nk, 8:9],
                            bass.AP(ps_gen, 0, [[512, nk], [1, 128]]), axis=AX.X),
                            deps=[t1], inc=True)
                t2 = ph.op("tensor", lambda e, nk=nk: e.transpose(ps_gen[0:1, 0:nk], kqm[0:nk, 8:9],
                                                                  self.identf[0:nk, 0:nk]),
                           deps=[r1m], inc=True)
                r2m = ph.op("vector", lambda e: e.reduce_max(
                    mrow[0:1, 0:8], bass.AP(ps_gen, 0, [[512, 1], [1, 8], [8, NT]]), axis=AX.X),
                    deps=[t2], inc=True)

                ph = self.phase(f"l1a_w{hp}")
                ph.dma("gpsimd", wB[:, 0:2048].rearrange("p (c f) -> p c f", c=4),
                       wuq_d[:, f0:f0 + 512].rearrange("(c p) f -> p c f", p=128),
                       deps=[self.tmark()])
                ph.dma("gpsimd", wC[:, 0:2048].rearrange("p (c f) -> p c f", c=4),
                       wqr_d[:, f0:f0 + 512].rearrange("(c p) f -> p c f", p=128))
                ph.dma("gpsimd", wA[:, 0:4096].rearrange("p (c f) -> p c f", c=4),
                       wo_d[f0:f0 + 512, :].rearrange("(c p) f -> p c f", p=128))
                ms_w = ph.qtotal("gpsimd")
                ms_rows = bias_load(ph, [(2, slice(f0, f0 + 512)), (3, slice(f0, f0 + 512))])
                mb_uq = bias_bcast(ph, 0, 512, bb1, ms_rows)
                mb_qr = bias_bcast(ph, 1, 512, bb2, ms_rows)
                mcast = ph.op("vector", lambda e: e.tensor_copy(brow[0:1, 0:8], mrow[0:1, 0:8]),
                              deps=[r2m], inc=True)
                mkb = ph.op("tensor", lambda e: e.matmul(ps_rr[:, 0:8], self.onesrow[0:1, :],
                                                         brow[0:1, 0:8], start=True, stop=True),
                            deps=[mcast, self.ms_ones, self._rr_guard], inc=True)
                mkbe = ph.op("vector", lambda e: e.tensor_copy(kqm[:, 0:8], ps_rr[:, 0:8]),
                             deps=[mkb], inc=True)
                self._rr_guard = mkbe
                if self.upto in ("l1s_kv", "l1s_kv1t") or (self.upto == "l1s_hp1kv" and hp == 1):
                    self.ms_x_raw = x_add_ms
                    return
                _micro = int(self.upto[8:]) if self.upto.startswith("l1s_hp1m") else 99

                def _mgate(n):
                    if hp == 1 and _micro <= n:
                        self.ms_x_raw = x_add_ms
                        return True
                    return False

                for qg in range(NQG):
                    if qg > 0:
                        ph = self.phase(f"l1a_w{hp}g{qg}")
                    ms_qfm = None
                    guard_gen2 = [mkbe]
                    for ii in range(4):
                        i = qg * 4 + ii
                        if self.upto == "l1s_hp1v1" and hp == 1:
                            ph.dma("gpsimd", ctab[:, 0:256], cos_d[i * 128:(i + 1) * 128, hp * 256:hp * 256 + 256],
                                   deps=[self._rope_guard, mb_qr])
                            ph.dma("gpsimd", stab[:, 0:256], sin_d[i * 128:(i + 1) * 128, hp * 256:hp * 256 + 256])
                            self.ms_x_raw = x_add_ms
                            return
                        if self.upto == "l1s_hp1v2" and hp == 1:
                            ph.dma("sync", ctab[:, 0:256], cos_d[i * 128:(i + 1) * 128, hp * 256:hp * 256 + 256])
                            ph.dma("sync", stab[:, 0:256], sin_d[i * 128:(i + 1) * 128, hp * 256:hp * 256 + 256])
                            self.ms_x_raw = x_add_ms
                            return
                        if self.upto == "l1s_hp1v3" and hp == 1:
                            ph.dma("sync", ctab[:, 0:256], cos_d[i * 128:(i + 1) * 128, hp * 256:hp * 256 + 256],
                                   deps=[self._rope_guard, mb_qr])
                            self.ms_x_raw = x_add_ms
                            return
                        ph.dma("sync", ctab[:, 0:256], cos_d[i * 128:(i + 1) * 128, hp * 256:hp * 256 + 256],
                               deps=[self._rope_guard, mb_qr])
                        ph.dma("sync", stab[:, 0:256], sin_d[i * 128:(i + 1) * 128, hp * 256:hp * 256 + 256])
                        ms_tab = ph.qtotal("sync")
                        if _mgate(1):
                            return
                        deps0 = [ms_ct, ms_w] + guard_gen2
                        for cqc in range(4):
                            lhs = ctq[:, cqc * T + i * 128: cqc * T + i * 128 + 128]
                            rhs = wC[:, cqc * 512:(cqc + 1) * 512]
                            ph.op("tensor", lambda e, l=lhs, r=rhs, st=(cqc == 0), sp=(cqc == 3):
                                  e.matmul(ps_gen[:], l, r, start=st, stop=sp),
                                  deps=deps0 if cqc == 0 else (), inc=(cqc == 3))
                        ms_mm = self.tmark()
                        ms_qr = ph.op("vector", lambda e: e.tensor_add(qr_s[:, 0:512], ps_gen[:], bb2[:, 0:512]),
                                      deps=[ms_mm, mb_qr, self._rope_guard], inc=True)
                        guard_gen2 = [ms_qr]
                        if _mgate(2):
                            return
                        ms_rope = self.emit_rope_l1(ph, qr_s, qr2, ctab, stab, rt1, rt2, deps=[ms_qr, ms_tab])
                        if _mgate(3):
                            return
                        deps0 = [ms_ct] + guard_gen2
                        for cqc in range(4):
                            lhs = ctq[:, cqc * T + i * 128: cqc * T + i * 128 + 128]
                            rhs = wB[:, cqc * 512:(cqc + 1) * 512]
                            ph.op("tensor", lambda e, l=lhs, r=rhs, st=(cqc == 0), sp=(cqc == 3):
                                  e.matmul(ps_gen[:], l, r, start=st, stop=sp),
                                  deps=deps0 if cqc == 0 else (), inc=(cqc == 3))
                        ms_mm = self.tmark()
                        m1 = ph.op("vector", lambda e: e.tensor_add(qc_s[:, 0:512], ps_gen[:], bb1[:, 0:512]),
                                   deps=[ms_mm, mb_uq], inc=True)
                        m2 = ph.op("vector", lambda e: e.tensor_add(qc_s[:, 0:512], qc_s[:, 0:512], qr2[:, 0:512]),
                                   deps=[m1, ms_rope], inc=True)
                        guard_gen2 = [m2]
                        if _mgate(4):
                            return
                        msq = ph.op("scalar", lambda e: e.activation(sqs[:, 0:512], qc_s[:, 0:512], AF.Square),
                                    deps=[m2], inc=True)
                        mqn = ph.op("vector", lambda e, ii=ii: e.reduce_sum(
                            qn2[:, ii * 8:(ii + 1) * 8], sqs.ap(0, [[64, 8], [1, 64]]), axis=AX.X),
                            deps=[msq], inc=True)
                        tms = None
                        for cl in range(4):
                            d0 = gen_b[:, cl * 128:cl * 128 + 128]
                            tms = ph.op("tensor", lambda e, d=d0, cl=cl: e.transpose(
                                d, qc_s[:, cl * 128:(cl + 1) * 128], self.identb[:]),
                                deps=([m2, msq] + guard_gen2) if cl == 0 else (), inc=(cl == 3))
                        evs = None
                        for cl in range(4):
                            d0 = gen_b[:, cl * 128:cl * 128 + 128]
                            evs = ph.op("vector", lambda e, d=d0, cl=cl, ii=ii: e.tensor_copy(
                                Qfm[:, cl * 512 + ii * 128: cl * 512 + ii * 128 + 128], d), deps=[tms], inc=True)
                        guard_gen2 = [evs]
                        ms_qfm = evs
                        if self.upto.startswith("l1s_hp1q") and hp == 1 and ii + 1 >= int(self.upto[8:]):
                            self.ms_x_raw = x_add_ms
                            return
                    if self.upto == "l1s_wq" or (self.upto == "l1s_hp1wq" and hp == 1) or \
                       (self.upto.startswith("l1s_qb") and qg + 1 >= int(self.upto[6:])):
                        self.ms_x_raw = x_add_ms
                        return
                    tq1 = ph.op("tensor", lambda e: e.transpose(ps_gen[0:32, 0:128], qn2[:, 0:32], self.identf[:]),
                                deps=[mqn, self.ms_identf] + guard_gen2, inc=True)
                    rq1 = ph.op("vector", lambda e: e.reduce_max(
                        kqm[0:32, 9:10], bass.AP(ps_gen, 0, [[512, 32], [1, 128]]), axis=AX.X),
                        deps=[tq1], inc=True)
                    tq2 = ph.op("tensor", lambda e: e.transpose(ps_gen[0:1, 0:32], kqm[0:32, 9:10], self.identf[0:32, 0:32]),
                                deps=[rq1], inc=True)
                    rq2 = ph.op("vector", lambda e: e.reduce_max(
                        mrow[0:1, 8:16], bass.AP(ps_gen, 0, [[512, 1], [1, 8], [8, 4]]), axis=AX.X),
                        deps=[tq2], inc=True)
                    n1 = ph.op("vector", lambda e: e.tensor_mul(mrow[0:1, 8:16], mrow[0:1, 8:16], mrow[0:1, 0:8]),
                               deps=[rq2, r2m], inc=True)
                    n2 = ph.op("scalar", lambda e: e.activation(mrow[0:1, 8:16], mrow[0:1, 8:16], AF.Sqrt),
                               deps=[n1], inc=True)
                    n3 = ph.op("vector", lambda e: e.tensor_scalar_mul(mrow[0:1, 8:16], mrow[0:1, 8:16], -l1s),
                               deps=[n2], inc=True)
                    n4 = ph.op("vector", lambda e: e.tensor_copy(brow[0:1, 8:16], mrow[0:1, 8:16]),
                               deps=[n3], inc=True)
                    n5 = ph.op("tensor", lambda e: e.matmul(ps_rr[:, 8:16], self.onesrow[0:1, :],
                                                            brow[0:1, 8:16], start=True, stop=True),
                               deps=[n4, self._rr_guard], inc=True)
                    n6 = ph.op("vector", lambda e: e.tensor_copy(nmb[:, 0:8], ps_rr[:, 8:16]),
                               deps=[n5], inc=True)
                    self._rr_guard = n6
                    pending_norm = None
                    for hl in range(8):
                        qb = 64 * (hl % 2)
                        kch = hl // 2
                        ms_av = None
                        guard_pt = [None, None]
                        ms_exp = [None] * NT
                        def emit_mss(kt, hl=hl, qb=qb, kch=kch, ms_exp=ms_exp, guard_pt=guard_pt):
                            lhs = Kfm[qb:qb + 64, kch * T + kt * 128: kch * T + kt * 128 + 128]
                            rhs = Qfm[qb:qb + 64, kch * 512: kch * 512 + 512]
                            mss = ph.op("tensor", lambda e, l=lhs, r=rhs, b=ps_st[kt % 2], tp=(qb, 0):
                                        e.matmul(b[:], l, r, start=True, stop=True, tile_position=tp),
                                        deps=[ms_qfm, guard_K[0], self._st_guard[kt % 2]], inc=True)
                            ms_exp[kt] = ph.op("scalar", lambda e, b=ps_st[kt % 2], kt=kt, hl=hl:
                                        e.activation(PT[:, (kt % 2) * 512:(kt % 2) * 512 + 512], b[:],
                                                     AF.Exp, bias=nmb[:, hl:hl + 1], scale=self.fc(l1s)),
                                        deps=[mss, n6, guard_pt[kt % 2]], inc=True)
                            self._st_guard[kt % 2] = ms_exp[kt]
                        emit_mss(0)
                        if NT > 1:
                            emit_mss(1)
                        if pending_norm is not None:
                            pending_norm()
                            pending_norm = None
                        for kt in range(NT):
                            msx = ms_exp[kt]
                            lmm = ph.op("tensor", lambda e, kt=kt: e.matmul(
                                ps_rr[0:1, :], self.onesb[:, 0:1], PT[:, (kt % 2) * 512:(kt % 2) * 512 + 512],
                                start=(kt == 0), stop=(kt == NT - 1)),
                                deps=[msx, self.ms_ones, self._rr_guard if kt == 0 else None], inc=True)
                            ms_av = ph.op("tensor", lambda e, kt=kt, hl=hl, qb=qb: e.matmul(
                                ps_ob[hl % 2][qb:qb + 64, :], Vtm[:, kt * 512 + hl * 64: kt * 512 + hl * 64 + 64],
                                PT[:, (kt % 2) * 512:(kt % 2) * 512 + 512],
                                start=(kt == 0), stop=(kt == NT - 1)),
                                deps=[msx, ms_v_all, self._ps_o_guard[hl % 2] if kt == 0 else None], inc=True)
                            guard_pt[kt % 2] = ms_av
                            if kt + 2 < NT:
                                emit_mss(kt + 2)

                        def pending_norm(hl=hl, qb=qb, lmm=lmm, ms_av=ms_av):
                            e2 = ph.op("vector", lambda e: e.reciprocal(o_row[0:1, 0:512], ps_rr[0:1, :]),
                                       deps=[lmm, self._orow_guard], inc=True)
                            self._rr_guard = e2
                            e3 = ph.op("vector", lambda e: e.tensor_copy(brow[0:1, 16:528], o_row[0:1, 0:512]),
                                       deps=[e2, self._brow_guard], inc=True)
                            self._orow_guard = e3
                            e4 = ph.op("tensor", lambda e, qb=qb: e.matmul(ps_nb[qb:qb + 64, :], self.onesrow[0:1, 0:64],
                                                                           brow[0:1, 16:528], start=True, stop=True),
                                       deps=[e3, self._nb_guard], inc=True)
                            self._brow_guard = e4
                            e5 = ph.op("scalar", lambda e, qb=qb: e.copy(rstdb[qb:qb + 64, 0:512], ps_nb[qb:qb + 64, :]),
                                       deps=[e4, self._rstd_guard], inc=True)
                            self._nb_guard = e5
                            ydst = yfm[qb:qb + 64, (hl // 2) * 512:(hl // 2) * 512 + 512]
                            e6 = ph.op("vector", lambda e, d=ydst, qb=qb, hl=hl: e.tensor_mul(
                                d, ps_ob[hl % 2][qb:qb + 64, :], rstdb[qb:qb + 64, 0:512]), deps=[e5, ms_av], inc=True)
                            self._rstd_guard = e6
                            self._ps_o_guard[hl % 2] = e6
                    pending_norm()
                    if self.upto in ("l1s_hp1wh1", "l1s_wh1"):
                        self.ms_x_raw = x_add_ms
                        return
                    yv = (kb.sems["vector"], kb.sems["vector"].n)
                    for ii in range(4):
                        qt = qg * 4 + ii
                        for fh in range(2):
                            deps0 = [yv, ms_w] + guard_gen2
                            for c in range(4):
                                lhs = yfm[:, c * 512 + ii * 128: c * 512 + ii * 128 + 128]
                                rhs = wA[:, c * 1024 + fh * 512: c * 1024 + fh * 512 + 512]
                                ph.op("tensor", lambda e, l=lhs, r=rhs, st=(c == 0), sp=(c == 3):
                                      e.matmul(ps_gen[:], l, r, start=st, stop=sp),
                                      deps=deps0 if c == 0 else (), inc=(c == 3))
                            ms_mm = self.tmark()
                            dstx = self.Xtm[:, qt * 1024 + fh * 512: qt * 1024 + fh * 512 + 512]
                            ms_res = ph.op("vector", lambda e, d=dstx: e.tensor_add(d, d, ps_gen[:]),
                                           deps=[ms_mm, x_add_ms[qt]], inc=True)
                            guard_gen2 = [ms_res]
                            x_add_ms[qt] = ms_res
                    if self.upto == "l1s_wo1" or (self.upto == "l1s_hp1w1" and hp == 1) or \
                       (self.upto.startswith("l1s_wg") and qg + 1 >= int(self.upto[6:])):
                        self.ms_x_raw = x_add_ms
                        return
                if self.upto == "l1s_whp0":
                    self.ms_x_raw = x_add_ms
                    return
            self.ms_x_raw = x_add_ms


    def p_l1_ffn(self):
        T, NT, NQG = self.T, self.NT, self.NQG
        nc, kb = self.nc, self.kb
        NW = NQG
        HID = 2816
        NI = HID // 128

        wqkv_d = self.dram_in("f_wqkv", [2, 1024, 1536])
        wo_d = self.dram_in("f_wo", [2, 512, 1024])
        w1_d = self.dram_in("f_w1", [2, 1024, 2 * HID])
        w2_d = self.dram_in("f_w2", [2, HID, 1024])
        b1_d = self.dram_in("f_b1", [2, 2 * HID])
        b2_d = self.dram_in("f_b2", [2, 1024])
        lng_d = self.dram_in("f_lng", [2, 1024])
        lnb_d = self.dram_in("f_lnb", [2, 1024])
        qs_d = self.dram_in("f_qs", [2, 64])
        ks_d = self.dram_in("f_ks", [2, 64])
        xq1_d = self.dram_in("xq1", [T, 64])
        xq2_d = self.dram_in("xq2", [T, 64])
        xk1_d = self.dram_in("xk1", [T, 64])
        xk2_d = self.dram_in("xk2", [T, 64])

        self.arena_reset()
        xpt = self.aB(1024)
        sqb = self.aB(512)
        t1b = self.aB(512)
        qsb = self.aB(64)
        ksb = self.aB(64)
        mb_ = self.aB(512)
        rb_ = self.aB(512)
        xln = [self.aB(4096), self.aB(4096)]
        w1s = [self.aB(1024), self.aB(1024), self.aB(1024), self.aB(1024)]
        ggb = self.aB(512)
        dtmp = self.aB(512)
        browf = self.aB(544)
        wqk = self.aB(8192)
        qtm = self.aB(512)
        ktm = self.aB(512)
        q3b = self.aB(512)
        Pb = self.aB(768)
        PTb = self.aB(768)
        yfm = self.aB(2048)
        wos = self.aB(2048)
        w2r = wqk.sub(0, 11264)
        Qfm = [self.aB(2048), self.aB(2048)]
        Kfm = [self.aB(2048), self.aB(2048)]
        Vtm = [self.aB(2048), self.aB(2048)]
        h4a = Qfm[0].sub(0, 11264)
        b1c = self.aF(48)
        b2c = self.aF(8)
        lgc = self.aF(8)
        lbc = self.aF(8)
        ssf = self.aF(16)
        lrow = self.aF(512)
        vrow = self.aF(512)
        nmx = self.aF(8)
        bsave = self.aF(4)
        bsave2 = self.aF(4)

        with contextlib.ExitStack() as pes:
            E = pes.enter_context
            pst = lambda n, s, d=F32: E(nc.psum_tensor(n, list(s), d))
            psQK = pst("f_qk", [128, 1024])
            psS = pst("f_s", [128, 640])
            psO = pst("f_o", [128, 512])
            psG = pst("f_g", [128, 512])
            gen_b = psG[:].bitcast(BF16)
            wsem = [Sem(pes.enter_context(nc.semaphore("fw_s0"))),
                    Sem(pes.enter_context(nc.semaphore("fw_s1"))),
                    Sem(pes.enter_context(nc.semaphore("fw_s2"))),
                    Sem(pes.enter_context(nc.semaphore("fw_s3")))]

            x_ms = list(self.ms_xtm) if isinstance(self.ms_xtm, list) else [self.ms_xtm] * NT
            g = dict(gen=None, wqk=None, psqk=None, psqk2=None, sqb=None, q3=None,
                     psS=None, Pb=None, PT=None, psO=None, t1=None, stream=None,
                     xpt=None, dtmp=None, gg=None, h4=None)
            g["w1"] = [None, None, None, None]
            g["w2"] = [None, None]
            g["gg0"] = None
            g["gg1"] = None
            g["dn"] = [None, None, None, None]

            def colload(ph, dst, dsrc, row, n):
                for cix in range(n):
                    ph.dma("sync", dst[:, cix:cix + 1],
                           dsrc[row:row + 1, cix * 128:(cix + 1) * 128].rearrange("o (p q) -> (o p) q", q=1))

            for l in range(2):
                ph = self.phase(f"f{l}_setup")
                colload(ph, b1c, b1_d, l, 44)
                colload(ph, b2c, b2_d, l, 8)
                colload(ph, lgc, lng_d, l, 8)
                colload(ph, lbc, lnb_d, l, 8)
                ph.dma("sync", lrow[0:1, 0:64], qs_d[l:l + 1, :])
                ph.dma("sync", lrow[0:1, 64:128], ks_d[l:l + 1, :])
                ms_cl = ph.qtotal("sync")
                c1 = ph.op("scalar", lambda e: e.copy(browf[0:1, 0:128], lrow[0:1, 0:128]),
                           deps=[ms_cl], inc=True)
                c2 = ph.op("tensor", lambda e: e.matmul(psG[:, 0:128], self.onesrow[0:1, :],
                                                        browf[0:1, 0:128], start=True, stop=True),
                           deps=[c1, self.ms_ones, g["gen"]], inc=True)
                c3 = ph.op("vector", lambda e: e.tensor_copy(qsb[:, 0:64], psG[:, 0:64]), deps=[c2], inc=True)
                c4 = ph.op("vector", lambda e: e.tensor_copy(ksb[:, 0:64], psG[:, 64:128]), deps=[c2], inc=True)
                g["gen"] = c4
                g["xpt"] = c4
                ms_scales = c4

                def qkv_build(ph, grp):
                    sl = grp % 2
                    ph.op("gpsimd", lambda e, l=l: e.dma_start(
                        wqk[:, 0:8192].rearrange("p (c f) -> p c f", c=8),
                        wqkv_d[l, :, 0:1024].rearrange("(c p) f -> p c f", p=128)),
                        deps=[g["wqk"]], inc=True, inc_sem=wsem[3], inc_n=16)
                    ms_qk = (wsem[3], wsem[3].n)
                    for j, tab in enumerate((xq1_d, xq2_d, xk1_d, xk2_d)):
                        ph.dma("gpsimd", xpt[:, j * 256:(j + 1) * 256].rearrange("p (a c) -> p a c", c=64),
                               tab[grp * 512:(grp + 1) * 512, :].rearrange("(a p) c -> p a c", p=128),
                               deps=[g["xpt"]] if j == 0 else ())
                    ms_xpt = ph.qtotal("gpsimd")
                    kevac = None
                    ms_v_l = None
                    for ii in range(4):
                        i = grp * 4 + ii
                        deps0 = [ms_qk, g["psqk"]]
                        for c in range(ND):
                            lhs = self.xfmb_chunk(c, shifted=(c >= 4))[:, i * 128:(i + 1) * 128]
                            ph.op("tensor", lambda e, lh=lhs, c=c, st=(c == 0), sp=(c == ND - 1):
                                  e.matmul(psQK[:, 0:512], lh, wqk[:, c * 1024:c * 1024 + 512], start=st, stop=sp),
                                  deps=deps0 if c == 0 else (), inc=(c == ND - 1))
                        ms_q = self.tmark()
                        for c in range(ND):
                            lhs = self.xfmb_chunk(c, shifted=(c >= 4))[:, i * 128:(i + 1) * 128]
                            ph.op("tensor", lambda e, lh=lhs, c=c, st=(c == 0), sp=(c == ND - 1):
                                  e.matmul(psQK[:, 512:1024], lh, wqk[:, c * 1024 + 512:(c + 1) * 1024],
                                           start=st, stop=sp),
                                  deps=[g["psqk2"]] if c == 0 else (), inc=(c == ND - 1))
                        ms_k = self.tmark()
                        eq = ph.op("vector", lambda e: e.tensor_copy(qtm[:, 0:512], psQK[:, 0:512]),
                                   deps=[ms_q, g["q3"]], inc=True)
                        ek = ph.op("vector", lambda e: e.tensor_copy(ktm[:, 0:512], psQK[:, 512:1024]),
                                   deps=[ms_k], inc=True)
                        g["psqk"] = eq
                        g["psqk2"] = ek
                        ms_tr_prev = None
                        for which, buf, scb, x1o, x2o in (("q", qtm, qsb, 0, 256), ("k", ktm, ksb, 512, 768)):
                            dep_in = eq if which == "q" else ek
                            msq_ = ph.op("scalar", lambda e, b=buf: e.activation(sqb[:, 0:512], b[:, 0:512], AF.Square),
                                         deps=[dep_in, g["sqb"]], inc=True)
                            mss = ph.op("vector", lambda e: e.reduce_sum(
                                ssf[:, 0:8], sqb.ap(0, [[64, 8], [1, 64]]), axis=AX.X), deps=[msq_], inc=True)
                            g["sqb"] = mss
                            ms1 = ph.op("scalar", lambda e: e.activation(ssf[:, 8:16], ssf[:, 0:8], AF.Sqrt),
                                        deps=[mss], inc=True)
                            ms2 = ph.op("vector", lambda e: e.tensor_scalar_max(ssf[:, 8:16], ssf[:, 8:16], 1e-12),
                                        deps=[ms1], inc=True)
                            ms3 = ph.op("vector", lambda e: e.reciprocal(ssf[:, 8:16], ssf[:, 8:16]),
                                        deps=[ms2], inc=True)
                            m4 = ph.op("vector", lambda e, b=buf: e.tensor_tensor(
                                t1b.ap(0, [[64, 8], [1, 64]]), b.ap(0, [[64, 8], [1, 64]]),
                                ssf.ap(8, [[1, 8], [0, 64]]), ALU.mult), deps=[ms3, dep_in], inc=True)
                            m5 = ph.op("vector", lambda e, sb=scb: e.tensor_tensor(
                                t1b.ap(0, [[64, 8], [1, 64]]), t1b.ap(0, [[64, 8], [1, 64]]),
                                sb.ap(0, [[0, 8], [1, 64]]), ALU.mult), deps=[m4, ms_scales], inc=True)
                            m6 = ph.op("gpsimd", lambda e, o=x1o, ii=ii: e.tensor_tensor(
                                q3b.ap(0, [[64, 8], [1, 64]]), t1b.ap(0, [[64, 8], [1, 64]]),
                                xpt.ap(o + ii * 64, [[0, 8], [1, 64]]), ALU.mult),
                                deps=[m5, ms_xpt, g["q3"]], inc=True)
                            m7 = ph.op("vector", lambda e, o=x2o, ii=ii: e.tensor_tensor(
                                sqb.ap(0, [[64, 8], [1, 64]]),
                                t1b.ap(32, [[64, 8], [-32, 2], [1, 32]]),
                                xpt.ap(o + ii * 64, [[0, 8], [0, 2], [1, 32]]), ALU.mult),
                                deps=[m5, ms_xpt, mss], inc=True)
                            m8 = ph.op("vector", lambda e: e.tensor_add(q3b[:, 0:512], q3b[:, 0:512], sqb[:, 0:512]),
                                       deps=[m6, m7], inc=True)
                            g["sqb"] = m8
                            g["q3"] = m8
                            dstv = Qfm[sl] if which == "q" else Kfm[sl]
                            tms = None
                            for cl in range(4):
                                tms = ph.op("tensor", lambda e, cl=cl: e.transpose(
                                    gen_b[:, cl * 128:(cl + 1) * 128], q3b[:, cl * 128:(cl + 1) * 128],
                                    self.identb[:]),
                                    deps=([m8, g["gen"], ms_tr_prev]) if cl == 0 else (), inc=(cl == 3))
                            evs = None
                            for cl in range(4):
                                evs = ph.op("vector", lambda e, cl=cl, ii=ii, d=dstv: e.tensor_copy(
                                    d[:, cl * 512 + ii * 128: cl * 512 + ii * 128 + 128],
                                    gen_b[:, cl * 128:(cl + 1) * 128]), deps=[tms], inc=True)
                            g["gen"] = evs
                            ms_tr_prev = tms
                            if which == "k":
                                kevac = evs
                    ph.op("gpsimd", lambda e, l=l: e.dma_start(
                        wqk[:, 0:4096].rearrange("p (c f) -> p c f", c=8),
                        wqkv_d[l, :, 1024:1536].rearrange("(c p) f -> p c f", p=128)),
                        deps=[self.tmark()], inc=True, inc_sem=wsem[3], inc_n=16)
                    ms_vw = (wsem[3], wsem[3].n)
                    for ii in range(4):
                        i = grp * 4 + ii
                        deps0 = [ms_vw, g["psqk"]]
                        for c in range(ND):
                            lhs = self.xfmb_chunk(c, shifted=(c >= 4))[:, i * 128:(i + 1) * 128]
                            ph.op("tensor", lambda e, lh=lhs, c=c, st=(c == 0), sp=(c == ND - 1):
                                  e.matmul(psQK[:, 0:512], lh, wqk[:, c * 512:(c + 1) * 512], start=st, stop=sp),
                                  deps=deps0 if c == 0 else (), inc=(c == ND - 1))
                        ms_mm = self.tmark()
                        mv = ph.op("vector", lambda e, ii=ii, sl=sl: e.tensor_copy(
                            Vtm[sl][:, ii * 512:(ii + 1) * 512], psQK[:, 0:512]), deps=[ms_mm], inc=True)
                        g["psqk"] = mv
                        ms_v_l = mv
                    g["wqk"] = self.tmark()
                    return kevac, ms_v_l

                ph = self.phase(f"f{l}_qkv0")
                kv_ms = [None] * NW
                kv_ms[0] = qkv_build(ph, 0)
                for grp in range(NW):
                    ph = self.phase(f"f{l}_w{grp}")
                    sl, slp = grp % 2, (grp - 1) % 2
                    for hl in range(8):
                        qb = 64 * (hl % 2)
                        kch = hl // 2
                        ms_oacc = None
                        for qq in range(4):
                            pieces = []
                            if grp > 0:
                                pieces.append((slp, qq, 4 - qq))
                            pieces.append((sl, 0, qq + 1))
                            col = 0
                            deps_mm = [kv_ms[grp][0], kv_ms[grp][1], g["psS"]]
                            if grp > 0:
                                deps_mm += [kv_ms[grp - 1][0], kv_ms[grp - 1][1]]
                            for (pslot, t0, ntl) in pieces:
                                lhs = Qfm[sl].ap(kch * 512 + qq * 128, [[1, 128]], pbase=qb, pcount=64)
                                rem = ntl * 128
                                roff = 0
                                while rem > 0:
                                    seg = min(rem, 512 - (col % 512))
                                    rhs = Kfm[pslot].ap(kch * 512 + t0 * 128 + roff,
                                                        [[1, seg]], pbase=qb, pcount=64)
                                    ph.op("tensor", lambda e, lh=lhs, r=rhs, c0=col, n=seg, tp=(qb, 0):
                                          e.matmul(psS[:, c0:c0 + n], lh, r, start=True, stop=True, tile_position=tp),
                                          deps=deps_mm if col == 0 else (), inc=True)
                                    col += seg
                                    roff += seg
                                    rem -= seg
                            ms_smm = self.tmark()
                            kc = col
                            ms_mk = ms_smm
                            if grp > 0:
                                ms_mk = ph.op("vector", lambda e: e.tensor_add(
                                    psS[:, 0:128], psS[:, 0:128], self.masklo[:]),
                                    deps=[ms_smm, self.ms_masklo], inc=True)
                            ms_mk = ph.op("vector", lambda e, kc=kc: e.tensor_add(
                                psS[:, kc - 128:kc], psS[:, kc - 128:kc], self.maskup[:]),
                                deps=[ms_mk, self.ms_maskup], inc=True)
                            ms_mx = ph.op("vector", lambda e, kc=kc: e.reduce_max(
                                nmx[:, 0:1], psS[:, 0:kc], axis=AX.X), deps=[ms_mk], inc=True)
                            ms_nm = ph.op("vector", lambda e: e.tensor_scalar_mul(nmx[:, 1:2], nmx[:, 0:1], -8.0),
                                          deps=[ms_mx], inc=True)
                            ms_ex = ph.op("scalar", lambda e, kc=kc: e.activation(
                                Pb[:, 0:kc], psS[:, 0:kc], AF.Exp, bias=nmx[:, 1:2],
                                scale=self.fc(8.0), accum_out=nmx[:, 2:3]),
                                deps=[ms_nm, g["Pb"]], inc=True)
                            g["psS"] = ms_ex
                            ms_rc = ph.op("vector", lambda e: e.reciprocal(nmx[:, 3:4], nmx[:, 2:3]),
                                          deps=[ms_ex], inc=True)
                            ms_pn = ph.op("vector", lambda e, kc=kc: e.tensor_scalar(
                                Pb[:, 0:kc], Pb[:, 0:kc], nmx[:, 3:4], None, ALU.mult),
                                deps=[ms_rc], inc=True)
                            nkt = kc // 128
                            tms = None
                            for kt in range(nkt):
                                tms = ph.op("tensor", lambda e, kt=kt: e.transpose(
                                    gen_b[:, kt * 128:(kt + 1) * 128], Pb[:, kt * 128:(kt + 1) * 128],
                                    self.identb[:]),
                                    deps=([ms_pn, g["gen"]]) if kt == 0 else (), inc=(kt == nkt - 1))
                            ev = ph.op("vector", lambda e, kc=kc: e.tensor_copy(PTb[:, 0:kc], gen_b[:, 0:kc]),
                                       deps=[tms, g["PT"]], inc=True)
                            g["gen"] = ev
                            g["Pb"] = ev
                            ms_av = None
                            for kt in range(nkt):
                                gkt = (4 * (grp - 1) + qq + kt) if grp > 0 else kt
                                vslot = (gkt // 4) % 2
                                vloc = gkt % 4
                                ms_av = ph.op("tensor", lambda e, kt=kt, vs=vslot, vl=vloc, hl=hl, qb=qb, qq=qq,
                                              st=(kt == 0), sp=(kt == nkt - 1):
                                              e.matmul(psO[qb:qb + 64, qq * 128:(qq + 1) * 128],
                                                       Vtm[vs][:, vl * 512 + hl * 64: vl * 512 + hl * 64 + 64],
                                                       PTb[:, kt * 128:(kt + 1) * 128], start=st, stop=sp),
                                              deps=[ev, g["psO"] if (kt == 0 and qq == 0) else None],
                                              inc=True)
                            ms_oacc = ms_av
                            g["PT"] = ms_av
                        ey = ph.op("vector", lambda e, qb=qb, hl=hl: e.tensor_copy(
                            yfm[qb:qb + 64, (hl // 2) * 512:(hl // 2) * 512 + 512], psO[qb:qb + 64, :]),
                            deps=[ms_oacc], inc=True)
                        g["psO"] = ey
                    yv = (kb.sems["vector"], kb.sems["vector"].n)
                    for half in range(2):
                        ph.op("gpsimd", lambda e, l=l, h=half: e.dma_start(
                            wos[:, 0:2048].rearrange("p (c f) -> p c f", c=4),
                            wo_d[l, :, h * 512:(h + 1) * 512].rearrange("(c p) f -> p c f", p=128)),
                            deps=[self.tmark()], inc=True, inc_sem=wsem[2], inc_n=16)
                        ms_wo = (wsem[2], wsem[2].n)
                        for fc in range(4):
                            f = half * 4 + fc
                            deps0 = [yv, ms_wo, g["gen"]]
                            for ic in range(4):
                                ph.op("tensor", lambda e, ic=ic, fc=fc, st=(ic == 0), sp=(ic == 3):
                                      e.matmul(psG[:], wos[:, ic * 512 + fc * 128: ic * 512 + fc * 128 + 128],
                                               yfm[:, ic * 512:(ic + 1) * 512], start=st, stop=sp),
                                      deps=deps0 if ic == 0 else (), inc=(ic == 3))
                            ms_mm = self.tmark()
                            if f >= 4:
                                dst = self.Xfmb.ap(f * (T + 1) + 1 + grp * 512, [[1, 511]])
                                ms_r = ph.op("vector", lambda e, d=dst: e.tensor_tensor(d, d, psG[:, 0:511], ALU.add),
                                             deps=[ms_mm, g["stream"]], inc=True)
                                ms_r = ph.op("vector", lambda e, f=f: e.tensor_copy(
                                    bsave[:, f - 4:f - 3], psG[:, 511:512]), deps=[ms_r], inc=True)
                            else:
                                dst = self.Xfmb.ap(f * (T + 1) + 1 + grp * 512, [[1, 512]])
                                ms_r = ph.op("vector", lambda e, d=dst: e.tensor_tensor(d, d, psG[:], ALU.add),
                                             deps=[ms_mm, g["stream"]], inc=True)
                            g["gen"] = ms_r
                            g["stream"] = ms_r
                    if grp + 1 < NW:
                        kv_ms[grp + 1] = qkv_build(ph, grp + 1)
                    ms_pe_done = self.tmark()
                    ms_ap = None
                    for f in range(4, 8):
                        dstb = self.Xfmb.ap(f * (T + 1) + 1 + grp * 512 + 511, [[1, 1]])
                        ms_ap = ph.op("vector", lambda e, d=dstb, f=f: e.tensor_add(d, d, bsave[:, f - 4:f - 3]),
                                      deps=([ms_pe_done, g["stream"]] if f == 4 else ()), inc=(f == 7))
                    g["stream"] = ms_ap

                def ln_build(ph, grp, sl, xg):
                    for c in range(ND):
                        src = self.xfmb_chunk(c, shifted=(c >= 4))[:, grp * 512:(grp + 1) * 512]
                        ph.op("tensor", lambda e, sr=src, st=(c == 0), sp=(c == ND - 1):
                              e.matmul(psG[0:1, :], self.onesb[:], sr, start=st, stop=sp),
                              deps=([g["stream"], self.ms_ones, g["gen"]]) if c == 0 else (),
                              inc=(c == ND - 1))
                    ms_s1 = self.tmark()
                    r1 = ph.op("vector", lambda e: e.tensor_scalar_mul(lrow[0:1, 0:512], psG[0:1, :], 1.0 / 1024),
                               deps=[ms_s1], inc=True)
                    for c in range(ND):
                        src = self.xfmb_chunk(c, shifted=(c >= 4))[:, grp * 512:(grp + 1) * 512]
                        mq = ph.op("scalar", lambda e, sr=src: e.activation(
                            sqb[:, 0:512], sr, AF.Square), deps=[g["stream"], g["sqb"]], inc=True)
                        mm2 = ph.op("tensor", lambda e, st=(c == 0), sp=(c == ND - 1):
                                    e.matmul(psG[0:1, :], self.onesb[:], sqb[:, 0:512], start=st, stop=sp),
                                    deps=[mq] + ([r1] if c == 0 else []), inc=True)
                        g["sqb"] = mm2
                    ms_s2 = self.tmark()
                    rA = ph.op("vector", lambda e: e.tensor_mul(vrow[0:1, 0:512], lrow[0:1, 0:512], lrow[0:1, 0:512]),
                               deps=[r1], inc=True)
                    rB = ph.op("vector", lambda e: e.scalar_tensor_tensor(
                        vrow[0:1, 0:512], psG[0:1, :], 1.0 / 1024, vrow[0:1, 0:512], ALU.mult, ALU.subtract),
                        deps=[rA, ms_s2], inc=True)
                    rC = ph.op("scalar", lambda e: e.activation(vrow[0:1, 0:512], vrow[0:1, 0:512], AF.Sqrt,
                                                                bias=self.fc(1e-5, 1)), deps=[rB], inc=True)
                    rD = ph.op("vector", lambda e: e.reciprocal(vrow[0:1, 0:512], vrow[0:1, 0:512]),
                               deps=[rC], inc=True)
                    rE = ph.op("vector", lambda e: e.tensor_copy(browf[0:1, 0:512], lrow[0:1, 0:512]),
                               deps=[r1, rA], inc=True)
                    rF = ph.op("tensor", lambda e: e.matmul(psG[:], self.onesrow[0:1, :], browf[0:1, 0:512],
                                                            start=True, stop=True),
                               deps=[rE, rB, g["gen"]], inc=True)
                    rG = ph.op("vector", lambda e: e.tensor_copy(mb_[:, 0:512], psG[:]), deps=[rF], inc=True)
                    rH = ph.op("vector", lambda e: e.tensor_copy(browf[0:1, 0:512], vrow[0:1, 0:512]),
                               deps=[rD, rF], inc=True)
                    rI = ph.op("tensor", lambda e: e.matmul(psG[:], self.onesrow[0:1, :], browf[0:1, 0:512],
                                                            start=True, stop=True),
                               deps=[rH, rG], inc=True)
                    rJ = ph.op("vector", lambda e: e.tensor_copy(rb_[:, 0:512], psG[:]), deps=[rI], inc=True)
                    g["gen"] = rJ
                    last = None
                    for c in range(ND):
                        src = self.xfmb_chunk(c, shifted=(c >= 4))[:, grp * 512:(grp + 1) * 512]
                        a1 = ph.op("vector", lambda e, sr=src: e.tensor_tensor(
                            t1b[:, 0:512], sr, mb_[:, 0:512], ALU.subtract),
                            deps=[rG, g["t1"]] + ([xg] if c == 0 else []), inc=True)
                        a2 = ph.op("vector", lambda e: e.tensor_mul(t1b[:, 0:512], t1b[:, 0:512], rb_[:, 0:512]),
                                   deps=[a1, rJ], inc=True)
                        a3 = ph.op("vector", lambda e, c=c, sl=sl: e.tensor_scalar(
                            xln[sl][:, c * 512:(c + 1) * 512], t1b[:, 0:512],
                            lgc[:, c:c + 1], lbc[:, c:c + 1], ALU.mult, ALU.add),
                            deps=[a2], inc=True)
                        g["t1"] = a3
                        last = a3
                    return last

                NIH = NI // 2
                qkb = psQK[:].bitcast(BF16)
                psb = psS[:].bitcast(BF16)
                pob = psO[:].bitcast(BF16)
                regs = [psQK[:, 0:512], psQK[:, 512:1024], psS[:, 0:512], psO[:, 0:512]]
                regs_b = [qkb[:, 0:512], qkb[:, 1024:1536], psb[:, 0:512], pob[:, 0:512]]
                g.setdefault("bk", [None, None, None, None])
                g.setdefault("w2r", None)
                g["w1"] = [None, None, None, None]
                w2sem = wsem[2]
                ln_done = {}
                ms_h2 = None

                ph = self.phase(f"f{l}_lnA")
                ln_done[0] = ln_build(ph, 0, 0, None)
                if NW > 1:
                    ln_done[1] = ln_build(ph, 1, 1, None)
                NM = (NW + 1) // 2
                for m in range(NM):
                    units = [(gg_, s_) for gg_ in range(2 * m, min(2 * m + 2, NW)) for s_ in (0, 1)]
                    U = len(units)
                    ph = self.phase(f"f{l}_f{m}")
                    if DBG_LN_SEQ and m == 1:
                        if NW > 2:
                            ln_done[2] = ln_build(ph, 2, 0, None)
                        if NW > 3:
                            ln_done[3] = ln_build(ph, 3, 1, None)
                        ms_apb = None
                        for f_ in range(4, 8):
                            dstb = self.Xfmb.ap(f_ * (T + 1) + 1 + 1023, [[1, 1]])
                            ms_apb = ph.op("vector", lambda e, d=dstb, f_=f_: e.tensor_add(
                                d, d, bsave2[:, f_ - 4:f_ - 3]),
                                deps=([ln_done[2], ln_done.get(3), g["stream"]] if f_ == 4 else ()),
                                inc=(f_ == 7))
                        g["stream"] = ms_apb
                    for h2 in range(2):
                        msw = [None, None, None, None]
                        for jl in range(NIH):
                            j = h2 * NIH + jl
                            for ab in range(2):
                                sw = (j % 2) * 2 + ab
                                off = (HID if ab else 0) + j * 128
                                ph.op("gpsimd", lambda e, l=l, o=off, sw=sw: e.dma_start(
                                    w1s[sw][:, 0:1024].rearrange("p (c f) -> p c f", c=8),
                                    w1_d[l, :, o:o + 128].rearrange("(c p) f -> p c f", p=128)),
                                    deps=[g["w1"][sw]], inc=True, inc_sem=wsem[sw], inc_n=16)
                                msw[sw] = (wsem[sw], wsem[sw].n)
                            if jl == 1 and h2 == 0:
                                pass
                            swa, swg = (j % 2) * 2, (j % 2) * 2 + 1
                            for u in range(U):
                                gg_, s_ = units[u]
                                par = u % 2
                                rG, rA = 1 + 2 * par, 0 + 2 * par
                                xsl, t0loc = gg_ - 2 * m, s_ * 256
                                deps0 = [msw[swg], ln_done.get(gg_), g["bk"][rG]]
                                for c in range(ND):
                                    ph.op("tensor", lambda e, c=c, sw=swg, rr=regs[rG], xsl=xsl, t0=t0loc,
                                          st=(c == 0), sp=(c == ND - 1):
                                          e.matmul(rr[:, 0:256] if False else bass.AP(rr.tensor, rr.offset, [rr.ap[0], [1, 256]]),
                                                   w1s[sw][:, c * 128:(c + 1) * 128],
                                                   xln[xsl][:, c * 512 + t0:c * 512 + t0 + 256],
                                                   start=st, stop=sp),
                                          deps=deps0 if c == 0 else (), inc=(c == ND - 1))
                                ms_g = self.tmark()
                                deps0 = [msw[swa], g["bk"][rA]]
                                for c in range(ND):
                                    ph.op("tensor", lambda e, c=c, sw=swa, rr=regs[rA], xsl=xsl, t0=t0loc,
                                          st=(c == 0), sp=(c == ND - 1):
                                          e.matmul(bass.AP(rr.tensor, rr.offset, [rr.ap[0], [1, 256]]),
                                                   w1s[sw][:, c * 128:(c + 1) * 128],
                                                   xln[xsl][:, c * 512 + t0:c * 512 + t0 + 256],
                                                   start=st, stop=sp),
                                          deps=deps0 if c == 0 else (), inc=(c == ND - 1))
                                ms_a = self.tmark()
                                mgg = ph.op("scalar", lambda e, j=j, rr=regs[rG], par=par: e.activation(
                                    ggb[:, par * 256:par * 256 + 256],
                                    bass.AP(rr.tensor, rr.offset, [rr.ap[0], [1, 256]]),
                                    AF.Gelu, bias=b1c[:, 22 + j:23 + j]),
                                    deps=[ms_g, g["gg%d" % par]], inc=True)
                                g["bk"][rG] = mgg
                                mh4 = ph.op("vector", lambda e, j=j, rr=regs[rA], par=par, jl=jl, u=u, U=U:
                                            e.scalar_tensor_tensor(
                                                h4a[:, (jl * U + u) * 256:(jl * U + u) * 256 + 256],
                                                bass.AP(rr.tensor, rr.offset, [rr.ap[0], [1, 256]]),
                                                b1c[:, j:j + 1], ggb[:, par * 256:par * 256 + 256],
                                                ALU.add, ALU.mult),
                                            deps=[ms_a, mgg, g["h4"], g["w2r"]], inc=True)
                                g["bk"][rA] = mh4
                                g["gg%d" % par] = mh4
                                g["h4"] = mh4
                            g["w1"][swa] = self.tmark()
                            g["w1"][swg] = g["w1"][swa]
                        ms_up_done = self.tmark()
                        for icl in range(NIH):
                            ph.op("gpsimd", lambda e, l=l, icl=icl, h2=h2: e.dma_start(
                                w2r[:, icl * 1024:(icl + 1) * 1024],
                                w2_d[l, h2 * NIH * 128 + icl * 128: h2 * NIH * 128 + (icl + 1) * 128, :]),
                                deps=[g["w2r"]] if icl == 0 else (), inc=True,
                                inc_sem=kb.sems["dgpsimd"], inc_n=16)
                        ms_w2r = ph.qtotal("gpsimd")
                        if m == 0 and h2 == 1 and not DBG_LN_SEQ:
                            if NW > 2:
                                ln_done[2] = ln_build(ph, 2, 0, ms_up_done)
                            if NW > 3:
                                ln_done[3] = ln_build(ph, 3, 1, ms_up_done)
                        for u in range(U):
                            gg_, s_ = units[u]
                            tb = gg_ * 512 + s_ * 256
                            for icl in range(NIH):
                                for t2 in range(2):
                                    for fh in range(2):
                                        bi = t2 * 2 + fh
                                        ph.op("tensor", lambda e, icl=icl, t2=t2, fh=fh, u=u, U=U, b=regs[t2 * 2 + fh],
                                              st=(icl == 0), sp=(icl == NIH - 1):
                                              e.matmul(b, h4a[:, (icl * U + u) * 256 + t2 * 128:
                                                              (icl * U + u) * 256 + t2 * 128 + 128],
                                                       w2r[:, icl * 1024 + fh * 512:icl * 1024 + fh * 512 + 512],
                                                       start=st, stop=sp),
                                              deps=([ms_w2r, g["h4"], g["bk"][bi]] if icl == 0 else ()),
                                              inc=(icl == NIH - 1))
                            ms_dn = self.tmark()
                            for t2 in range(2):
                                for fh in range(2):
                                    bi = t2 * 2 + fh
                                    ec = ph.op("vector", lambda e, b=regs[bi]: e.tensor_copy(dtmp[:, 0:512], b),
                                               deps=[ms_dn, g["dtmp"]], inc=True)
                                    tms = None
                                    for fc in range(4):
                                        tms = ph.op("tensor", lambda e, fc=fc, bb=regs_b[bi]: e.transpose(
                                            bb[:, fc * 128:(fc + 1) * 128], dtmp[:, fc * 128:(fc + 1) * 128],
                                            self.identb[:]), deps=([ec] if fc == 0 else ()),
                                            inc=(fc == 3))
                                    g["dtmp"] = tms
                                    last2 = None
                                    for fc in range(4):
                                        f = fh * 4 + fc
                                        defer = (NW > 2 and m == 0 and gg_ == 1 and s_ == 1
                                                 and t2 == 1 and f >= 4)
                                        w = 127 if defer else 128
                                        dst = self.Xfmb.ap(f * (T + 1) + 1 + tb + t2 * 128, [[1, w]])
                                        bb = regs_b[bi]
                                        if h2 == 0:
                                            last2 = ph.op("vector", lambda e, d=dst, f=f, fc=fc, bb=bb, w=w:
                                                          e.scalar_tensor_tensor(
                                                              d, bass.AP(bb.tensor, bb.offset + fc * 128, [bb.ap[0], [1, w]]),
                                                              b2c[:, f:f + 1], d, ALU.add, ALU.add),
                                                          deps=[tms, g["stream"]], inc=True)
                                        else:
                                            last2 = ph.op("vector", lambda e, d=dst, fc=fc, bb=bb, w=w:
                                                          e.tensor_add(d, d,
                                                                       bass.AP(bb.tensor, bb.offset + fc * 128, [bb.ap[0], [1, w]])),
                                                          deps=[tms, g["stream"]], inc=True)
                                        if defer:
                                            if h2 == 0:
                                                last2 = ph.op("vector", lambda e, f=f, fc=fc, bb=bb: e.tensor_scalar(
                                                    bsave2[:, f - 4:f - 3],
                                                    bass.AP(bb.tensor, bb.offset + fc * 128 + 127, [bb.ap[0], [1, 1]]),
                                                    b2c[:, f:f + 1], None, ALU.add),
                                                    deps=[last2], inc=True)
                                            else:
                                                last2 = ph.op("vector", lambda e, f=f, fc=fc, bb=bb: e.tensor_add(
                                                    bsave2[:, f - 4:f - 3], bsave2[:, f - 4:f - 3],
                                                    bass.AP(bb.tensor, bb.offset + fc * 128 + 127, [bb.ap[0], [1, 1]])),
                                                    deps=[last2], inc=True)
                                    g["bk"][bi] = last2
                                    g["stream"] = last2
                        g["w2r"] = self.tmark()
                    if NW > 2 and m == 0 and not DBG_LN_SEQ:
                        ms_apb = None
                        for f in range(4, 8):
                            dstb = self.Xfmb.ap(f * (T + 1) + 1 + 1023, [[1, 1]])
                            ms_apb = ph.op("vector", lambda e, d=dstb, f=f: e.tensor_add(
                                d, d, bsave2[:, f - 4:f - 3]),
                                deps=([ln_done[2], ln_done.get(3), g["stream"]] if f == 4 else ()),
                                inc=(f == 7))
                        g["stream"] = ms_apb
            ph = self.phase("f_final")
            x_ms2 = [None] * NT
            for i in range(NT):
                for half in range(2):
                    tms = None
                    for cl in range(4):
                        c = half * 4 + cl
                        src = self.Xfmb.ap(c * (T + 1) + 1 + i * 128, [[1, 128]])
                        tms = ph.op("tensor", lambda e, sr=src, cl=cl: e.transpose(
                            gen_b[:, cl * 128:(cl + 1) * 128], sr, self.identb[:]),
                            deps=([g["stream"], g["gen"]]) if cl == 0 else (), inc=(cl == 3))
                    a = ph.op("vector", lambda e, i=i, h=half: e.tensor_add(
                        self.Xtm[:, i * 1024 + h * 512: i * 1024 + h * 512 + 512],
                        self.Xtm[:, i * 1024 + h * 512: i * 1024 + h * 512 + 512],
                        gen_b[:, 0:512]), deps=[tms, x_ms[i], x_ms2[i]], inc=True)
                    g["gen"] = a
                    x_ms2[i] = a
            self.ms_x_raw = x_ms2

    def p_ln(self, tag, eps=1e-5):
        NT = self.NT
        ph = self.phase("ln_" + tag)
        ms_stats = []
        prev = None
        for i in range(NT):
            dep = [self.ms_x_raw[i] if isinstance(self.ms_x_raw, list) else self.ms_x_raw, prev]
            mcol = self.stat_m[:, i:i + 1]
            scol = self.stat_s[:, i:i + 1]
            m1 = ph.op("scalar", lambda e, i=i, m=mcol: e.activation(
                self.xtmb_scr[:], self.xtm_tile(i), AF.Identity, accum_out=m), deps=dep, inc=True)
            m2 = ph.op("scalar", lambda e, i=i, s=scol: e.activation(
                self.xtmb_scr[:], self.xtm_tile(i), AF.Square, accum_out=s), deps=dep + [m1], inc=True)
            prev = m2
            ms_stats.append(m2)
            ms_stats.append(m1)
        self._scr_guard = prev
        mm = ph.op("vector", lambda e: e.tensor_scalar_mul(
            self.stat_m[:, 0:NT], self.stat_m[:, 0:NT], 1.0 / 1024), deps=ms_stats, inc=True)
        m2 = ph.op("vector", lambda e: e.tensor_mul(
            self.stat_r[:, 0:NT], self.stat_m[:, 0:NT], self.stat_m[:, 0:NT]), deps=[mm], inc=True)
        m3 = ph.op("vector", lambda e: e.scalar_tensor_tensor(
            self.stat_s[:, 0:NT], self.stat_s[:, 0:NT], 1.0 / 1024, self.stat_r[:, 0:NT],
            ALU.mult, ALU.subtract), deps=[m2], inc=True)
        m4 = ph.op("scalar", lambda e, eps=eps: e.activation(
            self.stat_s[:, 0:NT], self.stat_s[:, 0:NT], AF.Sqrt, bias=self.fc(eps)), deps=[m3], inc=True)
        m5 = ph.op("vector", lambda e: e.reciprocal(self.stat_r[:, 0:NT], self.stat_s[:, 0:NT]),
                   deps=[m4], inc=True)
        ms_app = []
        for i in range(NT):
            ms = ph.op("vector", lambda e, i=i: e.tensor_scalar(
                self.xtm_tile(i), self.xtm_tile(i), self.stat_m[:, i:i + 1], self.stat_r[:, i:i + 1],
                ALU.subtract, ALU.mult), deps=[m5], inc=True)
            ms_app.append(ms)
        self.ms_xtm = ms_app
        with contextlib.ExitStack() as pes:
            trb = [pes.enter_context(self.nc.psum_tensor("trn0" + tag, [128, 128], BF16)),
                   pes.enter_context(self.nc.psum_tensor("trn1" + tag, [128, 128], BF16))]
            self.rebuild_xfmb(ph, ms_app, trb)

    def p_store(self):
        NT = self.NT
        out = self.dram_out("out", [self.T, 1024])
        ph = self.phase("store")
        last = []
        for i in range(NT):
            dep = [self.ms_xtm[i] if isinstance(self.ms_xtm, list) else self.ms_xtm]
            last.append(ph.dma("sync", out[i * 128:(i + 1) * 128, :], self.xtm_tile(i), deps=dep))
        ph.wait("sync", last)

    STAGE_ORDER = ("load", "l0_attn", "ln0a", "l0_ffn", "ln0f", "l1_attn", "ln1a",
                   "l1_ffn", "ln1f")

    def build(self):
        u = self.upto
        if u == "all":
            level = 99
        elif u.startswith("l0a_"):
            level = 1
        elif u == "l0_attn":
            level = 2
        elif u == "l0_ffn":
            level = 4
        elif u.startswith("l1s_") or u == "l1a_raw":
            level = 5
        elif u == "l1_attn":
            level = 6
        elif u == "l1_ffn":
            level = 8
        elif u == "load":
            level = 0
        else:
            raise ValueError(u)
        self.alloc_stream()
        self.p_load()
        self.ms_x_raw = self.ms_xtm
        if level >= 1:
            self.p_l0_attn()
        if level >= 2:
            self.p_ln("0a")
        if level >= 3:
            self._sig_guard = None
            self.p_l0_ffn()
        if level >= 4:
            self.p_ln("0f")
        if level >= 5:
            self.p_l1_attn()
        if level >= 6:
            self.p_ln("1a")
        if level >= 7:
            self.p_l1_ffn()
        if level >= 8:
            self.p_ln("1f")
        if level in (1, 5, 7):
            self.ms_xtm = self.ms_x_raw
        self.p_store()
        return self.finish()


def _make_host_arrays(inputs, T):
    consts = host_consts(T)
    w = host_weights(inputs)
    shared = {}
    for k, v in consts.items():
        shared[k] = np.ascontiguousarray(v.astype(np.float32))
    for k, v in w.items():
        shared[k] = np.ascontiguousarray(v)
    return shared


def kernel(**inputs):
    T = 2048
    shared = _make_host_arrays(inputs, T)
    x = np.ascontiguousarray(np.asarray(inputs["L0_attn_x"], dtype=np.float32))
    b = Builder(T, "all")
    nc = b.build()
    in_maps = []
    for ci in range(8):
        m = dict(shared)
        m["x"] = np.ascontiguousarray(x[ci])
        in_maps.append(m)
    res = run_bass_kernel_spmd(nc, in_maps, core_ids=list(range(8)))
    return np.stack([r["out"] for r in res.results], axis=0).astype(np.float32)
